# version 05
# baseline (speedup 1.0000x reference)
import numpy as np
from contextlib import ExitStack

import concourse.bass as bass
import concourse.mybir as mybir
from concourse.bass_utils import run_bass_kernel_spmd

F32 = mybir.dt.float32
F32R = mybir.dt.float32r
BF16 = mybir.dt.bfloat16
U32 = mybir.dt.uint32
AF = mybir.ActivationFunctionType
ALU = mybir.AluOpType
AX = mybir.AxisListType

T, D, NH, DH, FF = 2048, 1024, 8, 128, 4096
NCH, NG, DC, FC = 16, 4, 8, 32
SCALE = 1.0 / float(np.sqrt(DH))
NEGBIG = -1e30

EMAP = {"PE": "tensor", "DVE": "vector", "ACT": "scalar", "POOL": "gpsimd",
        "SYNCQ": "sync", "ACTQ": "scalar", "POOLQ": "gpsimd"}
ENGS = ("PE", "DVE", "ACT", "POOL")


class TR:

    def __init__(self):
        self.cnt = {}
        self.lastw = {}
        self.readers = {}
        self.waited = {}
        self.floor = {}
        self.ops = {k: [] for k in ("tensor", "vector", "scalar", "gpsimd", "sync")}
        self.sem_names = set(ENGS)

    def barrier(self):
        self.floor = dict(self.cnt)

    def _wl(self, stream, r, w):
        need = dict(self.floor)
        for k in r:
            lw = self.lastw.get(k)
            if lw:
                need[lw[0]] = max(need.get(lw[0], 0), lw[1])
        for k in w:
            lw = self.lastw.get(k)
            if lw:
                need[lw[0]] = max(need.get(lw[0], 0), lw[1])
            for s2, v2 in self.readers.get(k, {}).items():
                need[s2] = max(need.get(s2, 0), v2)
        out = []
        wt = self.waited.setdefault(stream, {})
        for s2 in sorted(need):
            if wt.get(s2, 0) < need[s2]:
                wt[s2] = need[s2]
                out.append((s2, need[s2]))
        return out

    def _commit(self, sem, val, r, w):
        for k in w:
            self.lastw[k] = (sem, val)
            self.readers[k] = {}
        for k in r:
            self.readers.setdefault(k, {})[sem] = val

    def op(self, eng, fn, r=(), w=()):
        wl = self._wl(eng, r, w)
        self.cnt[eng] = self.cnt.get(eng, 0) + 1
        val = self.cnt[eng]

        def emit(engobj, sems, _wl=wl, _fn=fn, _eng=eng):
            for s2, v2 in _wl:
                engobj.wait_ge(sems[s2], v2)
            _fn(engobj).then_inc(sems[_eng], 1)
        self.ops[EMAP[eng]].append(emit)
        self._commit(eng, val, r, w)

    def dma(self, q, dsem, fn, r=(), w=()):
        self.sem_names.add(dsem)
        wl = self._wl(q, r, w)
        self.cnt[dsem] = self.cnt.get(dsem, 0) + 16
        val = self.cnt[dsem]

        def emit(engobj, sems, _wl=wl, _fn=fn, _ds=dsem):
            for s2, v2 in _wl:
                engobj.wait_ge(sems[s2], v2)
            _fn(engobj).then_inc(sems[_ds], 16)
        self.ops[EMAP[q]].append(emit)
        self._commit(dsem, val, r, w)

    def final_wait(self, stream, sem):
        v = self.cnt.get(sem, 0)
        self.ops[EMAP[stream]].append(
            lambda e, s, _sm=sem, _v=v: e.wait_ge(s[_sm], _v) if _v else None)

    def build_blocks(self, nc, ctx):
        sems = {n: ctx.enter_context(nc.semaphore(f"s_{n}"))
                for n in sorted(self.sem_names)}
        with nc.Block() as block:
            @block.sync
            def _(eng):
                for emit in self.ops["sync"]:
                    emit(eng, sems)

            @block.tensor
            def _(eng):
                for emit in self.ops["tensor"]:
                    emit(eng, sems)

            @block.vector
            def _(eng):
                for emit in self.ops["vector"]:
                    emit(eng, sems)

            @block.scalar
            def _(eng):
                for emit in self.ops["scalar"]:
                    emit(eng, sems)

            @block.gpsimd
            def _(eng):
                for emit in self.ops["gpsimd"]:
                    emit(eng, sems)


SUB5 = 9


def build(upto=9):
    import ml_dtypes
    nc = bass.Bass("TRN2", target_bir_lowering=False, debug=False)

    def din(name, shape, dt=F32):
        return nc.dram_tensor(name, shape, dt, kind="ExternalInput").ap()
    x_nat = din("x_nat", [T, D])
    xT = din("xT", [D, T])
    wqT = din("wqT", [D, D]); wkT = din("wkT", [D, D]); wvT = din("wvT", [D, D])
    woutT = din("woutT", [D, D])
    wgP = din("wgP", [D, 128])
    b128 = din("b128", [128, 1])
    nw8 = din("nw8", [128, DC])
    gw8T = din("gw8T", [D, 8])
    acatT = din("acatT", [D, 32])
    g8 = din("g8", [8, 1])
    bcatT = din("bcatT", [32, FF])
    wiT = din("wiT", [D, FF])
    woT = din("woT", [FF, D])
    wib32 = din("wib32", [128, FC])
    wob8 = din("wob8", [128, DC])
    bout8 = din("bout8", [128, DC])
    outT = nc.dram_tensor("outT", [D, T], F32, kind="ExternalOutput").ap()
    x1h = nc.dram_tensor("x1h", [D, T], F32, kind="Internal").ap()

    eye_f = nc.inline_tensor(np.eye(128, dtype=np.float32), "eye_f").ap()
    eye_b = nc.inline_tensor(np.eye(128, dtype=ml_dtypes.bfloat16), "eye_b").ap()
    ones_f = nc.inline_tensor(np.ones((128, 128), np.float32), "ones_f").ap()
    lm_np = np.where(np.arange(128)[:, None] <= np.arange(128)[None, :], 0.0, NEGBIG)
    logmask_c = nc.inline_tensor(lm_np.astype(np.float32), "logmask").ap()
    iota8_c = nc.inline_tensor(
        np.broadcast_to(np.arange(8, dtype=np.float32), (128, 8)).copy(), "iota8").ap()
    oh_np = np.zeros((8, 1024), np.float32)
    for h in range(8):
        oh_np[h, h * 128:(h + 1) * 128] = 1.0
    onehot_c = nc.inline_tensor(oh_np, "onehot").ap()
    exp_np = np.zeros((8, 32), ml_dtypes.bfloat16)
    for e in range(8):
        exp_np[e, e * 4:(e + 1) * 4] = 1.0
    expand_c = nc.inline_tensor(exp_np, "expand832").ap()
    rot = np.zeros((128, 128), np.float64)
    rot[np.arange(64), 64 + np.arange(64)] = -1.0
    rot[64 + np.arange(64), np.arange(64)] = 1.0
    rotT_c = nc.inline_tensor(rot.T.astype(ml_dtypes.bfloat16), "rotT").ap()
    sel = np.zeros((128, 16), np.float32)
    sel[64 + np.arange(8), np.arange(8)] = 1.0
    sel[32 + np.arange(8), 8 + np.arange(8)] = 1.0
    sel_c = nc.inline_tensor(sel, "sel16").ap()
    half = DH // 2
    inv_freq = 1.0 / (10000.0 ** (np.arange(half, dtype=np.float64) / half))
    frq = np.arange(T, dtype=np.float64)[None, :] * inv_freq[:, None]
    cs2 = np.concatenate([np.cos(frq), np.cos(frq)], axis=0).astype(np.float32)
    sn2 = np.concatenate([np.sin(frq), np.sin(frq)], axis=0).astype(np.float32)
    cos2_c = nc.inline_tensor(cs2, "cos2").ap()
    sin2_c = nc.inline_tensor(sn2, "sin2").ap()
    z8_c = nc.inline_tensor(np.zeros((8, 128), np.float32), "z8").ap()

    tr = TR()

    class _Done(Exception):
        pass

    def ldc(dst, src, key):
        tr.dma("SYNCQ", "dc_" + key.replace(":", ""),
               lambda e, _d=dst, _s=src: e.dma_start(out=_d, in_=_s), w=[key])

    def pe_settle(ps_ap, keys):
        tr.op("PE", lambda e, _p=ps_ap: e.matmul(
            _p, lhsT=_dummy_lhs[0:1, 0:2], rhs=_dummy_rhs[0:1, 0:2],
            start=True, stop=True), r=["c:onesf"], w=list(keys))

    ctx = ExitStack()
    try:
      with ctx:
        SB = lambda n, s, d=F32: ctx.enter_context(nc.sbuf_tensor(n, s, d))
        eyef_s = SB("eyef", [128, 128])
        onesf_s = SB("onesf", [128, 128]); onesr_s = SB("onesr", [128, 128], F32R)
        iota8_s = SB("iota8s", [128, 8])
        expand_s = SB("expand", [8, 32], BF16)
        nw_s = SB("nw", [128, DC]); b128_s = SB("b128s", [128, 1])
        g8_s = SB("g8s", [8, 1])
        wib_s = SB("wibc", [128, FC]); wob_s = SB("wobc", [128, DC]); bout_s = SB("boutc", [128, DC])
        ldc(eyef_s[:, :], eye_f, "c:eyef")
        ldc(onesf_s[:, :], ones_f, "c:onesf")
        ldc(iota8_s[:, :], iota8_c, "c:iota"); ldc(expand_s[:, :], expand_c, "c:exp")
        ldc(nw_s[:, :], nw8, "c:nw"); ldc(b128_s[:, :], b128, "c:b128")
        ldc(g8_s[:, :], g8, "c:g8"); ldc(wib_s[:, :], wib32, "c:wib")
        ldc(wob_s[:, :], wob8, "c:wob"); ldc(bout_s[:, :], bout8, "c:bout")
        _dummy_lhs = onesf_s; _dummy_rhs = onesf_s
        cb_s = SB("cbias", [128, 4])
        for j, v in ((0, 1e-11), (1, 1.0), (2, 1e-5)):
            tr.op("DVE", lambda e, _j=j, _v=v: e.memset(cb_s[:, _j:_j + 1], _v), w=["c:cb"])
        tr.op("DVE", lambda e: e.tensor_copy(out=onesr_s[:, :], in_=onesf_s[:, :]),
              r=["c:onesf"], w=["c:onesr"])
        tr.dma("SYNCQ", "d_x1init", lambda e: e.dma_start(out=x1h, in_=xT),
               w=[f"x1h.{g}" for g in range(NG)])

        with ExitStack() as attn:
            _a = attn.enter_context
            eyeb_s = _a(nc.sbuf_tensor("eyeb", [128, 128], BF16))
            lmask_s = _a(nc.sbuf_tensor("lmask", [128, 128], F32))
            rotT_s = _a(nc.sbuf_tensor("rotTs", [128, 128], BF16))
            sel_s = _a(nc.sbuf_tensor("sel16s", [128, 16], F32))
            m1c_s = _a(nc.sbuf_tensor("m1c", [128, NCH], F32))
            r1c_s = _a(nc.sbuf_tensor("r1c", [128, NCH], F32))
            onehot_s = _a(nc.sbuf_tensor("onehots", [8, 1024], F32))
            lbuf_s = _a(nc.sbuf_tensor("lbuf", [8, T], F32))
            lhat_s = _a(nc.sbuf_tensor("lhat", [8, T], F32))
            lgml_s = _a(nc.sbuf_tensor("lgml", [8, T], F32))
            qb = _a(nc.sbuf_tensor("qb", [128, NH, T], BF16))
            kb = _a(nc.sbuf_tensor("kb", [128, NH, T], BF16))
            vb = _a(nc.sbuf_tensor("vb", [128, NCH, D], BF16))
            ldc(onehot_s[:, :], onehot_c, "c:oh")
            ldc(eyeb_s[:, :], eye_b, "c:eyeb"); ldc(lmask_s[:, :], logmask_c, "c:lm")
            ldc(rotT_s[:, :], rotT_c, "c:rot"); ldc(sel_s[:, :], sel_c, "c:sel")

            subAB = attn.enter_context(ExitStack())
            _ab = subAB.enter_context
            m1r_s = _ab(nc.sbuf_tensor("m1rs", [1, T], F32R))
            m1rb_s = _ab(nc.sbuf_tensor("m1rb", [1, T], BF16))
            ctq = _ab(nc.sbuf_tensor("ctq", [128, T], BF16))
            stq = _ab(nc.sbuf_tensor("stq", [128, T], BF16))
            ctk = _ab(nc.sbuf_tensor("ctk", [128, T], BF16))
            stk = _ab(nc.sbuf_tensor("stk", [128, T], BF16))

            with ExitStack() as subA:
                _sa = subA.enter_context
                r1rep_s = _sa(nc.sbuf_tensor("r1rep", [128, T], F32))

                with ExitStack() as sub1:
                    _s1 = sub1.enter_context
                    r1r_s = _s1(nc.sbuf_tensor("r1rs", [1, T], F32))
                    costab = _s1(nc.sbuf_tensor("costab", [128, T], F32))
                    sintab = _s1(nc.sbuf_tensor("sintab", [128, T], F32))
                    xtm = [_s1(nc.sbuf_tensor(f"xtm{i}", [128, D], F32)) for i in range(2)]
                    s1 = _s1(nc.sbuf_tensor("s1", [128, 1], F32))
                    sqs = _s1(nc.sbuf_tensor("sqs", [128, D], F32))
                    q1 = _s1(nc.sbuf_tensor("q1", [128, 1], F32))
                    msq = _s1(nc.sbuf_tensor("msq", [128, 1], F32))
                    var1 = _s1(nc.sbuf_tensor("var1", [128, 1], F32))
                    psRow = _s1(nc.psum_tensor("psRow", [128, 512], F32))
                    if upto < 1:
                        raise _Done()
                    ldc(costab[:, :], cos2_c, "c:cos")
                    ldc(sintab[:, :], sin2_c, "c:sin")
                    for c in range(NCH):
                        b = c % 2
                        tr.dma("SYNCQ", f"d_xtm{b}",
                               lambda e, _b=b, _c=c: e.dma_start(
                                   out=xtm[_b][:, :], in_=x_nat[_c * 128:(_c + 1) * 128, :]),
                               w=[f"xtm{b}"])
                        tr.op("DVE", lambda e, _b=b: e.reduce_sum(out=s1[:, :], in_=xtm[_b][:, :],
                                                                  axis=AX.X),
                              r=[f"xtm{b}"], w=["s1"])
                        tr.op("ACT", lambda e, _b=b: e.activation(out=sqs[:, :], in_=xtm[_b][:, :],
                                                                  func=AF.Square, accum_out=q1[:, :]),
                              r=[f"xtm{b}"], w=["sqs", "q1"])
                        tr.op("ACT", lambda e, _c=c: e.activation(out=m1c_s[:, _c:_c + 1], in_=s1[:, :],
                                                                  func=AF.Copy, scale=1.0 / D),
                              r=["s1"], w=[f"m1c.{c}"])
                        tr.op("ACT", lambda e, _c=c: e.activation(out=msq[:, :],
                                                                  in_=m1c_s[:, _c:_c + 1],
                                                                  func=AF.Square),
                              r=[f"m1c.{c}"], w=["msq"])
                        tr.op("DVE", lambda e: e.tensor_scalar(out=var1[:, :], in0=q1[:, :],
                                                               scalar1=1.0 / D, scalar2=None,
                                                               op0=ALU.mult),
                              r=["q1"], w=["var1"])
                        tr.op("DVE", lambda e: e.tensor_sub(var1[:, :], var1[:, :], msq[:, :]),
                              r=["var1", "msq"], w=["var1"])
                        tr.op("ACT", lambda e: e.activation(out=var1[:, :], in_=var1[:, :],
                                                            func=AF.Ln, scale=1.000001, bias=cb_s[:, 0:1]),
                              r=["var1", "c:cb"], w=["var1"])
                        tr.op("ACT", lambda e, _c=c: e.activation(out=r1c_s[:, _c:_c + 1],
                                                                  in_=var1[:, :],
                                                                  func=AF.Exp, scale=-0.5),
                              r=["var1"], w=[f"r1c.{c}"])
                    for g in range(NG):
                        def _trg(e, _g=g, _cols=m1c_s):
                            for cc in range(4):
                                i = e.transpose(psRow[0:1, cc * 128:(cc + 1) * 128],
                                                _cols[:, _g * 4 + cc:_g * 4 + cc + 1],
                                                eyef_s[:, :])
                            return i
                        tr.op("PE", _trg,
                              r=[f"m1c.{g * 4 + cc}" for cc in range(4)] + ["c:eyef", "psRow"],
                              w=["psRow"])
                        pe_settle(psRow[32:34, 0:2], ["psRow"])
                        tr.op("ACT", lambda e, _g=g: e.activation(
                            out=m1r_s[0:1, _g * 512:(_g + 1) * 512], in_=psRow[0:1, :],
                            func=AF.Copy, scale=-1.0),
                            r=["psRow"], w=[f"m1r.{g}"])
                        tr.op("ACT", lambda e, _g=g: e.activation(
                            out=m1rb_s[0:1, _g * 512:(_g + 1) * 512], in_=psRow[0:1, :],
                            func=AF.Copy, scale=-1.0),
                            r=["psRow"], w=[f"m1rb.{g}"])
                        def _trg2(e, _g=g, _cols=r1c_s):
                            for cc in range(4):
                                i = e.transpose(psRow[0:1, cc * 128:(cc + 1) * 128],
                                                _cols[:, _g * 4 + cc:_g * 4 + cc + 1],
                                                eyef_s[:, :])
                            return i
                        tr.op("PE", _trg2,
                              r=[f"r1c.{g * 4 + cc}" for cc in range(4)] + ["c:eyef", "psRow"],
                              w=["psRow"])
                        pe_settle(psRow[32:34, 0:2], ["psRow"])
                        tr.op("ACT", lambda e, _g=g: e.activation(
                            out=r1r_s[0:1, _g * 512:(_g + 1) * 512], in_=psRow[0:1, :],
                            func=AF.Copy),
                            r=["psRow"], w=[f"r1r.{g}"])
                    for g in range(NG):
                        tr.op("PE", lambda e, _g=g: e.matmul(
                            psRow[:, :], lhsT=onesf_s[0:1, :],
                            rhs=r1r_s[0:1, _g * 512:(_g + 1) * 512], start=True, stop=True),
                            r=[f"r1r.{g}", "c:onesf", "psRow"], w=["psRow"])
                        tr.op("ACT", lambda e, _g=g: e.activation(
                            out=r1rep_s[:, _g * 512:(_g + 1) * 512], in_=psRow[:, :], func=AF.Copy),
                            r=["psRow"], w=[f"r1rep.{g}"])
                    for g in range(NG):
                        sl = slice(g * 512, (g + 1) * 512)
                        for (dst, tab, tabk, sc, key) in (
                                (ctq, costab, "c:cos", SCALE, "ctq"),
                                (stq, sintab, "c:sin", SCALE, "stq"),
                                (ctk, costab, "c:cos", 1.0, "ctk"),
                                (stk, sintab, "c:sin", 1.0, "stk")):
                            tr.op("DVE", lambda e, _d=dst, _t=tab, _sc=sc, _sl=sl:
                                  e.scalar_tensor_tensor(
                                      out=_d[:, _sl], in0=r1rep_s[:, _sl], scalar=_sc,
                                      in1=_t[:, _sl], op0=ALU.mult, op1=ALU.mult),
                                  r=[f"r1rep.{g}", tabk], w=[f"{key}.{g}"])

                tr.barrier()
                with ExitStack() as sub3:
                    _s3 = sub3.enter_context
                    ls_s = _s3(nc.sbuf_tensor("ls", [128, T], F32))
                    wsg_s = _s3(nc.sbuf_tensor("wsg", [1, 128], F32R))
                    z8_s = _s3(nc.sbuf_tensor("z8s", [8, 128], F32))
                    wg = _s3(nc.sbuf_tensor("wg", [128, DC, 128], F32R))
                    zt = _s3(nc.sbuf_tensor("zt", [128, 256], F32))
                    sg1 = _s3(nc.sbuf_tensor("sg1", [128, 256], F32))
                    colt = _s3(nc.sbuf_tensor("colt", [128, 1], F32))
                    xsl3 = [_s3(nc.sbuf_tensor(f"xsl3{i}", [128, DC, 256], F32R))
                            for i in range(2)]
                    psZ = _s3(nc.psum_tensor("psZ", [128, 256], F32))
                    psG = _s3(nc.psum_tensor("psG", [8, 256], F32))
                    ldc(z8_s[:, :], z8_c, "c:z8")
                    if upto < 2:
                        raise _Done()
                    tr.dma("POOLQ", "d_wg",
                           lambda e: e.dma_start(out=wg[:, :, :],
                                                 in_=wgP.rearrange("(c p) e -> p c e", p=128)),
                           w=["wg"])
                    for dc in range(DC):
                        tr.op("ACT", lambda e, _dc=dc: e.mul(out=wg[:, _dc, :], in_=wg[:, _dc, :],
                                                             mul=nw_s[:, _dc:_dc + 1]),
                              r=["wg", "c:nw"], w=["wg"])
                    for dc in range(DC):
                        tr.op("PE", lambda e, _dc=dc: e.matmul(
                            psZ[:, 0:2], lhsT=wg[:, _dc, :], rhs=onesr_s[:, 0:2],
                            start=(_dc == 0), stop=(_dc == DC - 1)),
                            r=["wg", "c:onesr", "psZ"], w=["psZ"])
                    tr.op("ACT", lambda e: e.activation(out=colt[:, :], in_=psZ[:, 0:1],
                                                        func=AF.Copy),
                          r=["psZ"], w=["colt"])
                    tr.op("PE", lambda e: e.transpose(psZ[0:1, 0:128], colt[:, :], eyef_s[:, :]),
                          r=["colt", "c:eyef", "psZ"], w=["psZ"])
                    pe_settle(psZ[32:34, 200:202], ["psZ"])
                    tr.op("ACT", lambda e: e.activation(out=wsg_s[0:1, :], in_=psZ[0:1, 0:128],
                                                        func=AF.Copy),
                          r=["psZ"], w=["wsg"])
                    for gx in range(8):
                        b = gx % 2
                        xg = slice(gx * 256, (gx + 1) * 256)
                        g = gx // 2
                        tr.dma("POOLQ", f"d_xsl3{b}",
                               lambda e, _b=b, _xg=xg: e.dma_start(
                                   out=xsl3[_b][:, :, :],
                                   in_=xT.rearrange("(c p) t -> p c t", p=128)[:, :, _xg]),
                               w=[f"xsl3{b}"])
                        def _zg(e, _b=b, _xg=xg):
                            for dc in range(DC):
                                e.matmul(psZ[:, :], lhsT=wg[:, dc, :], rhs=xsl3[_b][:, dc, :],
                                         start=(dc == 0), stop=False)
                            return e.matmul(psZ[:, :], lhsT=wsg_s[0:1, :],
                                            rhs=m1r_s[0:1, _xg], start=False, stop=True)
                        tr.op("PE", _zg, r=["wg", f"xsl3{b}", "wsg", f"m1r.{g}", "psZ"],
                              w=["psZ"])
                        tr.op("DVE", lambda e, _xg=xg: e.tensor_mul(zt[:, :], psZ[:, :],
                                                                    r1rep_s[:, _xg]),
                              r=["psZ", f"r1rep.{g}"], w=["zt"])
                        tr.op("DVE", lambda e: e.tensor_scalar(out=zt[:, :], in0=zt[:, :],
                                                               scalar1=b128_s[:, :], scalar2=None,
                                                               op0=ALU.add),
                              r=["zt", "c:b128"], w=["zt"])
                        tr.op("ACT", lambda e: e.activation(out=sg1[:, :], in_=zt[:, :],
                                                            func=AF.Abs),
                              r=["zt"], w=["sg1"])
                        tr.op("ACT", lambda e: e.activation(out=sg1[:, :], in_=sg1[:, :],
                                                            func=AF.Exp, scale=-1.0),
                              r=["sg1"], w=["sg1"])
                        tr.op("ACT", lambda e: e.activation(out=sg1[:, :], in_=sg1[:, :],
                                                            func=AF.Ln, bias=cb_s[:, 1:2]),
                              r=["sg1", "c:cb"], w=["sg1"])
                        tr.op("DVE", lambda e: e.tensor_scalar_min(zt[:, :], zt[:, :], 0.0),
                              r=["zt"], w=["zt"])
                        tr.op("DVE", lambda e, _xg=xg: e.tensor_sub(ls_s[:, _xg], zt[:, :],
                                                                    sg1[:, :]),
                              r=["zt", "sg1"], w=[f"ls.{gx}"])
                        tr.op("PE", lambda e, _xg=xg: e.matmul(
                            psG[:, :], lhsT=sel_s[:, 0:8], rhs=ls_s[:, _xg],
                            start=True, stop=True),
                            r=["c:sel", f"ls.{gx}", "psG"], w=["psG"])
                        tr.op("ACT", lambda e, _xg=xg: e.activation(out=lhat_s[:, _xg],
                                                                    in_=psG[:, :], func=AF.Copy),
                              r=["psG"], w=[f"lo.{gx}"])
                        tr.op("PE", lambda e, _xg=xg: e.matmul(
                            psG[:, :], lhsT=sel_s[:, 8:16], rhs=ls_s[:, _xg],
                            start=True, stop=True),
                            r=["c:sel", f"ls.{gx}", "psG"], w=["psG"])
                        tr.op("ACT", lambda e, _xg=xg: e.activation(out=lgml_s[:, _xg],
                                                                    in_=psG[:, :], func=AF.Copy),
                              r=["psG"], w=[f"lg.{gx}"])
                    for c in range(NCH):
                        sl = slice(c * 128, (c + 1) * 128)
                        gx = c // 2
                        tr.op("DVE", lambda e, _sl=sl: e.tensor_tensor_scan(
                            out=lbuf_s[:, _sl], data0=ls_s[0:8, _sl], data1=z8_s[:, :],
                            initial=0.0, op0=ALU.add, op1=ALU.add),
                            r=[f"ls.{gx}", "c:z8"], w=[f"lbuf.{c}"])
                        tr.op("DVE", lambda e, _sl=sl: e.tensor_add(
                            lhat_s[:, _sl], lhat_s[:, _sl], lbuf_s[:, _sl]),
                            r=[f"lbuf.{c}", f"lo.{gx}"], w=[f"lhat.{c}"])
                        tr.op("DVE", lambda e, _sl=sl: e.tensor_sub(
                            lgml_s[:, _sl], lgml_s[:, _sl], lbuf_s[:, _sl]),
                            r=[f"lbuf.{c}", f"lg.{gx}"], w=[f"lgml.{c}"])

            tr.barrier()
            with ExitStack() as ph4:
                _e4 = ph4.enter_context
                wp = _e4(nc.sbuf_tensor("wp", [128, DC, D], BF16))
                onesb = _e4(nc.sbuf_tensor("onesb", [128, 2], BF16))
                wsrow = _e4(nc.sbuf_tensor("wsrow", [1, D], BF16))
                wsload = _e4(nc.sbuf_tensor("wsload", [128, 1], F32))
                ropeA = _e4(nc.sbuf_tensor("ropeA", [128, 512], BF16))
                tmp1 = _e4(nc.sbuf_tensor("tmp1", [128, 512], BF16))
                tmp2 = _e4(nc.sbuf_tensor("tmp2", [128, 512], BF16))
                xsl4 = [_e4(nc.sbuf_tensor(f"xsl4{i}", [128, DC, 512], BF16))
                        for i in range(2)]
                psP0 = _e4(nc.psum_tensor("psP0", [128, 512], F32))
                psP1 = _e4(nc.psum_tensor("psP1", [128, 512], F32))
                psRot = _e4(nc.psum_tensor("psRot", [128, 512], F32))
                psW = _e4(nc.psum_tensor("psW", [128, 512], F32))
                tr.op("DVE", lambda e: e.tensor_copy(out=onesb[:, :], in_=onesf_s[:, 0:2]),
                      r=["c:onesf"], w=["c:onesb"])

                def load_xsl4(b, g):
                    tr.dma("POOLQ", f"d_xsl4{b}",
                           lambda e, _b=b, _g=g: e.dma_start(
                               out=xsl4[_b][:, :, :],
                               in_=xT.rearrange("(c p) t -> p c t", p=128)
                               [:, :, _g * 512:(_g + 1) * 512]),
                           w=[f"xsl4{b}"])

                def load_wp(src):
                    tr.dma("POOLQ", "d_wp",
                           lambda e, _s=src: e.dma_start(
                               out=wp[:, :, :],
                               in_=_s.rearrange("(c p) e -> p c e", p=128)),
                           w=["wp"])
                    def _fold(e):
                        for dc in range(DC):
                            i = e.mul(out=wp[:, dc, :], in_=wp[:, dc, :],
                                      mul=nw_s[:, dc:dc + 1])
                        return i
                    tr.op("ACT", _fold, r=["wp", "c:nw"], w=["wp"])

                def colsums():
                    for ec in range(DC):
                        def _cs(e, _ec=ec):
                            for dc in range(DC):
                                i = e.matmul(psW[:, 0:2],
                                             lhsT=wp[:, dc, _ec * 128:(_ec + 1) * 128],
                                             rhs=onesb[:, :], start=(dc == 0),
                                             stop=(dc == DC - 1))
                            return i
                        tr.op("PE", _cs, r=["wp", "c:onesb", "psW"], w=["psW"])
                        tr.op("ACT", lambda e: e.activation(out=wsload[:, :], in_=psW[:, 0:1],
                                                            func=AF.Copy),
                              r=["psW"], w=["wsload"])
                        tr.op("PE", lambda e: e.transpose(psW[0:1, 0:128], wsload[:, :],
                                                          eyef_s[:, :]),
                              r=["wsload", "c:eyef", "psW"], w=["psW"])
                        pe_settle(psW[32:34, 200:202], ["psW", "wsload"])
                        tr.op("ACT", lambda e, _ec=ec: e.activation(
                            out=wsrow[0:1, _ec * 128:(_ec + 1) * 128], in_=psW[0:1, 0:128],
                            func=AF.Copy),
                            r=["psW"], w=["wsrow"])

                load_wp(wvT)
                colsums()
                for g in range(NG):
                    b = g % 2
                    load_xsl4(b, g)
                    for tc4 in range(4):
                        c = g * 4 + tc4
                        for eh in range(2):
                            pp, pk = (psP0, "psP0") if eh == 0 else (psP1, "psP1")
                            def _vg(e, _b=b, _tc=tc4, _eh=eh, _pp=pp, _c=c):
                                for dc in range(DC):
                                    e.matmul(_pp[:, :],
                                             lhsT=xsl4[_b][:, dc, _tc * 128:(_tc + 1) * 128],
                                             rhs=wp[:, dc, _eh * 512:(_eh + 1) * 512],
                                             start=(dc == 0), stop=False)
                                return e.matmul(_pp[:, :],
                                                lhsT=m1rb_s[0:1, _c * 128:(_c + 1) * 128],
                                                rhs=wsrow[0:1, _eh * 512:(_eh + 1) * 512],
                                                start=False, stop=True)
                            tr.op("PE", _vg,
                                  r=[f"xsl4{b}", "wp", "wsrow", f"m1rb.{c // 4}", pk], w=[pk])
                            tr.op("DVE", lambda e, _c=c, _eh=eh, _pp=pp: e.tensor_scalar(
                                out=vb[:, _c, _eh * 512:(_eh + 1) * 512], in0=_pp[:, :],
                                scalar1=r1c_s[:, _c:_c + 1], scalar2=None, op0=ALU.mult),
                                r=[pk, f"r1c.{c}"], w=[f"vb.{c}"])

                for (proj, wsrc, dst, ct, st_, dkey) in (
                        ("k", wkT, kb, ctk, stk, "kb"), ("q", wqT, qb, ctq, stq, "qb")):
                    load_wp(wsrc)
                    colsums()
                    for g in range(NG):
                        b = g % 2
                        sl = slice(g * 512, (g + 1) * 512)
                        load_xsl4(b, g)
                        for h in range(NH):
                            pp, pk = (psP0, "psP0") if h % 2 == 0 else (psP1, "psP1")
                            def _qg(e, _b=b, _h=h, _sl=sl, _pp=pp):
                                for dc in range(DC):
                                    e.matmul(_pp[:, :],
                                             lhsT=wp[:, dc, _h * 128:(_h + 1) * 128],
                                             rhs=xsl4[_b][:, dc, :], start=(dc == 0),
                                             stop=False)
                                return e.matmul(_pp[:, :],
                                                lhsT=wsrow[0:1, _h * 128:(_h + 1) * 128],
                                                rhs=m1rb_s[0:1, _sl], start=False, stop=True)
                            tr.op("PE", _qg,
                                  r=["wp", f"xsl4{b}", "wsrow", f"m1rb.{g}", pk], w=[pk])
                            tr.op("ACT", lambda e, _pp=pp: e.activation(
                                out=ropeA[:, :], in_=_pp[:, :], func=AF.Copy),
                                r=[pk], w=["ropeA"])
                            tr.op("PE", lambda e: e.matmul(
                                psRot[:, :], lhsT=rotT_s[:, :], rhs=ropeA[:, :],
                                start=True, stop=True),
                                r=["c:rot", "ropeA", "psRot"], w=["psRot"])
                            tr.op("DVE", lambda e, _ct=ct, _sl=sl: e.tensor_mul(
                                tmp1[:, :], ropeA[:, :], _ct[:, _sl]),
                                r=["ropeA", f"ct{proj}.{g}"], w=["tmp1"])
                            tr.op("DVE", lambda e, _st=st_, _sl=sl: e.tensor_mul(
                                tmp2[:, :], psRot[:, :], _st[:, _sl]),
                                r=["psRot", f"st{proj}.{g}"], w=["tmp2"])
                            tr.op("DVE", lambda e, _d=dst, _h=h, _sl=sl: e.tensor_add(
                                _d[:, _h, _sl], tmp1[:, :], tmp2[:, :]),
                                r=["tmp1", "tmp2"], w=[f"{dkey}.{h}.g{g}"])

            subAB.close()

            tr.barrier()
            with ExitStack() as ph5:
                _e5 = ph5.enter_context
                woutb = _e5(nc.sbuf_tensor("woutb", [128, NH, D], BF16))
                tmg = _e5(nc.sbuf_tensor("tmg", [128, 16], F32))
                lrep = _e5(nc.sbuf_tensor("lrep", [128, NH, 128], F32))
                tbc = _e5(nc.sbuf_tensor("tbc", [128, NH * 128], F32))
                aexp = _e5(nc.sbuf_tensor("aexp", [128, NH * 128], BF16))
                pbt = _e5(nc.sbuf_tensor("pbt", [128, NH * 128], BF16))
                qtl = _e5(nc.sbuf_tensor("qtl", [128, NH, 128], BF16))
                ktm = _e5(nc.sbuf_tensor("ktm", [128, NH, 128], BF16))
                vtl = _e5(nc.sbuf_tensor("vtl", [128, NH, 128], BF16))
                msb = _e5(nc.sbuf_tensor("msb", [128, NH, DH], F32))
                mbb = _e5(nc.sbuf_tensor("mbb", [128, NH, DH], BF16))
                ue8 = _e5(nc.sbuf_tensor("ue8", [128, 8], F32))
                ac8 = _e5(nc.sbuf_tensor("ac8", [128, 8], F32))
                yring = _e5(nc.sbuf_tensor("yring", [128, NH, 512], BF16))
                aosl = _e5(nc.sbuf_tensor("aosl", [128, DC, 256], F32))
                psA = _e5(nc.psum_tensor("psA", [128, NH * 128], F32))
                psT = _e5(nc.psum_tensor("psT", [128, NH * 128], F32))
                psE = _e5(nc.psum_tensor("psE", [128, NH * 128], F32))
                psKT = _e5(nc.psum_tensor("psKT", [128, NH * 128], BF16))
                psM = _e5(nc.psum_tensor("psM", [128, 512], F32))
                if upto < 4:
                    raise _Done()
                tr.dma("POOLQ", "d_wout",
                       lambda e: e.dma_start(out=woutb[:, :, :],
                                             in_=woutT.rearrange("(c p) e -> p c e", p=128)),
                       w=["woutb"])

                bankA = lambda h: f"psA.{h // 4}"
                bankT = lambda h: f"psT.{h // 4}"
                bankE = lambda h: f"psE.{h // 4}"
                ALLA = ["psA.0", "psA.1"]
                ALLT = ["psT.0", "psT.1"]
                ALLE = ["psE.0", "psE.1"]

                for g in range(NG):
                    for tc4 in range(4):
                        c = g * 4 + tc4
                        csl = slice(c * 128, (c + 1) * 128)
                        gx = c // 2
                        tr.op("PE", lambda e, _csl=csl: e.transpose(
                            psM[:, 0:8], lhat_s[:, _csl], eyef_s[0:8, 0:8]),
                            r=[f"lhat.{c}", "c:eyef", "psM"], w=["psM"])
                        tr.op("PE", lambda e, _csl=csl: e.transpose(
                            psM[:, 8:16], lgml_s[:, _csl], eyef_s[0:8, 0:8]),
                            r=[f"lgml.{c}", "c:eyef", "psM"], w=["psM"])
                        for h in range(NH):
                            def _skt(e, _h=h, _csl=csl):
                                e.transpose(psKT[:, _h * 128:(_h + 1) * 128],
                                            kb[:, _h, _csl], eyeb_s[:, :])
                                return e.matmul(psA[:, _h * 128:(_h + 1) * 128],
                                                lhsT=kb[:, _h, _csl], rhs=qb[:, _h, _csl],
                                                start=True, stop=True)
                            tr.op("PE", _skt,
                                  r=[f"kb.{h}.g{g}", f"qb.{h}.g{g}", "c:eyeb", "psKT",
                                     bankA(h)],
                                  w=[bankA(h), "psKT"] + (["psM"] if h == 0 else []))
                        tr.op("ACT", lambda e: e.activation(out=tmg[:, :], in_=psM[:, 0:16],
                                                            func=AF.Copy),
                              r=["psM"], w=["tmg"])
                        if SUB5 < 2:
                            continue
                        for h in range(NH):
                            tr.op("DVE", lambda e, _h=h: e.tensor_copy(
                                out=lrep[:, _h, :],
                                in_=tmg[:, _h:_h + 1].broadcast_to([128, 128])),
                                r=["tmg", "lrepfence"], w=[f"lrep.{h}"])
                            tr.op("PE", lambda e, _h=h: e.transpose(
                                psT[:, _h * 128:(_h + 1) * 128], lrep[:, _h, :], eyef_s[:, :]),
                                r=[f"lrep.{h}", "c:eyef", bankT(h)], w=[bankT(h)])
                        def _lcg(e, _c=c):
                            for h in range(NH):
                                i = e.matmul(psM[:, h:h + 1],
                                             lhsT=onehot_s[:, h * 128:(h + 1) * 128],
                                             rhs=lbuf_s[:, _c * 128 + 127:_c * 128 + 128],
                                             start=True, stop=True)
                            return i
                        tr.op("PE", _lcg, r=["c:oh", f"lbuf.{c}", "psM"],
                              w=["psM", "psT.0", "psT.1", "lrepfence"])
                        if SUB5 < 3:
                            continue
                        for bk in range(2):
                            tr.op("ACT", lambda e, _b=bk: e.activation(
                                out=tbc[:, _b * 512:(_b + 1) * 512],
                                in_=psT[:, _b * 512:(_b + 1) * 512], func=AF.Copy),
                                r=[f"psT.{bk}"], w=["tbc"])
                        tr.op("ACT", lambda e: e.activation(out=aexp[:, :], in_=tbc[:, :],
                                                            func=AF.Exp),
                              r=["tbc"], w=["aexp"])
                        if SUB5 == 30:
                            continue
                        def _eadds(e):
                            for h in range(NH):
                                i = e.tensor_scalar(
                                    out=tbc[:, h * 128:(h + 1) * 128],
                                    in0=tbc[:, h * 128:(h + 1) * 128],
                                    scalar1=tmg[:, 8 + h:9 + h], scalar2=None, op0=ALU.add)
                            return i
                        tr.op("DVE", _eadds, r=["tbc", "tmg", "aexp"], w=["tbc"])
                        if SUB5 < 4:
                            continue
                        tr.op("DVE", lambda e: e.tensor_add(
                            tbc[:, :].rearrange("p (h t) -> p h t", h=NH),
                            tbc[:, :].rearrange("p (h t) -> p h t", h=NH),
                            lmask_s[:, :].rearrange("p (o t) -> p o t", o=1)
                            .broadcast_to([128, NH, 128])),
                            r=["tbc", "c:lm"], w=["tbc"])
                        if SUB5 < 5:
                            continue
                        tr.op("ACT", lambda e: e.activation(out=tbc[:, :], in_=tbc[:, :],
                                                            func=AF.Exp),
                              r=["tbc"], w=["tbc"])
                        for bk in range(2):
                            tr.op("DVE", lambda e, _b=bk: e.tensor_mul(
                                pbt[:, _b * 512:(_b + 1) * 512],
                                tbc[:, _b * 512:(_b + 1) * 512],
                                psA[:, _b * 512:(_b + 1) * 512]),
                                r=["tbc", f"psA.{bk}"], w=["pbt"])
                        if SUB5 < 6:
                            continue
                        def _qtg(e, _csl=csl):
                            for h in range(NH):
                                i = e.tensor_mul(qtl[:, h, :], qb[:, h, _csl],
                                                 aexp[:, h * 128:(h + 1) * 128])
                            return i
                        tr.op("DVE", _qtg,
                              r=[f"qb.{h}.g{g}" for h in range(NH)] + ["aexp"],
                              w=["qtl"])
                        tr.op("ACT", lambda e: e.activation(
                            out=ktm[:, :, :].rearrange("p h t -> p (h t)"), in_=psKT[:, :],
                            func=AF.Copy),
                            r=["psKT"], w=["ktm"])
                        tr.op("DVE", lambda e: e.tensor_add(ue8[:, :], psM[:, 0:8], tmg[:, 8:16]),
                              r=["psM", "tmg"], w=["ue8"])
                        tr.op("ACT", lambda e: e.activation(out=ue8[:, :], in_=ue8[:, :],
                                                            func=AF.Exp),
                              r=["ue8"], w=["ue8"])
                        tr.op("ACT", lambda e: e.activation(out=ac8[:, :], in_=psM[:, 0:8],
                                                            func=AF.Exp),
                              r=["psM"], w=["ac8"])
                        if SUB5 < 7:
                            continue
                        if c < NCH - 1:
                            def _vtg(e, _c=c):
                                for h in range(NH):
                                    i = e.tensor_scalar(
                                        out=vtl[:, h, :],
                                        in0=vb[:, _c, h * 128:(h + 1) * 128],
                                        scalar1=ue8[:, h:h + 1], scalar2=None, op0=ALU.mult)
                                return i
                            tr.op("DVE", _vtg, r=[f"vb.{c}", "ue8"], w=["vtl"])
                        for h in range(NH):
                            def _yg(e, _h=h, _c=c):
                                i = e.matmul(psE[:, _h * 128:(_h + 1) * 128],
                                             lhsT=vb[:, _c, _h * 128:(_h + 1) * 128],
                                             rhs=pbt[:, _h * 128:(_h + 1) * 128],
                                             start=True, stop=(_c == 0))
                                if _c > 0:
                                    i = e.matmul(psE[:, _h * 128:(_h + 1) * 128],
                                                 lhsT=mbb[:, _h, :], rhs=qtl[:, _h, :],
                                                 start=False, stop=True)
                                return i
                            tr.op("PE", _yg,
                                  r=[f"vb.{c}", "pbt", "mbb", "qtl", bankE(h)], w=[bankE(h)])
                            if c < NCH - 1:
                                tr.op("PE", lambda e, _h=h: e.matmul(
                                    psA[:, _h * 128:(_h + 1) * 128], lhsT=ktm[:, _h, :],
                                    rhs=vtl[:, _h, :], start=True, stop=True),
                                    r=["ktm", "vtl", bankA(h)], w=[bankA(h)])
                        for bk in range(2):
                            tr.op("DVE", lambda e, _tc=tc4, _b=bk: e.tensor_copy(
                                out=yring[:, 4 * _b:4 * (_b + 1), _tc * 128:(_tc + 1) * 128],
                                in_=psE[:, _b * 512:(_b + 1) * 512]
                                .rearrange("p (h t) -> p h t", h=4)),
                                r=[f"psE.{bk}"], w=[f"yring.{tc4}"])
                        if SUB5 < 8:
                            continue
                        if c < NCH - 1:
                            if c == 0:
                                for bk in range(2):
                                    tr.op("DVE", lambda e, _b=bk: e.tensor_copy(
                                        out=msb[:, 4 * _b:4 * (_b + 1), :],
                                        in_=psA[:, _b * 512:(_b + 1) * 512]
                                        .rearrange("p (h t) -> p h t", h=4)),
                                        r=[f"psA.{bk}"], w=["msb"])
                            else:
                                def _mdec(e):
                                    for h in range(NH):
                                        i = e.tensor_scalar(
                                            out=msb[:, h, :], in0=msb[:, h, :],
                                            scalar1=ac8[:, h:h + 1], scalar2=None,
                                            op0=ALU.mult)
                                    return i
                                tr.op("DVE", _mdec, r=["msb", "ac8"], w=["msb"])
                                for bk in range(2):
                                    tr.op("DVE", lambda e, _b=bk: e.tensor_add(
                                        msb[:, 4 * _b:4 * (_b + 1), :],
                                        msb[:, 4 * _b:4 * (_b + 1), :],
                                        psA[:, _b * 512:(_b + 1) * 512]
                                        .rearrange("p (h t) -> p h t", h=4)),
                                        r=["msb", f"psA.{bk}"], w=["msb"])
                            tr.op("ACT", lambda e: e.activation(
                                out=mbb[:, :, :].rearrange("p h t -> p (h t)"),
                                in_=msb[:, :, :].rearrange("p h t -> p (h t)"), func=AF.Copy),
                                r=["msb"], w=["mbb"])
                    if SUB5 < 9:
                        continue
                    for wh in range(2):
                        for dcp in range(DC):
                            if dcp % 2 == 0:
                                pp, rk = psM, ["psM"]
                            else:
                                pp, rk = psE, ["psE.0"]
                            def _wog(e, _dcp=dcp, _pp=pp, _wh=wh):
                                for h in range(NH):
                                    i = e.matmul(
                                        _pp[:, 0:256],
                                        lhsT=woutb[:, h, _dcp * 128:(_dcp + 1) * 128],
                                        rhs=yring[:, h, _wh * 256:(_wh + 1) * 256],
                                        start=(h == 0), stop=(h == NH - 1))
                                return i
                            tr.op("PE", _wog,
                                  r=["woutb"] + [f"yring.{i}" for i in range(4)] + rk, w=rk)
                            tr.op("DVE", lambda e, _dcp=dcp, _pp=pp: e.tensor_scalar(
                                out=aosl[:, _dcp, :], in0=_pp[:, 0:256],
                                scalar1=bout_s[:, _dcp:_dcp + 1], scalar2=None, op0=ALU.add),
                                r=rk + ["c:bout"], w=["aosl"])
                        tr.dma("POOLQ", "d_ao",
                               lambda e, _g=g, _wh=wh: e.dma_start(
                                   out=x1h.rearrange("(c p) t -> p c t", p=128)
                                   [:, :, _g * 512 + _wh * 256:_g * 512 + (_wh + 1) * 256],
                                   in_=aosl[:, :, :], accum_op=ALU.add),
                               r=["aosl", f"x1h.{g}"], w=[f"x1h.{g}"])

        tr.barrier()
        with ExitStack() as ph6:
            _e6 = ph6.enter_context
            wib = _e6(nc.sbuf_tensor("wibuf", [128, DC, FF], BF16))
            wob = _e6(nc.sbuf_tensor("wobuf", [128, FC, D], BF16))
            bcat = _e6(nc.sbuf_tensor("bcat", [32, FF], BF16))
            gab = _e6(nc.sbuf_tensor("gab", [128, DC, 8], BF16))
            aab = _e6(nc.sbuf_tensor("aab", [128, DC, 32], BF16))
            x1sl = _e6(nc.sbuf_tensor("x1sl", [128, DC, 512], BF16))
            onesb6 = _e6(nc.sbuf_tensor("onesb6", [128, 128], BF16))
            fsc = _e6(nc.sbuf_tensor("fsc", [128, 512], BF16))
            xnb = [_e6(nc.sbuf_tensor(f"xnb{i}", [128, DC, 512], BF16)) for i in range(2)]
            inter = _e6(nc.sbuf_tensor("inter", [128, FC, 512], BF16))
            x2sl = [_e6(nc.sbuf_tensor("x2sl0", [128, 512], F32))] * 2
            sqscr = _e6(nc.sbuf_tensor("sqscr", [128, 512], BF16))
            m2bc = _e6(nc.sbuf_tensor("m2bc", [128, 512], F32))
            v2bc = _e6(nc.sbuf_tensor("v2bc", [128, 512], F32))
            rp8 = _e6(nc.sbuf_tensor("rp8", [8, 512], F32))
            pp32 = _e6(nc.sbuf_tensor("pp32", [32, 512], BF16))
            tm8 = _e6(nc.sbuf_tensor("tm8", [128, 32], F32))
            mx8 = _e6(nc.sbuf_tensor("mx8", [128, 8], F32))
            ix8 = _e6(nc.sbuf_tensor("ix8", [128, 8], U32))
            ixf = _e6(nc.sbuf_tensor("ixf", [128, 4], F32))
            m8t = _e6(nc.sbuf_tensor("m8t", [128, 32], F32))
            m8f = _e6(nc.sbuf_tensor("m8f", [8, 512], BF16))
            psS1 = _e6(nc.psum_tensor("psS1", [128, 512], F32))
            psS2 = _e6(nc.psum_tensor("psS2", [128, 512], F32))
            psR8 = _e6(nc.psum_tensor("psR8", [128, 512], F32))
            psP32 = _e6(nc.psum_tensor("psP32", [32, 512], F32))
            psF0 = _e6(nc.psum_tensor("psF0", [128, 512], F32))
            psF1 = _e6(nc.psum_tensor("psF1", [128, 512], F32))
            psO0 = _e6(nc.psum_tensor("psO0", [128, 512], F32))
            psO1 = _e6(nc.psum_tensor("psO1", [128, 512], F32))
            if upto < 5:
                raise _Done()
            for i in range(4):
                tr.dma("POOLQ", f"d_wi{i}",
                       lambda e, _i=i: e.dma_start(
                           out=wib[:, 2 * _i:2 * _i + 2, :],
                           in_=wiT.rearrange("(c p) f -> p c f", p=128)[:, 2 * _i:2 * _i + 2, :]),
                       w=[f"wib.{i}"])
                tr.dma("POOLQ", f"d_wo{i}",
                       lambda e, _i=i: e.dma_start(
                           out=wob[:, 8 * _i:8 * _i + 8, :],
                           in_=woT.rearrange("(c p) f -> p c f", p=128)[:, 8 * _i:8 * _i + 8, :]),
                       w=[f"wob.{i}"])
            tr.dma("POOLQ", "d_bcat", lambda e: e.dma_start(out=bcat[:, :], in_=bcatT),
                   w=["bcat"])
            tr.op("DVE", lambda e: e.tensor_copy(out=onesb6[:, :], in_=onesf_s[:, :]),
                  r=["c:onesf"], w=["c:onesb6"])
            tr.dma("POOLQ", "d_ga",
                   lambda e: e.dma_start(out=gab[:, :, :],
                                         in_=gw8T.rearrange("(c p) e -> p c e", p=128)),
                   w=["gab"])
            tr.dma("POOLQ", "d_aa",
                   lambda e: e.dma_start(out=aab[:, :, :],
                                         in_=acatT.rearrange("(c p) e -> p c e", p=128)),
                   w=["aab"])
            def frontA(g):
                bb = g % 2
                tr.dma("POOLQ", "d_x1sl",
                       lambda e, _g=g: e.dma_start(
                           out=x1sl[:, :, :],
                           in_=x1h.rearrange("(c p) t -> p c t", p=128)
                           [:, :, _g * 512:(_g + 1) * 512]),
                       r=[f"x1h.{g}"], w=["x1sl"])
                tr.dma("SYNCQ", f"d_oinit{g}",
                       lambda e, _g=g: e.dma_start(
                           out=outT[:, _g * 512:(_g + 1) * 512],
                           in_=x1h[:, _g * 512:(_g + 1) * 512]),
                       r=[f"x1h.{g}"], w=[f"outT.{g}"])
                def _s1g(e):
                    for dc in range(DC):
                        i = e.matmul(psS1[:, :], lhsT=onesb6[:, :], rhs=x1sl[:, dc, :],
                                     start=(dc == 0), stop=(dc == DC - 1))
                    return i
                tr.op("PE", _s1g, r=["c:onesb6", "x1sl", "psS1"], w=["psS1"])
                for dc in range(DC):
                    tr.op("ACT", lambda e, _dc=dc: e.activation(out=sqscr[:, :],
                                                                in_=x1sl[:, _dc, :],
                                                                func=AF.Square),
                          r=["x1sl"], w=["sqscr"])
                    tr.op("PE", lambda e, _dc=dc: e.matmul(
                        psS2[:, :], lhsT=onesb6[:, :], rhs=sqscr[:, :],
                        start=(_dc == 0), stop=(_dc == DC - 1)),
                        r=["c:onesb6", "sqscr", "psS2"], w=["psS2"])
                tr.op("ACT", lambda e: e.activation(out=m2bc[:, :], in_=psS1[:, :],
                                                    func=AF.Copy, scale=1.0 / D),
                      r=["psS1"], w=["m2bc"])
                tr.op("ACT", lambda e: e.activation(out=v2bc[:, :], in_=psS2[:, :],
                                                    func=AF.Copy, scale=1.0 / D),
                      r=["psS2"], w=["v2bc"])
                tr.op("DVE", lambda e: e.tensor_mul(fsc[:, :], m2bc[:, :], m2bc[:, :]),
                      r=["m2bc"], w=["fsc"])
                tr.op("DVE", lambda e: e.tensor_sub(v2bc[:, :], v2bc[:, :], fsc[:, :]),
                      r=["v2bc", "fsc"], w=["v2bc"])
                tr.op("ACT", lambda e: e.activation(out=v2bc[:, :], in_=v2bc[:, :],
                                                    func=AF.Ln, bias=cb_s[:, 2:3]),
                      r=["v2bc", "c:cb"], w=["v2bc"])
                tr.op("ACT", lambda e: e.activation(out=v2bc[:, :], in_=v2bc[:, :],
                                                    func=AF.Exp, scale=-0.5),
                      r=["v2bc"], w=["v2bc"])
                for dc in range(DC):
                    tr.op("DVE", lambda e, _dc=dc: e.tensor_sub(fsc[:, :], x1sl[:, _dc, :],
                                                                m2bc[:, :]),
                          r=["x1sl", "m2bc"], w=["fsc"])
                    tr.op("DVE", lambda e, _dc=dc, _bb=bb: e.tensor_mul(
                        xnb[_bb][:, _dc, :], fsc[:, :], v2bc[:, :]),
                        r=["fsc", "v2bc"], w=[f"xnb{bb}"])

            def frontB(g):
                bb = g % 2
                def _rg(e, _bb=bb):
                    for dc in range(DC):
                        e.matmul(psR8[0:8, :], lhsT=gab[:, dc, :], rhs=xnb[_bb][:, dc, :],
                                 start=(dc == 0), stop=(dc == DC - 1))
                        i = e.matmul(psP32[:, :], lhsT=aab[:, dc, :], rhs=xnb[_bb][:, dc, :],
                                     start=(dc == 0), stop=(dc == DC - 1))
                    return i
                tr.op("PE", _rg, r=["gab", "aab", f"xnb{bb}", "psR8", "psP32"],
                      w=["psR8", "psP32"])
                tr.op("DVE", lambda e: e.tensor_scalar(out=rp8[:, :], in0=psR8[0:8, :],
                                                       scalar1=g8_s[:, :], scalar2=None,
                                                       op0=ALU.add),
                      r=["psR8", "c:g8"], w=["rp8"])
                tr.op("ACT", lambda e: e.activation(out=pp32[:, :], in_=psP32[:, :],
                                                    func=AF.Copy),
                      r=["psP32"], w=["pp32"])
                for tc4 in range(4):
                    tr.op("PE", lambda e, _tc=tc4: e.transpose(
                        psS1[:, _tc * 8:(_tc + 1) * 8], rp8[:, _tc * 128:(_tc + 1) * 128],
                        eyef_s[0:8, 0:8]),
                        r=["rp8", "c:eyef", "psS1"], w=["psS1"])
                pe_settle(psS1[32:34, 40:42], ["psS1", "rp8"])
                tr.op("ACT", lambda e: e.activation(
                    out=tm8[:, :], in_=psS1[:, 0:32], func=AF.Copy),
                    r=["psS1"], w=["tm8"])
                for tc4 in range(4):
                    tr.op("DVE", lambda e, _tc=tc4: e.max(
                        out=mx8[:, :], in_=tm8[:, _tc * 8:(_tc + 1) * 8]),
                        r=["tm8"], w=["mx8"])
                    tr.op("DVE", lambda e, _tc=tc4: e.max_index(
                        out=ix8[:, :], in_max=mx8[:, :],
                        in_values=tm8[:, _tc * 8:(_tc + 1) * 8]),
                        r=["mx8", "tm8"], w=["ix8"])
                    tr.op("DVE", lambda e, _tc=tc4: e.tensor_copy(
                        out=ixf[:, _tc:_tc + 1], in_=ix8[:, 0:1]),
                        r=["ix8"], w=["ixf"])
                    tr.op("DVE", lambda e, _tc=tc4: e.tensor_scalar(
                        out=m8t[:, _tc * 8:(_tc + 1) * 8], in0=iota8_s[:, :],
                        scalar1=ixf[:, _tc:_tc + 1], scalar2=None, op0=ALU.is_equal),
                        r=["c:iota", "ixf"], w=["m8t"])
                for tc4 in range(4):
                    tr.op("PE", lambda e, _tc=tc4: e.transpose(
                        psR8[0:8, _tc * 128:(_tc + 1) * 128], m8t[:, _tc * 8:(_tc + 1) * 8],
                        eyef_s[:, :]),
                        r=["m8t", "c:eyef", "psR8"], w=["psR8"])
                pe_settle(psR8[32:34, 40:42], ["psR8", "m8t"])
                tr.op("ACT", lambda e: e.activation(
                    out=m8f[:, :], in_=psR8[0:8, :], func=AF.Copy),
                    r=["psR8"], w=["m8f"])
                tr.op("PE", lambda e: e.matmul(psP32[:, :], lhsT=expand_s[:, :], rhs=m8f[:, :],
                                               start=True, stop=True),
                      r=["c:exp", "m8f", "psP32"], w=["psP32"])
                tr.op("DVE", lambda e: e.tensor_mul(pp32[:, :], pp32[:, :], psP32[:, :]),
                      r=["pp32", "psP32"], w=["pp32"])

            def body_wi(g):
                bb = g % 2
                for fc in range(FC):
                    pp, pk = (psF0, "psF0") if fc % 2 == 0 else (psF1, "psF1")
                    def _wig(e, _fc=fc, _pp=pp, _bb=bb):
                        for dc in range(DC):
                            e.matmul(_pp[:, :],
                                     lhsT=wib[:, dc, _fc * 128:(_fc + 1) * 128],
                                     rhs=xnb[_bb][:, dc, :], start=(dc == 0), stop=False)
                        return e.matmul(_pp[:, :],
                                        lhsT=bcat[:, _fc * 128:(_fc + 1) * 128],
                                        rhs=pp32[:, :], start=False, stop=True)
                    tr.op("PE", _wig,
                          r=[f"wib.{i}" for i in range(4)] + [f"xnb{bb}", "bcat", "pp32", pk],
                          w=[pk])
                    tr.op("ACT", lambda e, _fc=fc, _pp=pp: e.activation(
                        out=inter[:, _fc, :], in_=_pp[:, :], func=AF.Relu,
                        bias=wib_s[:, _fc:_fc + 1]),
                        r=[pk, "c:wib"], w=[f"inter.{fc}"])

            def body_wo(g):
                for dcp in range(DC):
                    pp, pk = (psO0, "psO0") if dcp % 2 == 0 else (psO1, "psO1")
                    def _wog2(e, _dcp=dcp, _pp=pp):
                        for fc in range(FC):
                            i = e.matmul(_pp[:, :],
                                         lhsT=wob[:, fc, _dcp * 128:(_dcp + 1) * 128],
                                         rhs=inter[:, fc, :], start=(fc == 0),
                                         stop=(fc == FC - 1))
                        return i
                    tr.op("PE", _wog2,
                          r=[f"wob.{i}" for i in range(4)]
                          + [f"inter.{fc}" for fc in range(FC)] + [pk],
                          w=[pk])
                    tr.op("DVE", lambda e, _dcp=dcp, _pp=pp: e.tensor_scalar(
                        out=x2sl[0][:, :], in0=_pp[:, :], scalar1=wob_s[:, _dcp:_dcp + 1],
                        scalar2=None, op0=ALU.add),
                        r=[pk, "c:wob", "x2sl0"], w=["x2sl0"])
                    tr.dma("POOLQ", "d_out0",
                           lambda e, _g=g, _dcp=dcp: e.dma_start(
                               out=outT[_dcp * 128:(_dcp + 1) * 128,
                                        _g * 512:(_g + 1) * 512],
                               in_=x2sl[0][:, :], accum_op=ALU.add),
                           r=["x2sl0", f"outT.{g}"], w=[f"outT.{g}"])

            frontA(0)
            frontB(0)
            for g in range(NG):
                body_wi(g)
                if g + 1 < NG:
                    frontA(g + 1)
                    frontB(g + 1)
                body_wo(g)
        raise _Done()
    except _Done:
        with ExitStack() as c2:
            tr.final_wait("POOLQ", "d_out0")
            tr.build_blocks(nc, c2)
    return nc


_NC_CACHE = {}


def _get_nc():
    if "nc" not in _NC_CACHE:
        _NC_CACHE["nc"] = build()
    return _NC_CACHE["nc"]


def kernel(**inputs):
    x = np.asarray(inputs["L0_attn_x"])
    nc = _get_nc()
    shared = _shared_arrays(inputs)
    in_maps = []
    for c in range(8):
        m = dict(shared)
        m["x_nat"] = np.ascontiguousarray(x[c])
        m["xT"] = np.ascontiguousarray(x[c].T)
        in_maps.append(m)
    res = run_bass_kernel_spmd(nc, in_maps, core_ids=list(range(8)))
    return np.stack([np.asarray(r["outT"]).T for r in res.results], axis=0)


def _shared_arrays(inputs):
    ii = {k: np.asarray(v) for k, v in inputs.items()}
    wgP = np.zeros((D, 128), np.float32)
    wgP[:, 0:8] = ii["L0_attn_Wbeta"].T
    wgP[:, 32:40] = ii["L0_attn_Wig"].T
    wgP[:, 64:72] = ii["L0_attn_Wog"].T
    b128 = np.zeros((128, 1), np.float32)
    b128[0:8, 0] = ii["L0_attn_bbeta"]
    b128[32:40, 0] = ii["L0_attn_big"]
    b128[64:72, 0] = ii["L0_attn_bog"]
    return {
        "wqT": np.ascontiguousarray(ii["L0_attn_Wq"].T),
        "wkT": np.ascontiguousarray(ii["L0_attn_Wk"].T),
        "wvT": np.ascontiguousarray(ii["L0_attn_Wv"].T),
        "woutT": np.ascontiguousarray(ii["L0_attn_Wout"].T),
        "wgP": wgP, "b128": b128,
        "nw8": np.ascontiguousarray(ii["L0_attn_norm_w"].reshape(DC, 128).T),
        "gw8T": np.ascontiguousarray(ii["L0_ffn_gate_W"].T),
        "acatT": np.ascontiguousarray(
            np.transpose(ii["L0_ffn_lora_A"], (2, 0, 1)).reshape(D, 32)),
        "g8": np.ascontiguousarray(ii["L0_ffn_gate_b"].reshape(8, 1)),
        "bcatT": np.ascontiguousarray(
            np.transpose(ii["L0_ffn_lora_B"], (0, 2, 1)).reshape(32, FF)),
        "wiT": np.ascontiguousarray(ii["L0_ffn_wi_W"].T),
        "woT": np.ascontiguousarray(ii["L0_ffn_wo_W"].T),
        "wib32": np.ascontiguousarray(ii["L0_ffn_wi_b"].reshape(FC, 128).T),
        "wob8": np.ascontiguousarray(ii["L0_ffn_wo_b"].reshape(DC, 128).T),
        "bout8": np.ascontiguousarray(ii["L0_attn_bout"].reshape(DC, 128).T),
    }


# version 08
# speedup vs baseline: 1.0324x; 1.0324x over previous
import numpy as np
from contextlib import ExitStack

import concourse.bass as bass
import concourse.mybir as mybir
from concourse.bass_utils import run_bass_kernel_spmd

F32 = mybir.dt.float32
F32R = mybir.dt.float32r
BF16 = mybir.dt.bfloat16
U32 = mybir.dt.uint32
AF = mybir.ActivationFunctionType
ALU = mybir.AluOpType
AX = mybir.AxisListType

T, D, NH, DH, FF = 2048, 1024, 8, 128, 4096
NCH, NG, DC, FC = 16, 4, 8, 32
SCALE = 1.0 / float(np.sqrt(DH))
NEGBIG = -1e30

EMAP = {"PE": "tensor", "DVE": "vector", "ACT": "scalar", "POOL": "gpsimd",
        "SYNCQ": "sync", "ACTQ": "scalar", "POOLQ": "gpsimd"}
ENGS = ("PE", "DVE", "ACT", "POOL")


class TR:

    def __init__(self):
        self.cnt = {}
        self.lastw = {}
        self.readers = {}
        self.waited = {}
        self.floor = {}
        self.ops = {k: [] for k in ("tensor", "vector", "scalar", "gpsimd", "sync")}
        self.sem_names = set(ENGS)

    def barrier(self):
        self.floor = dict(self.cnt)

    def _wl(self, stream, r, w):
        need = dict(self.floor)
        for k in r:
            lw = self.lastw.get(k)
            if lw:
                need[lw[0]] = max(need.get(lw[0], 0), lw[1])
        for k in w:
            lw = self.lastw.get(k)
            if lw:
                need[lw[0]] = max(need.get(lw[0], 0), lw[1])
            for s2, v2 in self.readers.get(k, {}).items():
                need[s2] = max(need.get(s2, 0), v2)
        out = []
        wt = self.waited.setdefault(stream, {})
        for s2 in sorted(need):
            if wt.get(s2, 0) < need[s2]:
                wt[s2] = need[s2]
                out.append((s2, need[s2]))
        return out

    def _commit(self, sem, val, r, w):
        for k in w:
            self.lastw[k] = (sem, val)
            self.readers[k] = {}
        for k in r:
            self.readers.setdefault(k, {})[sem] = val

    def op(self, eng, fn, r=(), w=()):
        wl = self._wl(eng, r, w)
        self.cnt[eng] = self.cnt.get(eng, 0) + 1
        val = self.cnt[eng]

        def emit(engobj, sems, _wl=wl, _fn=fn, _eng=eng):
            for s2, v2 in _wl:
                engobj.wait_ge(sems[s2], v2)
            _fn(engobj).then_inc(sems[_eng], 1)
        self.ops[EMAP[eng]].append(emit)
        self._commit(eng, val, r, w)

    def dma(self, q, dsem, fn, r=(), w=()):
        self.sem_names.add(dsem)
        wl = self._wl(q, r, w)
        self.cnt[dsem] = self.cnt.get(dsem, 0) + 16
        val = self.cnt[dsem]

        def emit(engobj, sems, _wl=wl, _fn=fn, _ds=dsem):
            for s2, v2 in _wl:
                engobj.wait_ge(sems[s2], v2)
            _fn(engobj).then_inc(sems[_ds], 16)
        self.ops[EMAP[q]].append(emit)
        self._commit(dsem, val, r, w)

    def final_wait(self, stream, sem):
        v = self.cnt.get(sem, 0)
        self.ops[EMAP[stream]].append(
            lambda e, s, _sm=sem, _v=v: e.wait_ge(s[_sm], _v) if _v else None)

    def build_blocks(self, nc, ctx):
        sems = {n: ctx.enter_context(nc.semaphore(f"s_{n}"))
                for n in sorted(self.sem_names)}
        with nc.Block() as block:
            @block.sync
            def _(eng):
                for emit in self.ops["sync"]:
                    emit(eng, sems)

            @block.tensor
            def _(eng):
                for emit in self.ops["tensor"]:
                    emit(eng, sems)

            @block.vector
            def _(eng):
                for emit in self.ops["vector"]:
                    emit(eng, sems)

            @block.scalar
            def _(eng):
                for emit in self.ops["scalar"]:
                    emit(eng, sems)

            @block.gpsimd
            def _(eng):
                for emit in self.ops["gpsimd"]:
                    emit(eng, sems)


SUB5 = 9


def build(upto=9):
    import ml_dtypes
    nc = bass.Bass("TRN2", target_bir_lowering=False, debug=False)

    def din(name, shape, dt=F32):
        return nc.dram_tensor(name, shape, dt, kind="ExternalInput").ap()
    x_nat = din("x_nat", [T, D])
    xT = din("xT", [D, T])
    wqT = din("wqT", [D, D]); wkT = din("wkT", [D, D]); wvT = din("wvT", [D, D])
    woutT = din("woutT", [D, D])
    wgP = din("wgP", [D, 128])
    b128 = din("b128", [128, 1])
    nw8 = din("nw8", [128, DC])
    gw8T = din("gw8T", [D, 8])
    acatT = din("acatT", [D, 32])
    g8 = din("g8", [8, 1])
    bcatT = din("bcatT", [32, FF])
    wiT = din("wiT", [D, FF])
    woT = din("woT", [FF, D])
    wib32 = din("wib32", [128, FC])
    wob8 = din("wob8", [128, DC])
    bout8 = din("bout8", [128, DC])
    outT = nc.dram_tensor("outT", [D, T], F32, kind="ExternalOutput").ap()
    x1h = nc.dram_tensor("x1h", [D, T], F32, kind="Internal").ap()
    wi16 = nc.dram_tensor("wi16", [D, FF], BF16, kind="Internal").ap()
    wo16 = nc.dram_tensor("wo16", [FF, D], BF16, kind="Internal").ap()

    eye_f = nc.inline_tensor(np.eye(128, dtype=np.float32), "eye_f").ap()
    eye_b = nc.inline_tensor(np.eye(128, dtype=ml_dtypes.bfloat16), "eye_b").ap()
    ones_f = nc.inline_tensor(np.ones((128, 128), np.float32), "ones_f").ap()
    lm_np = np.where(np.arange(128)[:, None] <= np.arange(128)[None, :], 0.0, NEGBIG)
    logmask_c = nc.inline_tensor(lm_np.astype(np.float32), "logmask").ap()
    iota8_c = nc.inline_tensor(
        np.broadcast_to(np.arange(8, dtype=np.float32), (128, 8)).copy(), "iota8").ap()
    oh_np = np.zeros((8, 1024), np.float32)
    for h in range(8):
        oh_np[h, h * 128:(h + 1) * 128] = 1.0
    onehot_c = nc.inline_tensor(oh_np, "onehot").ap()
    exp_np = np.zeros((8, 32), ml_dtypes.bfloat16)
    for e in range(8):
        exp_np[e, e * 4:(e + 1) * 4] = 1.0
    expand_c = nc.inline_tensor(exp_np, "expand832").ap()
    rot = np.zeros((128, 128), np.float64)
    rot[np.arange(64), 64 + np.arange(64)] = -1.0
    rot[64 + np.arange(64), np.arange(64)] = 1.0
    rotT_c = nc.inline_tensor(rot.T.astype(ml_dtypes.bfloat16), "rotT").ap()
    sel = np.zeros((128, 16), np.float32)
    sel[64 + np.arange(8), np.arange(8)] = 1.0
    sel[32 + np.arange(8), 8 + np.arange(8)] = 1.0
    sel_c = nc.inline_tensor(sel, "sel16").ap()
    half = DH // 2
    inv_freq = 1.0 / (10000.0 ** (np.arange(half, dtype=np.float64) / half))
    frq = np.arange(T, dtype=np.float64)[None, :] * inv_freq[:, None]
    cs2 = np.concatenate([np.cos(frq), np.cos(frq)], axis=0).astype(np.float32)
    sn2 = np.concatenate([np.sin(frq), np.sin(frq)], axis=0).astype(np.float32)
    cos2_c = nc.inline_tensor(cs2, "cos2").ap()
    sin2_c = nc.inline_tensor(sn2, "sin2").ap()
    z8_c = nc.inline_tensor(np.zeros((8, 128), np.float32), "z8").ap()

    tr = TR()

    class _Done(Exception):
        pass

    def ldc(dst, src, key):
        tr.dma("SYNCQ", "dc_" + key.replace(":", ""),
               lambda e, _d=dst, _s=src: e.dma_start(out=_d, in_=_s), w=[key])

    def pe_settle(ps_ap, keys):
        tr.op("PE", lambda e, _p=ps_ap: e.matmul(
            _p, lhsT=_dummy_lhs[0:1, 0:2], rhs=_dummy_rhs[0:1, 0:2],
            start=True, stop=True), r=["c:onesf"], w=list(keys))

    ctx = ExitStack()
    try:
      with ctx:
        SB = lambda n, s, d=F32: ctx.enter_context(nc.sbuf_tensor(n, s, d))
        eyef_s = SB("eyef", [128, 128])
        onesf_s = SB("onesf", [128, 128]); onesr_s = SB("onesr", [128, 128], F32R)
        iota8_s = SB("iota8s", [128, 8])
        expand_s = SB("expand", [8, 32], BF16)
        nw_s = SB("nw", [128, DC]); b128_s = SB("b128s", [128, 1])
        g8_s = SB("g8s", [8, 1])
        wib_s = SB("wibc", [128, FC]); wob_s = SB("wobc", [128, DC]); bout_s = SB("boutc", [128, DC])
        ldc(eyef_s[:, :], eye_f, "c:eyef")
        ldc(onesf_s[:, :], ones_f, "c:onesf")
        ldc(iota8_s[:, :], iota8_c, "c:iota"); ldc(expand_s[:, :], expand_c, "c:exp")
        ldc(nw_s[:, :], nw8, "c:nw"); ldc(b128_s[:, :], b128, "c:b128")
        ldc(g8_s[:, :], g8, "c:g8"); ldc(wib_s[:, :], wib32, "c:wib")
        ldc(wob_s[:, :], wob8, "c:wob"); ldc(bout_s[:, :], bout8, "c:bout")
        _dummy_lhs = onesf_s; _dummy_rhs = onesf_s
        cb_s = SB("cbias", [128, 4])
        for j, v in ((0, 1e-11), (1, 1.0), (2, 1e-5)):
            tr.op("DVE", lambda e, _j=j, _v=v: e.memset(cb_s[:, _j:_j + 1], _v), w=["c:cb"])
        tr.op("DVE", lambda e: e.tensor_copy(out=onesr_s[:, :], in_=onesf_s[:, :]),
              r=["c:onesf"], w=["c:onesr"])
        tr.dma("ACTQ", "d_x1init", lambda e: e.dma_start(out=x1h, in_=xT),
               w=[f"x1h.{g}" for g in range(NG)])

        with ExitStack() as attn:
            _a = attn.enter_context
            eyeb_s = _a(nc.sbuf_tensor("eyeb", [128, 128], BF16))
            lmask_s = _a(nc.sbuf_tensor("lmask", [128, 128], F32))
            rotT_s = _a(nc.sbuf_tensor("rotTs", [128, 128], BF16))
            sel_s = _a(nc.sbuf_tensor("sel16s", [128, 16], F32))
            m1c_s = _a(nc.sbuf_tensor("m1c", [128, NCH], F32))
            r1c_s = _a(nc.sbuf_tensor("r1c", [128, NCH], F32))
            onehot_s = _a(nc.sbuf_tensor("onehots", [8, 1024], F32))
            lbuf_s = _a(nc.sbuf_tensor("lbuf", [8, T], F32))
            lhat_s = _a(nc.sbuf_tensor("lhat", [8, T], F32))
            lgml_s = _a(nc.sbuf_tensor("lgml", [8, T], F32))
            qb = _a(nc.sbuf_tensor("qb", [128, NH, T], BF16))
            kb = _a(nc.sbuf_tensor("kb", [128, NH, T], BF16))
            vb = _a(nc.sbuf_tensor("vb", [128, NCH, D], BF16))
            ldc(onehot_s[:, :], onehot_c, "c:oh")
            ldc(eyeb_s[:, :], eye_b, "c:eyeb"); ldc(lmask_s[:, :], logmask_c, "c:lm")
            ldc(rotT_s[:, :], rotT_c, "c:rot"); ldc(sel_s[:, :], sel_c, "c:sel")

            subAB = attn.enter_context(ExitStack())
            _ab = subAB.enter_context
            m1r_s = _ab(nc.sbuf_tensor("m1rs", [1, T], F32R))
            m1rb_s = _ab(nc.sbuf_tensor("m1rb", [1, T], BF16))
            ctq = _ab(nc.sbuf_tensor("ctq", [128, T], BF16))
            stq = _ab(nc.sbuf_tensor("stq", [128, T], BF16))
            ctk = _ab(nc.sbuf_tensor("ctk", [128, T], BF16))
            stk = _ab(nc.sbuf_tensor("stk", [128, T], BF16))

            with ExitStack() as subA:
                _sa = subA.enter_context
                r1rep_s = _sa(nc.sbuf_tensor("r1rep", [128, T], F32))

                with ExitStack() as sub1:
                    _s1 = sub1.enter_context
                    r1r_s = _s1(nc.sbuf_tensor("r1rs", [1, T], F32))
                    costab = _s1(nc.sbuf_tensor("costab", [128, T], F32))
                    sintab = _s1(nc.sbuf_tensor("sintab", [128, T], F32))
                    xtm = [_s1(nc.sbuf_tensor(f"xtm{i}", [128, D], F32)) for i in range(2)]
                    s1 = _s1(nc.sbuf_tensor("s1", [128, NCH], F32))
                    sqs = _s1(nc.sbuf_tensor("sqs", [128, D], F32))
                    q1 = _s1(nc.sbuf_tensor("q1", [128, NCH], F32))
                    msq = _s1(nc.sbuf_tensor("msq", [128, NCH], F32))
                    var1 = _s1(nc.sbuf_tensor("var1", [128, NCH], F32))
                    psRow = _s1(nc.psum_tensor("psRow", [128, 512], F32))
                    if upto < 1:
                        raise _Done()
                    for c in range(2):
                        tr.dma("SYNCQ", f"d_xtm{c}",
                               lambda e, _c=c: e.dma_start(
                                   out=xtm[_c][:, :], in_=x_nat[_c * 128:(_c + 1) * 128, :]),
                               w=[f"xtm{c}"])
                    ldc(costab[:, :], cos2_c, "c:cos")
                    ldc(sintab[:, :], sin2_c, "c:sin")
                    for c in range(NCH):
                        b = c % 2
                        if c >= 2:
                            tr.dma("SYNCQ", f"d_xtm{b}",
                                   lambda e, _b=b, _c=c: e.dma_start(
                                       out=xtm[_b][:, :], in_=x_nat[_c * 128:(_c + 1) * 128, :]),
                                   w=[f"xtm{b}"])
                        tr.op("DVE", lambda e, _b=b, _c=c: e.reduce_sum(
                            out=s1[:, _c:_c + 1], in_=xtm[_b][:, :], axis=AX.X),
                            r=[f"xtm{b}"], w=[f"s1.{c}"])
                        tr.op("ACT", lambda e, _b=b, _c=c: e.activation(
                            out=sqs[:, :], in_=xtm[_b][:, :], func=AF.Square,
                            accum_out=q1[:, _c:_c + 1]),
                            r=[f"xtm{b}"], w=["sqs", f"q1.{c}"])
                    tr.op("ACT", lambda e: e.activation(out=m1c_s[:, :], in_=s1[:, :],
                                                        func=AF.Copy, scale=1.0 / D),
                          r=[f"s1.{c}" for c in range(NCH)],
                          w=[f"m1c.{c}" for c in range(NCH)])
                    tr.op("ACT", lambda e: e.activation(out=msq[:, :], in_=m1c_s[:, :],
                                                        func=AF.Square),
                          r=[f"m1c.{c}" for c in range(NCH)], w=["msq"])
                    tr.op("DVE", lambda e: e.tensor_scalar(out=var1[:, :], in0=q1[:, :],
                                                           scalar1=1.0 / D, scalar2=None,
                                                           op0=ALU.mult),
                          r=[f"q1.{c}" for c in range(NCH)], w=["var1"])
                    tr.op("DVE", lambda e: e.tensor_sub(var1[:, :], var1[:, :], msq[:, :]),
                          r=["var1", "msq"], w=["var1"])
                    tr.op("ACT", lambda e: e.activation(out=var1[:, :], in_=var1[:, :],
                                                        func=AF.Ln, scale=1.000001,
                                                        bias=cb_s[:, 0:1]),
                          r=["var1", "c:cb"], w=["var1"])
                    tr.op("ACT", lambda e: e.activation(out=r1c_s[:, :], in_=var1[:, :],
                                                        func=AF.Exp, scale=-0.5),
                          r=["var1"], w=[f"r1c.{c}" for c in range(NCH)])
                    for g in range(NG):
                        def _trg(e, _g=g, _cols=m1c_s):
                            for cc in range(4):
                                i = e.transpose(psRow[0:1, cc * 128:(cc + 1) * 128],
                                                _cols[:, _g * 4 + cc:_g * 4 + cc + 1],
                                                eyef_s[:, :])
                            return i
                        tr.op("PE", _trg,
                              r=[f"m1c.{g * 4 + cc}" for cc in range(4)] + ["c:eyef", "psRow"],
                              w=["psRow"])
                        pe_settle(psRow[32:34, 0:2], ["psRow"])
                        tr.op("ACT", lambda e, _g=g: e.activation(
                            out=m1r_s[0:1, _g * 512:(_g + 1) * 512], in_=psRow[0:1, :],
                            func=AF.Copy, scale=-1.0),
                            r=["psRow"], w=[f"m1r.{g}"])
                        tr.op("ACT", lambda e, _g=g: e.activation(
                            out=m1rb_s[0:1, _g * 512:(_g + 1) * 512], in_=psRow[0:1, :],
                            func=AF.Copy, scale=-1.0),
                            r=["psRow"], w=[f"m1rb.{g}"])
                        def _trg2(e, _g=g, _cols=r1c_s):
                            for cc in range(4):
                                i = e.transpose(psRow[0:1, cc * 128:(cc + 1) * 128],
                                                _cols[:, _g * 4 + cc:_g * 4 + cc + 1],
                                                eyef_s[:, :])
                            return i
                        tr.op("PE", _trg2,
                              r=[f"r1c.{g * 4 + cc}" for cc in range(4)] + ["c:eyef", "psRow"],
                              w=["psRow"])
                        pe_settle(psRow[32:34, 0:2], ["psRow"])
                        tr.op("ACT", lambda e, _g=g: e.activation(
                            out=r1r_s[0:1, _g * 512:(_g + 1) * 512], in_=psRow[0:1, :],
                            func=AF.Copy),
                            r=["psRow"], w=[f"r1r.{g}"])
                    for g in range(NG):
                        tr.op("PE", lambda e, _g=g: e.matmul(
                            psRow[:, :], lhsT=onesf_s[0:1, :],
                            rhs=r1r_s[0:1, _g * 512:(_g + 1) * 512], start=True, stop=True),
                            r=[f"r1r.{g}", "c:onesf", "psRow"], w=["psRow"])
                        tr.op("ACT", lambda e, _g=g: e.activation(
                            out=r1rep_s[:, _g * 512:(_g + 1) * 512], in_=psRow[:, :], func=AF.Copy),
                            r=["psRow"], w=[f"r1rep.{g}"])
                    for g in range(NG):
                        sl = slice(g * 512, (g + 1) * 512)
                        for (dst, tab, tabk, sc, key) in (
                                (ctq, costab, "c:cos", SCALE, "ctq"),
                                (stq, sintab, "c:sin", SCALE, "stq"),
                                (ctk, costab, "c:cos", 1.0, "ctk"),
                                (stk, sintab, "c:sin", 1.0, "stk")):
                            tr.op("DVE", lambda e, _d=dst, _t=tab, _sc=sc, _sl=sl:
                                  e.scalar_tensor_tensor(
                                      out=_d[:, _sl], in0=r1rep_s[:, _sl], scalar=_sc,
                                      in1=_t[:, _sl], op0=ALU.mult, op1=ALU.mult),
                                  r=[f"r1rep.{g}", tabk], w=[f"{key}.{g}"])

                tr.barrier()
                with ExitStack() as sub3:
                    _s3 = sub3.enter_context
                    ls_s = _s3(nc.sbuf_tensor("ls", [128, T], F32))
                    wsg_s = _s3(nc.sbuf_tensor("wsg", [1, 128], F32R))
                    z8_s = _s3(nc.sbuf_tensor("z8s", [8, 128], F32))
                    wg = _s3(nc.sbuf_tensor("wg", [128, DC, 128], F32R))
                    zt = _s3(nc.sbuf_tensor("zt", [128, 256], F32))
                    sg1 = _s3(nc.sbuf_tensor("sg1", [128, 256], F32))
                    colt = _s3(nc.sbuf_tensor("colt", [128, 1], F32))
                    xsl3 = [_s3(nc.sbuf_tensor(f"xsl3{i}", [128, DC, 256], F32R))
                            for i in range(2)]
                    psZ = _s3(nc.psum_tensor("psZ", [128, 256], F32))
                    psG = _s3(nc.psum_tensor("psG", [8, 256], F32))
                    ldc(z8_s[:, :], z8_c, "c:z8")
                    if upto < 2:
                        raise _Done()
                    tr.dma("POOLQ", "d_wg",
                           lambda e: e.dma_start(out=wg[:, :, :],
                                                 in_=wgP.rearrange("(c p) e -> p c e", p=128)),
                           w=["wg"])
                    for dc in range(DC):
                        tr.op("ACT", lambda e, _dc=dc: e.mul(out=wg[:, _dc, :], in_=wg[:, _dc, :],
                                                             mul=nw_s[:, _dc:_dc + 1]),
                              r=["wg", "c:nw"], w=["wg"])
                    for dc in range(DC):
                        tr.op("PE", lambda e, _dc=dc: e.matmul(
                            psZ[:, 0:2], lhsT=wg[:, _dc, :], rhs=onesr_s[:, 0:2],
                            start=(_dc == 0), stop=(_dc == DC - 1)),
                            r=["wg", "c:onesr", "psZ"], w=["psZ"])
                    tr.op("ACT", lambda e: e.activation(out=colt[:, :], in_=psZ[:, 0:1],
                                                        func=AF.Copy),
                          r=["psZ"], w=["colt"])
                    tr.op("PE", lambda e: e.transpose(psZ[0:1, 0:128], colt[:, :], eyef_s[:, :]),
                          r=["colt", "c:eyef", "psZ"], w=["psZ"])
                    pe_settle(psZ[32:34, 200:202], ["psZ"])
                    tr.op("ACT", lambda e: e.activation(out=wsg_s[0:1, :], in_=psZ[0:1, 0:128],
                                                        func=AF.Copy),
                          r=["psZ"], w=["wsg"])
                    for gx in range(8):
                        b = gx % 2
                        xg = slice(gx * 256, (gx + 1) * 256)
                        g = gx // 2
                        tr.dma("POOLQ", f"d_xsl3{b}",
                               lambda e, _b=b, _xg=xg: e.dma_start(
                                   out=xsl3[_b][:, :, :],
                                   in_=xT.rearrange("(c p) t -> p c t", p=128)[:, :, _xg]),
                               w=[f"xsl3{b}"])
                        def _zg(e, _b=b, _xg=xg):
                            for dc in range(DC):
                                e.matmul(psZ[:, :], lhsT=wg[:, dc, :], rhs=xsl3[_b][:, dc, :],
                                         start=(dc == 0), stop=False)
                            return e.matmul(psZ[:, :], lhsT=wsg_s[0:1, :],
                                            rhs=m1r_s[0:1, _xg], start=False, stop=True)
                        tr.op("PE", _zg, r=["wg", f"xsl3{b}", "wsg", f"m1r.{g}", "psZ"],
                              w=["psZ"])
                        tr.op("DVE", lambda e, _xg=xg: e.tensor_mul(zt[:, :], psZ[:, :],
                                                                    r1rep_s[:, _xg]),
                              r=["psZ", f"r1rep.{g}"], w=["zt"])
                        tr.op("DVE", lambda e: e.tensor_scalar(out=zt[:, :], in0=zt[:, :],
                                                               scalar1=b128_s[:, :], scalar2=None,
                                                               op0=ALU.add),
                              r=["zt", "c:b128"], w=["zt"])
                        tr.op("ACT", lambda e: e.activation(out=sg1[:, :], in_=zt[:, :],
                                                            func=AF.Abs),
                              r=["zt"], w=["sg1"])
                        tr.op("ACT", lambda e: e.activation(out=sg1[:, :], in_=sg1[:, :],
                                                            func=AF.Exp, scale=-1.0),
                              r=["sg1"], w=["sg1"])
                        tr.op("ACT", lambda e: e.activation(out=sg1[:, :], in_=sg1[:, :],
                                                            func=AF.Ln, bias=cb_s[:, 1:2]),
                              r=["sg1", "c:cb"], w=["sg1"])
                        tr.op("DVE", lambda e: e.tensor_scalar_min(zt[:, :], zt[:, :], 0.0),
                              r=["zt"], w=["zt"])
                        tr.op("DVE", lambda e, _xg=xg: e.tensor_sub(ls_s[:, _xg], zt[:, :],
                                                                    sg1[:, :]),
                              r=["zt", "sg1"], w=[f"ls.{gx}"])
                        tr.op("PE", lambda e, _xg=xg: e.matmul(
                            psG[:, :], lhsT=sel_s[:, 0:8], rhs=ls_s[:, _xg],
                            start=True, stop=True),
                            r=["c:sel", f"ls.{gx}", "psG"], w=["psG"])
                        tr.op("ACT", lambda e, _xg=xg: e.activation(out=lhat_s[:, _xg],
                                                                    in_=psG[:, :], func=AF.Copy),
                              r=["psG"], w=[f"lo.{gx}"])
                        tr.op("PE", lambda e, _xg=xg: e.matmul(
                            psG[:, :], lhsT=sel_s[:, 8:16], rhs=ls_s[:, _xg],
                            start=True, stop=True),
                            r=["c:sel", f"ls.{gx}", "psG"], w=["psG"])
                        tr.op("ACT", lambda e, _xg=xg: e.activation(out=lgml_s[:, _xg],
                                                                    in_=psG[:, :], func=AF.Copy),
                              r=["psG"], w=[f"lg.{gx}"])
                    for c in range(NCH):
                        sl = slice(c * 128, (c + 1) * 128)
                        gx = c // 2
                        tr.op("DVE", lambda e, _sl=sl: e.tensor_tensor_scan(
                            out=lbuf_s[:, _sl], data0=ls_s[0:8, _sl], data1=z8_s[:, :],
                            initial=0.0, op0=ALU.add, op1=ALU.add),
                            r=[f"ls.{gx}", "c:z8"], w=[f"lbuf.{c}"])
                        tr.op("DVE", lambda e, _sl=sl: e.tensor_add(
                            lhat_s[:, _sl], lhat_s[:, _sl], lbuf_s[:, _sl]),
                            r=[f"lbuf.{c}", f"lo.{gx}"], w=[f"lhat.{c}"])
                        tr.op("DVE", lambda e, _sl=sl: e.tensor_sub(
                            lgml_s[:, _sl], lgml_s[:, _sl], lbuf_s[:, _sl]),
                            r=[f"lbuf.{c}", f"lg.{gx}"], w=[f"lgml.{c}"])

            tr.barrier()
            with ExitStack() as ph4:
                _e4 = ph4.enter_context
                wp = _e4(nc.sbuf_tensor("wp", [128, DC, D], BF16))
                onesb = _e4(nc.sbuf_tensor("onesb", [128, 2], BF16))
                wsrow = _e4(nc.sbuf_tensor("wsrow", [1, D], BF16))
                wsload = _e4(nc.sbuf_tensor("wsload", [128, 1], F32))
                ropeA = [_e4(nc.sbuf_tensor(f"ropeA{i}", [128, 512], BF16)) for i in range(2)]
                tmp1 = [_e4(nc.sbuf_tensor(f"tmp1{i}", [128, 512], BF16)) for i in range(2)]
                tmp2 = [_e4(nc.sbuf_tensor(f"tmp2{i}", [128, 512], BF16)) for i in range(2)]
                xsl4 = [_e4(nc.sbuf_tensor(f"xsl4{i}", [128, DC, 512], BF16))
                        for i in range(2)]
                psP0 = _e4(nc.psum_tensor("psP0", [128, 512], F32))
                psP1 = _e4(nc.psum_tensor("psP1", [128, 512], F32))
                psRot = [_e4(nc.psum_tensor(f"psRot{i}", [128, 512], F32)) for i in range(2)]
                psW = _e4(nc.psum_tensor("psW", [128, 512], F32))
                tr.op("DVE", lambda e: e.tensor_copy(out=onesb[:, :], in_=onesf_s[:, 0:2]),
                      r=["c:onesf"], w=["c:onesb"])

                def load_xsl4(b, g):
                    tr.dma("POOLQ", f"d_xsl4{b}",
                           lambda e, _b=b, _g=g: e.dma_start(
                               out=xsl4[_b][:, :, :],
                               in_=xT.rearrange("(c p) t -> p c t", p=128)
                               [:, :, _g * 512:(_g + 1) * 512]),
                           w=[f"xsl4{b}"])

                def load_wp(src):
                    tr.dma("POOLQ", "d_wp",
                           lambda e, _s=src: e.dma_start(
                               out=wp[:, :, :],
                               in_=_s.rearrange("(c p) e -> p c e", p=128)),
                           w=["wp"])
                    def _fold(e):
                        for dc in range(DC):
                            i = e.mul(out=wp[:, dc, :], in_=wp[:, dc, :],
                                      mul=nw_s[:, dc:dc + 1])
                        return i
                    tr.op("ACT", _fold, r=["wp", "c:nw"], w=["wp"])

                def colsums():
                    for ec in range(DC):
                        def _cs(e, _ec=ec):
                            for dc in range(DC):
                                i = e.matmul(psW[:, 0:2],
                                             lhsT=wp[:, dc, _ec * 128:(_ec + 1) * 128],
                                             rhs=onesb[:, :], start=(dc == 0),
                                             stop=(dc == DC - 1))
                            return i
                        tr.op("PE", _cs, r=["wp", "c:onesb", "psW"], w=["psW"])
                        tr.op("ACT", lambda e: e.activation(out=wsload[:, :], in_=psW[:, 0:1],
                                                            func=AF.Copy),
                              r=["psW"], w=["wsload"])
                        tr.op("PE", lambda e: e.transpose(psW[0:1, 0:128], wsload[:, :],
                                                          eyef_s[:, :]),
                              r=["wsload", "c:eyef", "psW"], w=["psW"])
                        pe_settle(psW[32:34, 200:202], ["psW", "wsload"])
                        tr.op("ACT", lambda e, _ec=ec: e.activation(
                            out=wsrow[0:1, _ec * 128:(_ec + 1) * 128], in_=psW[0:1, 0:128],
                            func=AF.Copy),
                            r=["psW"], w=["wsrow"])

                load_wp(wvT)
                colsums()
                for g in range(NG):
                    b = g % 2
                    load_xsl4(b, g)
                    for tc4 in range(4):
                        c = g * 4 + tc4
                        for eh in range(2):
                            pp, pk = (psP0, "psP0") if eh == 0 else (psP1, "psP1")
                            def _vg(e, _b=b, _tc=tc4, _eh=eh, _pp=pp, _c=c):
                                for dc in range(DC):
                                    e.matmul(_pp[:, :],
                                             lhsT=xsl4[_b][:, dc, _tc * 128:(_tc + 1) * 128],
                                             rhs=wp[:, dc, _eh * 512:(_eh + 1) * 512],
                                             start=(dc == 0), stop=False)
                                return e.matmul(_pp[:, :],
                                                lhsT=m1rb_s[0:1, _c * 128:(_c + 1) * 128],
                                                rhs=wsrow[0:1, _eh * 512:(_eh + 1) * 512],
                                                start=False, stop=True)
                            tr.op("PE", _vg,
                                  r=[f"xsl4{b}", "wp", "wsrow", f"m1rb.{c // 4}", pk], w=[pk])
                            tr.op("DVE", lambda e, _c=c, _eh=eh, _pp=pp: e.tensor_scalar(
                                out=vb[:, _c, _eh * 512:(_eh + 1) * 512], in0=_pp[:, :],
                                scalar1=r1c_s[:, _c:_c + 1], scalar2=None, op0=ALU.mult),
                                r=[pk, f"r1c.{c}"], w=[f"vb.{c}"])

                for (proj, wsrc, dst, ct, st_, dkey) in (
                        ("k", wkT, kb, ctk, stk, "kb"), ("q", wqT, qb, ctq, stq, "qb")):
                    load_wp(wsrc)
                    colsums()
                    for g in range(NG):
                        b = g % 2
                        sl = slice(g * 512, (g + 1) * 512)
                        load_xsl4(b, g)
                        for h in range(NH):
                            pp, pk = (psP0, "psP0") if h % 2 == 0 else (psP1, "psP1")
                            def _qg(e, _b=b, _h=h, _sl=sl, _pp=pp):
                                for dc in range(DC):
                                    e.matmul(_pp[:, :],
                                             lhsT=wp[:, dc, _h * 128:(_h + 1) * 128],
                                             rhs=xsl4[_b][:, dc, :], start=(dc == 0),
                                             stop=False)
                                return e.matmul(_pp[:, :],
                                                lhsT=wsrow[0:1, _h * 128:(_h + 1) * 128],
                                                rhs=m1rb_s[0:1, _sl], start=False, stop=True)
                            tr.op("PE", _qg,
                                  r=["wp", f"xsl4{b}", "wsrow", f"m1rb.{g}", pk], w=[pk])
                            hb = h % 2
                            tr.op("ACT", lambda e, _pp=pp, _hb=hb: e.activation(
                                out=ropeA[_hb][:, :], in_=_pp[:, :], func=AF.Copy),
                                r=[pk], w=[f"ropeA{hb}"])
                            tr.op("PE", lambda e, _hb=hb: e.matmul(
                                psRot[_hb][:, :], lhsT=rotT_s[:, :], rhs=ropeA[_hb][:, :],
                                start=True, stop=True),
                                r=["c:rot", f"ropeA{hb}", f"psRot{hb}"], w=[f"psRot{hb}"])
                            tr.op("DVE", lambda e, _ct=ct, _sl=sl, _hb=hb: e.tensor_mul(
                                tmp1[_hb][:, :], ropeA[_hb][:, :], _ct[:, _sl]),
                                r=[f"ropeA{hb}", f"ct{proj}.{g}"], w=[f"tmp1{hb}"])
                            tr.op("DVE", lambda e, _st=st_, _sl=sl, _hb=hb: e.tensor_mul(
                                tmp2[_hb][:, :], psRot[_hb][:, :], _st[:, _sl]),
                                r=[f"psRot{hb}", f"st{proj}.{g}"], w=[f"tmp2{hb}"])
                            tr.op("DVE", lambda e, _d=dst, _h=h, _sl=sl, _hb=hb: e.tensor_add(
                                _d[:, _h, _sl], tmp1[_hb][:, :], tmp2[_hb][:, :]),
                                r=[f"tmp1{hb}", f"tmp2{hb}"], w=[f"{dkey}.{h}.g{g}"])

            subAB.close()

            tr.barrier()
            with ExitStack() as ph5:
                _e5 = ph5.enter_context
                woutb = _e5(nc.sbuf_tensor("woutb", [128, NH, D], BF16))
                tmg = _e5(nc.sbuf_tensor("tmg", [128, 16], F32))
                lrep = _e5(nc.sbuf_tensor("lrep", [128, NH, 128], F32))
                tbc = _e5(nc.sbuf_tensor("tbc", [128, NH * 128], F32))
                aexp = _e5(nc.sbuf_tensor("aexp", [128, NH * 128], BF16))
                pbt = _e5(nc.sbuf_tensor("pbt", [128, NH * 128], BF16))
                qtl = _e5(nc.sbuf_tensor("qtl", [128, NH, 128], BF16))
                ktm = _e5(nc.sbuf_tensor("ktm", [128, NH, 128], BF16))
                vtl = _e5(nc.sbuf_tensor("vtl", [128, NH, 128], BF16))
                msb = _e5(nc.sbuf_tensor("msb", [128, NH, DH], F32))
                mbb = _e5(nc.sbuf_tensor("mbb", [128, NH, DH], BF16))
                ue8 = _e5(nc.sbuf_tensor("ue8", [128, 8], F32))
                ac8 = _e5(nc.sbuf_tensor("ac8", [128, 8], F32))
                yring = _e5(nc.sbuf_tensor("yring", [128, NH, 512], BF16))
                aosl = _e5(nc.sbuf_tensor("aosl", [128, DC, 256], F32))
                psA = _e5(nc.psum_tensor("psA", [128, NH * 128], F32))
                psT = _e5(nc.psum_tensor("psT", [128, NH * 128], F32))
                psE = _e5(nc.psum_tensor("psE", [128, NH * 128], F32))
                psKT = _e5(nc.psum_tensor("psKT", [128, NH * 128], BF16))
                psM = _e5(nc.psum_tensor("psM", [128, 512], F32))
                if upto < 4:
                    raise _Done()
                tr.dma("POOLQ", "d_wout",
                       lambda e: e.dma_start(out=woutb[:, :, :],
                                             in_=woutT.rearrange("(c p) e -> p c e", p=128)),
                       w=["woutb"])
                tr.dma("POOLQ", "d_wi16", lambda e: e.dma_start(out=wi16, in_=wiT),
                       w=["wi16"])
                tr.dma("POOLQ", "d_wo16", lambda e: e.dma_start(out=wo16, in_=woT),
                       w=["wo16"])

                bankA = lambda h: f"psA.{h // 4}"
                bankT = lambda h: f"psT.{h // 4}"
                bankE = lambda h: f"psE.{h // 4}"
                ALLA = ["psA.0", "psA.1"]
                ALLT = ["psT.0", "psT.1"]
                ALLE = ["psE.0", "psE.1"]

                for g in range(NG):
                    for tc4 in range(4):
                        c = g * 4 + tc4
                        csl = slice(c * 128, (c + 1) * 128)
                        gx = c // 2
                        tr.op("PE", lambda e, _csl=csl: e.transpose(
                            psM[:, 0:8], lhat_s[:, _csl], eyef_s[0:8, 0:8]),
                            r=[f"lhat.{c}", "c:eyef", "psM"], w=["psM"])
                        tr.op("PE", lambda e, _csl=csl: e.transpose(
                            psM[:, 8:16], lgml_s[:, _csl], eyef_s[0:8, 0:8]),
                            r=[f"lgml.{c}", "c:eyef", "psM"], w=["psM"])
                        for h in range(NH):
                            def _skt(e, _h=h, _csl=csl):
                                e.transpose(psKT[:, _h * 128:(_h + 1) * 128],
                                            kb[:, _h, _csl], eyeb_s[:, :])
                                return e.matmul(psA[:, _h * 128:(_h + 1) * 128],
                                                lhsT=kb[:, _h, _csl], rhs=qb[:, _h, _csl],
                                                start=True, stop=True)
                            tr.op("PE", _skt,
                                  r=[f"kb.{h}.g{g}", f"qb.{h}.g{g}", "c:eyeb", "psKT",
                                     bankA(h)],
                                  w=[bankA(h), "psKT"] + (["psM"] if h == 0 else []))
                        tr.op("ACT", lambda e: e.activation(out=tmg[:, :], in_=psM[:, 0:16],
                                                            func=AF.Copy),
                              r=["psM"], w=["tmg"])
                        if SUB5 < 2:
                            continue
                        for h in range(NH):
                            tr.op("DVE", lambda e, _h=h: e.tensor_copy(
                                out=lrep[:, _h, :],
                                in_=tmg[:, _h:_h + 1].broadcast_to([128, 128])),
                                r=["tmg", "lrepfence"], w=[f"lrep.{h}"])
                            tr.op("PE", lambda e, _h=h: e.transpose(
                                psT[:, _h * 128:(_h + 1) * 128], lrep[:, _h, :], eyef_s[:, :]),
                                r=[f"lrep.{h}", "c:eyef", bankT(h)], w=[bankT(h)])
                        def _lcg(e, _c=c):
                            for h in range(NH):
                                i = e.matmul(psM[:, h:h + 1],
                                             lhsT=onehot_s[:, h * 128:(h + 1) * 128],
                                             rhs=lbuf_s[:, _c * 128 + 127:_c * 128 + 128],
                                             start=True, stop=True)
                            return i
                        tr.op("PE", _lcg, r=["c:oh", f"lbuf.{c}", "psM"],
                              w=["psM", "psT.0", "psT.1", "lrepfence"])
                        tr.op("DVE", lambda e: e.tensor_add(ue8[:, :], psM[:, 0:8], tmg[:, 8:16]),
                              r=["psM", "tmg"], w=["ue8"])
                        tr.op("ACT", lambda e: e.activation(out=ue8[:, :], in_=ue8[:, :],
                                                            func=AF.Exp),
                              r=["ue8"], w=["ue8"])
                        tr.op("ACT", lambda e: e.activation(out=ac8[:, :], in_=psM[:, 0:8],
                                                            func=AF.Exp),
                              r=["psM"], w=["ac8"])
                        if SUB5 < 3:
                            continue
                        for bk in range(2):
                            tr.op("ACT", lambda e, _b=bk: e.activation(
                                out=tbc[:, _b * 512:(_b + 1) * 512],
                                in_=psT[:, _b * 512:(_b + 1) * 512], func=AF.Copy),
                                r=[f"psT.{bk}"], w=["tbc"])
                        tr.op("ACT", lambda e: e.activation(out=aexp[:, :], in_=tbc[:, :],
                                                            func=AF.Exp),
                              r=["tbc"], w=["aexp"])
                        if SUB5 == 30:
                            continue
                        def _eadds(e):
                            for h in range(NH):
                                i = e.tensor_scalar(
                                    out=tbc[:, h * 128:(h + 1) * 128],
                                    in0=tbc[:, h * 128:(h + 1) * 128],
                                    scalar1=tmg[:, 8 + h:9 + h], scalar2=None, op0=ALU.add)
                            return i
                        tr.op("DVE", _eadds, r=["tbc", "tmg", "aexp"], w=["tbc"])
                        if SUB5 < 4:
                            continue
                        tr.op("DVE", lambda e: e.tensor_add(
                            tbc[:, :].rearrange("p (h t) -> p h t", h=NH),
                            tbc[:, :].rearrange("p (h t) -> p h t", h=NH),
                            lmask_s[:, :].rearrange("p (o t) -> p o t", o=1)
                            .broadcast_to([128, NH, 128])),
                            r=["tbc", "c:lm"], w=["tbc"])
                        if SUB5 < 5:
                            continue
                        tr.op("ACT", lambda e: e.activation(out=tbc[:, :], in_=tbc[:, :],
                                                            func=AF.Exp),
                              r=["tbc"], w=["tbc"])
                        for bk in range(2):
                            tr.op("DVE", lambda e, _b=bk: e.tensor_mul(
                                pbt[:, _b * 512:(_b + 1) * 512],
                                tbc[:, _b * 512:(_b + 1) * 512],
                                psA[:, _b * 512:(_b + 1) * 512]),
                                r=["tbc", f"psA.{bk}"], w=["pbt"])
                        if SUB5 < 6:
                            continue
                        def _qtg(e, _csl=csl):
                            for h in range(NH):
                                i = e.tensor_mul(qtl[:, h, :], qb[:, h, _csl],
                                                 aexp[:, h * 128:(h + 1) * 128])
                            return i
                        tr.op("DVE", _qtg,
                              r=[f"qb.{h}.g{g}" for h in range(NH)] + ["aexp"],
                              w=["qtl"])
                        tr.op("ACT", lambda e: e.activation(
                            out=ktm[:, :, :].rearrange("p h t -> p (h t)"), in_=psKT[:, :],
                            func=AF.Copy),
                            r=["psKT"], w=["ktm"])
                        if SUB5 < 7:
                            continue
                        if c < NCH - 1:
                            def _vtg(e, _c=c):
                                for h in range(NH):
                                    i = e.tensor_scalar(
                                        out=vtl[:, h, :],
                                        in0=vb[:, _c, h * 128:(h + 1) * 128],
                                        scalar1=ue8[:, h:h + 1], scalar2=None, op0=ALU.mult)
                                return i
                            tr.op("DVE", _vtg, r=[f"vb.{c}", "ue8"], w=["vtl"])
                        for h in range(NH):
                            def _yg(e, _h=h, _c=c):
                                i = e.matmul(psE[:, _h * 128:(_h + 1) * 128],
                                             lhsT=vb[:, _c, _h * 128:(_h + 1) * 128],
                                             rhs=pbt[:, _h * 128:(_h + 1) * 128],
                                             start=True, stop=(_c == 0))
                                if _c > 0:
                                    i = e.matmul(psE[:, _h * 128:(_h + 1) * 128],
                                                 lhsT=mbb[:, _h, :], rhs=qtl[:, _h, :],
                                                 start=False, stop=True)
                                return i
                            tr.op("PE", _yg,
                                  r=[f"vb.{c}", "pbt", "mbb", "qtl", bankE(h)], w=[bankE(h)])
                            if c < NCH - 1:
                                tr.op("PE", lambda e, _h=h: e.matmul(
                                    psA[:, _h * 128:(_h + 1) * 128], lhsT=ktm[:, _h, :],
                                    rhs=vtl[:, _h, :], start=True, stop=True),
                                    r=["ktm", "vtl", bankA(h)], w=[bankA(h)])
                        for bk in range(2):
                            tr.op("DVE", lambda e, _tc=tc4, _b=bk: e.tensor_copy(
                                out=yring[:, 4 * _b:4 * (_b + 1), _tc * 128:(_tc + 1) * 128],
                                in_=psE[:, _b * 512:(_b + 1) * 512]
                                .rearrange("p (h t) -> p h t", h=4)),
                                r=[f"psE.{bk}"], w=[f"yring.{tc4}"])
                        if SUB5 < 8:
                            continue
                        if c < NCH - 1:
                            if c == 0:
                                for bk in range(2):
                                    tr.op("DVE", lambda e, _b=bk: e.tensor_copy(
                                        out=msb[:, 4 * _b:4 * (_b + 1), :],
                                        in_=psA[:, _b * 512:(_b + 1) * 512]
                                        .rearrange("p (h t) -> p h t", h=4)),
                                        r=[f"psA.{bk}"], w=["msb"])
                            else:
                                def _mdec(e):
                                    for h in range(NH):
                                        i = e.tensor_scalar(
                                            out=msb[:, h, :], in0=msb[:, h, :],
                                            scalar1=ac8[:, h:h + 1], scalar2=None,
                                            op0=ALU.mult)
                                    return i
                                tr.op("DVE", _mdec, r=["msb", "ac8"], w=["msb"])
                                for bk in range(2):
                                    tr.op("DVE", lambda e, _b=bk: e.tensor_add(
                                        msb[:, 4 * _b:4 * (_b + 1), :],
                                        msb[:, 4 * _b:4 * (_b + 1), :],
                                        psA[:, _b * 512:(_b + 1) * 512]
                                        .rearrange("p (h t) -> p h t", h=4)),
                                        r=["msb", f"psA.{bk}"], w=["msb"])
                            tr.op("ACT", lambda e: e.activation(
                                out=mbb[:, :, :].rearrange("p h t -> p (h t)"),
                                in_=msb[:, :, :].rearrange("p h t -> p (h t)"), func=AF.Copy),
                                r=["msb"], w=["mbb"])
                    if SUB5 < 9:
                        continue
                    for wh in range(2):
                        for dcp in range(DC):
                            if dcp % 2 == 0:
                                pp, rk = psM, ["psM"]
                            else:
                                pp, rk = psE, ["psE.0"]
                            def _wog(e, _dcp=dcp, _pp=pp, _wh=wh):
                                for h in range(NH):
                                    i = e.matmul(
                                        _pp[:, 0:256],
                                        lhsT=woutb[:, h, _dcp * 128:(_dcp + 1) * 128],
                                        rhs=yring[:, h, _wh * 256:(_wh + 1) * 256],
                                        start=(h == 0), stop=(h == NH - 1))
                                return i
                            tr.op("PE", _wog,
                                  r=["woutb"] + [f"yring.{i}" for i in range(4)] + rk, w=rk)
                            tr.op("DVE", lambda e, _dcp=dcp, _pp=pp: e.tensor_scalar(
                                out=aosl[:, _dcp, :], in0=_pp[:, 0:256],
                                scalar1=bout_s[:, _dcp:_dcp + 1], scalar2=None, op0=ALU.add),
                                r=rk + ["c:bout"], w=["aosl"])
                        tr.dma("POOLQ", "d_ao",
                               lambda e, _g=g, _wh=wh: e.dma_start(
                                   out=x1h.rearrange("(c p) t -> p c t", p=128)
                                   [:, :, _g * 512 + _wh * 256:_g * 512 + (_wh + 1) * 256],
                                   in_=aosl[:, :, :], accum_op=ALU.add),
                               r=["aosl", f"x1h.{g}"], w=[f"x1h.{g}"])

        tr.barrier()
        with ExitStack() as ph6:
            _e6 = ph6.enter_context
            wib = _e6(nc.sbuf_tensor("wibuf", [128, DC, FF], BF16))
            wob = _e6(nc.sbuf_tensor("wobuf", [128, FC, D], BF16))
            bcat = _e6(nc.sbuf_tensor("bcat", [32, FF], BF16))
            gab = _e6(nc.sbuf_tensor("gab", [128, DC, 8], BF16))
            aab = _e6(nc.sbuf_tensor("aab", [128, DC, 32], BF16))
            x1sl = _e6(nc.sbuf_tensor("x1sl", [128, DC, 512], BF16))
            onesb6 = _e6(nc.sbuf_tensor("onesb6", [128, 128], BF16))
            fsc = _e6(nc.sbuf_tensor("fsc", [128, 512], BF16))
            xnb = [_e6(nc.sbuf_tensor(f"xnb{i}", [128, DC, 512], BF16)) for i in range(2)]
            inter = _e6(nc.sbuf_tensor("inter", [128, FC, 512], BF16))
            x2sl = [_e6(nc.sbuf_tensor("x2sl0", [128, 512], F32))] * 2
            sqscr = _e6(nc.sbuf_tensor("sqscr", [128, 512], BF16))
            m2bc = _e6(nc.sbuf_tensor("m2bc", [128, 512], F32))
            v2bc = _e6(nc.sbuf_tensor("v2bc", [128, 512], F32))
            rp8 = _e6(nc.sbuf_tensor("rp8", [8, 512], F32))
            pp32 = _e6(nc.sbuf_tensor("pp32", [32, 512], BF16))
            tm8 = _e6(nc.sbuf_tensor("tm8", [128, 32], F32))
            mx8 = _e6(nc.sbuf_tensor("mx8", [128, 8], F32))
            ix8 = _e6(nc.sbuf_tensor("ix8", [128, 8], U32))
            ixf = _e6(nc.sbuf_tensor("ixf", [128, 4], F32))
            m8t = _e6(nc.sbuf_tensor("m8t", [128, 32], F32))
            m8f = _e6(nc.sbuf_tensor("m8f", [8, 512], BF16))
            psS1 = _e6(nc.psum_tensor("psS1", [128, 512], F32))
            psS2 = _e6(nc.psum_tensor("psS2", [128, 512], F32))
            psR8 = _e6(nc.psum_tensor("psR8", [128, 512], F32))
            psP32 = _e6(nc.psum_tensor("psP32", [32, 512], F32))
            psF0 = _e6(nc.psum_tensor("psF0", [128, 512], F32))
            psF1 = _e6(nc.psum_tensor("psF1", [128, 512], F32))
            psO0 = _e6(nc.psum_tensor("psO0", [128, 512], F32))
            psO1 = _e6(nc.psum_tensor("psO1", [128, 512], F32))
            if upto < 5:
                raise _Done()
            tr.op("DVE", lambda e: e.tensor_copy(out=onesb6[:, :], in_=onesf_s[:, :]),
                  r=["c:onesf"], w=["c:onesb6"])
            tr.dma("POOLQ", "d_ga",
                   lambda e: e.dma_start(out=gab[:, :, :],
                                         in_=gw8T.rearrange("(c p) e -> p c e", p=128)),
                   w=["gab"])
            tr.dma("POOLQ", "d_aa",
                   lambda e: e.dma_start(out=aab[:, :, :],
                                         in_=acatT.rearrange("(c p) e -> p c e", p=128)),
                   w=["aab"])
            tr.dma("POOLQ", "d_bcat", lambda e: e.dma_start(out=bcat[:, :], in_=bcatT),
                   w=["bcat"])

            def load_ffn_weights():
                wiq = ("SYNCQ", "POOLQ", "SYNCQ", "POOLQ")
                woq = ("ACTQ", "SYNCQ", "ACTQ", "POOLQ")
                for i in range(4):
                    tr.dma(wiq[i], f"d_wi{i}",
                           lambda e, _i=i: e.dma_start(
                               out=wib[:, 2 * _i:2 * _i + 2, :],
                               in_=wi16.rearrange("(c p) f -> p c f", p=128)[:, 2 * _i:2 * _i + 2, :]),
                           r=["wi16"], w=[f"wib.{i}"])
                    tr.dma(woq[i], f"d_wo{i}",
                           lambda e, _i=i: e.dma_start(
                               out=wob[:, 8 * _i:8 * _i + 8, :],
                               in_=wo16.rearrange("(c p) f -> p c f", p=128)[:, 8 * _i:8 * _i + 8, :]),
                           r=["wo16"], w=[f"wob.{i}"])
            def frontA(g):
                bb = g % 2
                tr.dma("POOLQ", "d_x1sl",
                       lambda e, _g=g: e.dma_start(
                           out=x1sl[:, :, :],
                           in_=x1h.rearrange("(c p) t -> p c t", p=128)
                           [:, :, _g * 512:(_g + 1) * 512]),
                       r=[f"x1h.{g}"], w=["x1sl"])
                tr.dma("SYNCQ", f"d_oinit{g}",
                       lambda e, _g=g: e.dma_start(
                           out=outT[:, _g * 512:(_g + 1) * 512],
                           in_=x1h[:, _g * 512:(_g + 1) * 512]),
                       r=[f"x1h.{g}"], w=[f"outT.{g}"])
                def _s1g(e):
                    for dc in range(DC):
                        i = e.matmul(psS1[:, :], lhsT=onesb6[:, :], rhs=x1sl[:, dc, :],
                                     start=(dc == 0), stop=(dc == DC - 1))
                    return i
                tr.op("PE", _s1g, r=["c:onesb6", "x1sl", "psS1"], w=["psS1"])
                for dc in range(DC):
                    tr.op("ACT", lambda e, _dc=dc: e.activation(out=sqscr[:, :],
                                                                in_=x1sl[:, _dc, :],
                                                                func=AF.Square),
                          r=["x1sl"], w=["sqscr"])
                    tr.op("PE", lambda e, _dc=dc: e.matmul(
                        psS2[:, :], lhsT=onesb6[:, :], rhs=sqscr[:, :],
                        start=(_dc == 0), stop=(_dc == DC - 1)),
                        r=["c:onesb6", "sqscr", "psS2"], w=["psS2"])
                tr.op("ACT", lambda e: e.activation(out=m2bc[:, :], in_=psS1[:, :],
                                                    func=AF.Copy, scale=1.0 / D),
                      r=["psS1"], w=["m2bc"])
                tr.op("ACT", lambda e: e.activation(out=v2bc[:, :], in_=psS2[:, :],
                                                    func=AF.Copy, scale=1.0 / D),
                      r=["psS2"], w=["v2bc"])
                tr.op("DVE", lambda e: e.tensor_mul(fsc[:, :], m2bc[:, :], m2bc[:, :]),
                      r=["m2bc"], w=["fsc"])
                tr.op("DVE", lambda e: e.tensor_sub(v2bc[:, :], v2bc[:, :], fsc[:, :]),
                      r=["v2bc", "fsc"], w=["v2bc"])
                tr.op("ACT", lambda e: e.activation(out=v2bc[:, :], in_=v2bc[:, :],
                                                    func=AF.Ln, bias=cb_s[:, 2:3]),
                      r=["v2bc", "c:cb"], w=["v2bc"])
                tr.op("ACT", lambda e: e.activation(out=v2bc[:, :], in_=v2bc[:, :],
                                                    func=AF.Exp, scale=-0.5),
                      r=["v2bc"], w=["v2bc"])
                for dc in range(DC):
                    tr.op("DVE", lambda e, _dc=dc: e.tensor_sub(fsc[:, :], x1sl[:, _dc, :],
                                                                m2bc[:, :]),
                          r=["x1sl", "m2bc"], w=["fsc"])
                    tr.op("DVE", lambda e, _dc=dc, _bb=bb: e.tensor_mul(
                        xnb[_bb][:, _dc, :], fsc[:, :], v2bc[:, :]),
                        r=["fsc", "v2bc"], w=[f"xnb{bb}"])

            def frontB(g):
                bb = g % 2
                def _rg(e, _bb=bb):
                    for dc in range(DC):
                        e.matmul(psR8[0:8, :], lhsT=gab[:, dc, :], rhs=xnb[_bb][:, dc, :],
                                 start=(dc == 0), stop=(dc == DC - 1))
                        i = e.matmul(psP32[:, :], lhsT=aab[:, dc, :], rhs=xnb[_bb][:, dc, :],
                                     start=(dc == 0), stop=(dc == DC - 1))
                    return i
                tr.op("PE", _rg, r=["gab", "aab", f"xnb{bb}", "psR8", "psP32"],
                      w=["psR8", "psP32"])
                tr.op("DVE", lambda e: e.tensor_scalar(out=rp8[:, :], in0=psR8[0:8, :],
                                                       scalar1=g8_s[:, :], scalar2=None,
                                                       op0=ALU.add),
                      r=["psR8", "c:g8"], w=["rp8"])
                tr.op("ACT", lambda e: e.activation(out=pp32[:, :], in_=psP32[:, :],
                                                    func=AF.Copy),
                      r=["psP32"], w=["pp32"])
                for tc4 in range(4):
                    tr.op("PE", lambda e, _tc=tc4: e.transpose(
                        psS1[:, _tc * 8:(_tc + 1) * 8], rp8[:, _tc * 128:(_tc + 1) * 128],
                        eyef_s[0:8, 0:8]),
                        r=["rp8", "c:eyef", "psS1"], w=["psS1"])
                pe_settle(psS1[32:34, 40:42], ["psS1", "rp8"])
                tr.op("ACT", lambda e: e.activation(
                    out=tm8[:, :], in_=psS1[:, 0:32], func=AF.Copy),
                    r=["psS1"], w=["tm8"])
                for tc4 in range(4):
                    tr.op("DVE", lambda e, _tc=tc4: e.max(
                        out=mx8[:, :], in_=tm8[:, _tc * 8:(_tc + 1) * 8]),
                        r=["tm8"], w=["mx8"])
                    tr.op("DVE", lambda e, _tc=tc4: e.max_index(
                        out=ix8[:, :], in_max=mx8[:, :],
                        in_values=tm8[:, _tc * 8:(_tc + 1) * 8]),
                        r=["mx8", "tm8"], w=["ix8"])
                    tr.op("DVE", lambda e, _tc=tc4: e.tensor_copy(
                        out=ixf[:, _tc:_tc + 1], in_=ix8[:, 0:1]),
                        r=["ix8"], w=["ixf"])
                    tr.op("DVE", lambda e, _tc=tc4: e.tensor_scalar(
                        out=m8t[:, _tc * 8:(_tc + 1) * 8], in0=iota8_s[:, :],
                        scalar1=ixf[:, _tc:_tc + 1], scalar2=None, op0=ALU.is_equal),
                        r=["c:iota", "ixf"], w=["m8t"])
                for tc4 in range(4):
                    tr.op("PE", lambda e, _tc=tc4: e.transpose(
                        psR8[0:8, _tc * 128:(_tc + 1) * 128], m8t[:, _tc * 8:(_tc + 1) * 8],
                        eyef_s[:, :]),
                        r=["m8t", "c:eyef", "psR8"], w=["psR8"])
                pe_settle(psR8[32:34, 40:42], ["psR8", "m8t"])
                tr.op("ACT", lambda e: e.activation(
                    out=m8f[:, :], in_=psR8[0:8, :], func=AF.Copy),
                    r=["psR8"], w=["m8f"])
                tr.op("PE", lambda e: e.matmul(psP32[:, :], lhsT=expand_s[:, :], rhs=m8f[:, :],
                                               start=True, stop=True),
                      r=["c:exp", "m8f", "psP32"], w=["psP32"])
                tr.op("DVE", lambda e: e.tensor_mul(pp32[:, :], pp32[:, :], psP32[:, :]),
                      r=["pp32", "psP32"], w=["pp32"])

            def body_wi(g):
                bb = g % 2
                for fc in range(FC):
                    pp, pk = (psF0, "psF0") if fc % 2 == 0 else (psF1, "psF1")
                    def _wig(e, _fc=fc, _pp=pp, _bb=bb):
                        for dc in range(DC):
                            e.matmul(_pp[:, :],
                                     lhsT=wib[:, dc, _fc * 128:(_fc + 1) * 128],
                                     rhs=xnb[_bb][:, dc, :], start=(dc == 0), stop=False)
                        return e.matmul(_pp[:, :],
                                        lhsT=bcat[:, _fc * 128:(_fc + 1) * 128],
                                        rhs=pp32[:, :], start=False, stop=True)
                    tr.op("PE", _wig,
                          r=[f"wib.{i}" for i in range(4)] + [f"xnb{bb}", "bcat", "pp32", pk],
                          w=[pk])
                    tr.op("ACT", lambda e, _fc=fc, _pp=pp: e.activation(
                        out=inter[:, _fc, :], in_=_pp[:, :], func=AF.Relu,
                        bias=wib_s[:, _fc:_fc + 1]),
                        r=[pk, "c:wib"], w=[f"inter.{fc}"])

            def body_wo(g):
                for dcp in range(DC):
                    pp, pk = (psO0, "psO0") if dcp % 2 == 0 else (psO1, "psO1")
                    def _wog2(e, _dcp=dcp, _pp=pp):
                        for fc in range(FC):
                            i = e.matmul(_pp[:, :],
                                         lhsT=wob[:, fc, _dcp * 128:(_dcp + 1) * 128],
                                         rhs=inter[:, fc, :], start=(fc == 0),
                                         stop=(fc == FC - 1))
                        return i
                    tr.op("PE", _wog2,
                          r=[f"wob.{i}" for i in range(4)]
                          + [f"inter.{fc}" for fc in range(FC)] + [pk],
                          w=[pk])
                    tr.op("DVE", lambda e, _dcp=dcp, _pp=pp: e.tensor_scalar(
                        out=x2sl[0][:, :], in0=_pp[:, :], scalar1=wob_s[:, _dcp:_dcp + 1],
                        scalar2=None, op0=ALU.add),
                        r=[pk, "c:wob", "x2sl0"], w=["x2sl0"])
                    tr.dma("POOLQ", "d_out0",
                           lambda e, _g=g, _dcp=dcp: e.dma_start(
                               out=outT[_dcp * 128:(_dcp + 1) * 128,
                                        _g * 512:(_g + 1) * 512],
                               in_=x2sl[0][:, :], accum_op=ALU.add),
                           r=["x2sl0", f"outT.{g}"], w=[f"outT.{g}"])

            frontA(0)
            frontB(0)
            load_ffn_weights()
            for g in range(NG):
                body_wi(g)
                if g + 1 < NG:
                    frontA(g + 1)
                    frontB(g + 1)
                body_wo(g)
        raise _Done()
    except _Done:
        with ExitStack() as c2:
            tr.final_wait("POOLQ", "d_out0")
            tr.build_blocks(nc, c2)
    return nc


_NC_CACHE = {}


def _get_nc():
    if "nc" not in _NC_CACHE:
        _NC_CACHE["nc"] = build()
    return _NC_CACHE["nc"]


def kernel(**inputs):
    x = np.asarray(inputs["L0_attn_x"])
    nc = _get_nc()
    shared = _shared_arrays(inputs)
    in_maps = []
    for c in range(8):
        m = dict(shared)
        m["x_nat"] = np.ascontiguousarray(x[c])
        m["xT"] = np.ascontiguousarray(x[c].T)
        in_maps.append(m)
    res = run_bass_kernel_spmd(nc, in_maps, core_ids=list(range(8)))
    return np.stack([np.asarray(r["outT"]).T for r in res.results], axis=0)


def _shared_arrays(inputs):
    ii = {k: np.asarray(v) for k, v in inputs.items()}
    wgP = np.zeros((D, 128), np.float32)
    wgP[:, 0:8] = ii["L0_attn_Wbeta"].T
    wgP[:, 32:40] = ii["L0_attn_Wig"].T
    wgP[:, 64:72] = ii["L0_attn_Wog"].T
    b128 = np.zeros((128, 1), np.float32)
    b128[0:8, 0] = ii["L0_attn_bbeta"]
    b128[32:40, 0] = ii["L0_attn_big"]
    b128[64:72, 0] = ii["L0_attn_bog"]
    return {
        "wqT": np.ascontiguousarray(ii["L0_attn_Wq"].T),
        "wkT": np.ascontiguousarray(ii["L0_attn_Wk"].T),
        "wvT": np.ascontiguousarray(ii["L0_attn_Wv"].T),
        "woutT": np.ascontiguousarray(ii["L0_attn_Wout"].T),
        "wgP": wgP, "b128": b128,
        "nw8": np.ascontiguousarray(ii["L0_attn_norm_w"].reshape(DC, 128).T),
        "gw8T": np.ascontiguousarray(ii["L0_ffn_gate_W"].T),
        "acatT": np.ascontiguousarray(
            np.transpose(ii["L0_ffn_lora_A"], (2, 0, 1)).reshape(D, 32)),
        "g8": np.ascontiguousarray(ii["L0_ffn_gate_b"].reshape(8, 1)),
        "bcatT": np.ascontiguousarray(
            np.transpose(ii["L0_ffn_lora_B"], (0, 2, 1)).reshape(32, FF)),
        "wiT": np.ascontiguousarray(ii["L0_ffn_wi_W"].T),
        "woT": np.ascontiguousarray(ii["L0_ffn_wo_W"].T),
        "wib32": np.ascontiguousarray(ii["L0_ffn_wi_b"].reshape(FC, 128).T),
        "wob8": np.ascontiguousarray(ii["L0_ffn_wo_b"].reshape(DC, 128).T),
        "bout8": np.ascontiguousarray(ii["L0_attn_bout"].reshape(DC, 128).T),
    }


# version 11
# speedup vs baseline: 1.1274x; 1.0920x over previous
import numpy as np
from contextlib import ExitStack

import concourse.bass as bass
import concourse.mybir as mybir
from concourse.bass_utils import run_bass_kernel_spmd

F32 = mybir.dt.float32
F32R = mybir.dt.float32r
BF16 = mybir.dt.bfloat16
U32 = mybir.dt.uint32
AF = mybir.ActivationFunctionType
ALU = mybir.AluOpType
AX = mybir.AxisListType

T, D, NH, DH, FF = 2048, 1024, 8, 128, 4096
NCH, NG, DC, FC = 16, 4, 8, 32
SCALE = 1.0 / float(np.sqrt(DH))

EMAP = {"PE": "tensor", "DVE": "vector", "ACT": "scalar", "POOL": "gpsimd",
        "SYNCQ": "sync", "ACTQ": "scalar", "POOLQ": "gpsimd"}
ENGS = ("PE", "DVE", "ACT", "POOL")


class TR:

    def __init__(self):
        self.cnt = {}
        self.lastw = {}
        self.readers = {}
        self.waited = {}
        self.floor = {}
        self.ops = {k: [] for k in ("tensor", "vector", "scalar", "gpsimd", "sync")}
        self.sem_names = set(ENGS)

    def barrier(self):
        self.floor = dict(self.cnt)

    def _wl(self, stream, r, w):
        need = dict(self.floor)
        for k in r:
            lw = self.lastw.get(k)
            if lw:
                need[lw[0]] = max(need.get(lw[0], 0), lw[1])
        for k in w:
            lw = self.lastw.get(k)
            if lw:
                need[lw[0]] = max(need.get(lw[0], 0), lw[1])
            for s2, v2 in self.readers.get(k, {}).items():
                need[s2] = max(need.get(s2, 0), v2)
        out = []
        wt = self.waited.setdefault(stream, {})
        for s2 in sorted(need):
            if wt.get(s2, 0) < need[s2]:
                wt[s2] = need[s2]
                out.append((s2, need[s2]))
        return out

    def _commit(self, sem, val, r, w):
        for k in w:
            self.lastw[k] = (sem, val)
            self.readers[k] = {}
        for k in r:
            self.readers.setdefault(k, {})[sem] = val

    def op(self, eng, fn, r=(), w=()):
        wl = self._wl(eng, r, w)
        self.cnt[eng] = self.cnt.get(eng, 0) + 1
        val = self.cnt[eng]

        def emit(engobj, sems, _wl=wl, _fn=fn, _eng=eng):
            for s2, v2 in _wl:
                engobj.wait_ge(sems[s2], v2)
            _fn(engobj).then_inc(sems[_eng], 1)
        self.ops[EMAP[eng]].append(emit)
        self._commit(eng, val, r, w)

    def dma(self, q, dsem, fn, r=(), w=()):
        self.sem_names.add(dsem)
        wl = self._wl(q, r, w)
        self.cnt[dsem] = self.cnt.get(dsem, 0) + 16
        val = self.cnt[dsem]

        def emit(engobj, sems, _wl=wl, _fn=fn, _ds=dsem):
            for s2, v2 in _wl:
                engobj.wait_ge(sems[s2], v2)
            _fn(engobj).then_inc(sems[_ds], 16)
        self.ops[EMAP[q]].append(emit)
        self._commit(dsem, val, r, w)

    def final_wait(self, stream, sem):
        v = self.cnt.get(sem, 0)
        self.ops[EMAP[stream]].append(
            lambda e, s, _sm=sem, _v=v: e.wait_ge(s[_sm], _v) if _v else None)

    def build_blocks(self, nc, ctx):
        sems = {n: ctx.enter_context(nc.semaphore(f"s_{n}"))
                for n in sorted(self.sem_names)}
        with nc.Block() as block:
            @block.sync
            def _(eng):
                for emit in self.ops["sync"]:
                    emit(eng, sems)

            @block.tensor
            def _(eng):
                for emit in self.ops["tensor"]:
                    emit(eng, sems)

            @block.vector
            def _(eng):
                for emit in self.ops["vector"]:
                    emit(eng, sems)

            @block.scalar
            def _(eng):
                for emit in self.ops["scalar"]:
                    emit(eng, sems)

            @block.gpsimd
            def _(eng):
                for emit in self.ops["gpsimd"]:
                    emit(eng, sems)


def build(upto=9):
    import ml_dtypes
    nc = bass.Bass("TRN2", target_bir_lowering=False, debug=False)

    def din(name, shape, dt=F32):
        return nc.dram_tensor(name, shape, dt, kind="ExternalInput").ap()
    xT = din("xT", [D, T], F32R)
    xTf = din("xTf", [D, T])
    wqT = din("wqT", [D, D]); wkT = din("wkT", [D, D]); wvT = din("wvT", [D, D])
    woutT = din("woutT", [D, D])
    wgP = din("wgP", [D, 128])
    b128 = din("b128", [128, 1])
    nw8 = din("nw8", [128, DC])
    gw8T = din("gw8T", [D, 8])
    acatT = din("acatT", [D, 32])
    g8 = din("g8", [8, 1])
    bcatT = din("bcatT", [32, FF])
    wiT = din("wiT", [D, FF])
    woT = din("woT", [FF, D])
    wib32 = din("wib32", [128, FC])
    wob8 = din("wob8", [128, DC])
    bout8 = din("bout8", [128, DC])
    outT = nc.dram_tensor("outT", [D, T], F32, kind="ExternalOutput").ap()
    x1h = nc.dram_tensor("x1h", [D, T], F32, kind="Internal").ap()
    wi16 = nc.dram_tensor("wi16", [D, FF], BF16, kind="Internal").ap()
    wo16 = nc.dram_tensor("wo16", [FF, D], BF16, kind="Internal").ap()
    wv16 = nc.dram_tensor("wv16", [D, D], BF16, kind="Internal").ap()
    wk16 = nc.dram_tensor("wk16", [D, D], BF16, kind="Internal").ap()
    wq16 = nc.dram_tensor("wq16", [D, D], BF16, kind="Internal").ap()

    eye_f = nc.inline_tensor(np.eye(128, dtype=np.float32), "eye_f").ap()
    eye_b = nc.inline_tensor(np.eye(128, dtype=ml_dtypes.bfloat16), "eye_b").ap()
    ones_f = nc.inline_tensor(np.ones((128, 128), np.float32), "ones_f").ap()
    m01 = (np.arange(128)[:, None] <= np.arange(128)[None, :]).astype(np.float64)
    mask8_np = np.tile(m01, (1, 8)).astype(ml_dtypes.bfloat16)
    mask8_c = nc.inline_tensor(mask8_np, "mask8").ap()
    hm_np = np.zeros((8, 1024), np.float64)
    for h in range(8):
        hm_np[h, h * 128:(h + 1) * 128] = 1.0
    hmask_c = nc.inline_tensor(hm_np.astype(ml_dtypes.bfloat16), "hmask8").ap()
    iota8_c = nc.inline_tensor(
        np.broadcast_to(np.arange(8, dtype=np.float32), (128, 8)).copy(), "iota8").ap()
    exp_np = np.zeros((8, 32), ml_dtypes.bfloat16)
    for e in range(8):
        exp_np[e, e * 4:(e + 1) * 4] = 1.0
    expand_c = nc.inline_tensor(exp_np, "expand832").ap()
    rot = np.zeros((128, 128), np.float64)
    rot[np.arange(64), 64 + np.arange(64)] = -1.0
    rot[64 + np.arange(64), np.arange(64)] = 1.0
    rotT_c = nc.inline_tensor(rot.T.astype(ml_dtypes.bfloat16), "rotT").ap()
    sel = np.zeros((128, 16), np.float32)
    sel[64 + np.arange(8), np.arange(8)] = 1.0
    sel[32 + np.arange(8), 8 + np.arange(8)] = 1.0
    sel_c = nc.inline_tensor(sel, "sel16").ap()
    half = DH // 2
    inv_freq = 1.0 / (10000.0 ** (np.arange(half, dtype=np.float64) / half))
    frq = np.arange(T, dtype=np.float64)[None, :] * inv_freq[:, None]
    cs2 = np.concatenate([np.cos(frq), np.cos(frq)], axis=0).astype(ml_dtypes.bfloat16)
    sn2 = np.concatenate([np.sin(frq), np.sin(frq)], axis=0).astype(ml_dtypes.bfloat16)
    ctab_c = nc.inline_tensor(cs2, "ctab").ap()
    stab_c = nc.inline_tensor(sn2, "stab").ap()
    z8_c = nc.inline_tensor(np.zeros((8, 128), np.float32), "z8").ap()

    tr = TR()

    class _Done(Exception):
        pass

    def ldc(dst, src, key):
        tr.dma("SYNCQ", "dc_" + key.replace(":", ""),
               lambda e, _d=dst, _s=src: e.dma_start(out=_d, in_=_s), w=[key])

    def pe_settle(ps_ap, keys):
        tr.op("PE", lambda e, _p=ps_ap: e.matmul(
            _p, lhsT=onesf_s[0:1, 0:2], rhs=onesf_s[0:1, 0:2],
            start=True, stop=True), r=["c:onesf"], w=list(keys))

    ctx = ExitStack()
    try:
      with ctx:
        SB = lambda n, s, d=F32: ctx.enter_context(nc.sbuf_tensor(n, s, d))
        eyef_s = SB("eyef", [128, 128])
        eyeb_s = SB("eyeb", [128, 128], BF16)
        onesf_s = SB("onesf", [128, 128]); onesr_s = SB("onesr", [128, 128], F32R)
        onesb_s = SB("onesb", [128, 128], BF16)
        iota8_s = SB("iota8s", [128, 8])
        expand_s = SB("expand", [8, 32], BF16)
        nw_s = SB("nw", [128, DC]); b128_s = SB("b128s", [128, 1])
        g8_s = SB("g8s", [8, 1])
        wib_s = SB("wibc", [128, FC]); wob_s = SB("wobc", [128, DC]); bout_s = SB("boutc", [128, DC])
        cb_s = SB("cbias", [128, 4])
        ldc(eyef_s[:, :], eye_f, "c:eyef")
        ldc(eyeb_s[:, :], eye_b, "c:eyeb")
        ldc(onesf_s[:, :], ones_f, "c:onesf")
        ldc(iota8_s[:, :], iota8_c, "c:iota"); ldc(expand_s[:, :], expand_c, "c:exp")
        ldc(nw_s[:, :], nw8, "c:nw"); ldc(b128_s[:, :], b128, "c:b128")
        ldc(g8_s[:, :], g8, "c:g8"); ldc(wib_s[:, :], wib32, "c:wib")
        ldc(wob_s[:, :], wob8, "c:wob"); ldc(bout_s[:, :], bout8, "c:bout")
        for j, v in ((0, 1e-11), (1, 1.0), (2, 1e-5),
                     (3, float(np.log(SCALE)))):
            tr.op("DVE", lambda e, _j=j, _v=v: e.memset(cb_s[:, _j:_j + 1], _v), w=["c:cb"])
        tr.op("DVE", lambda e: e.tensor_copy(out=onesr_s[:, :], in_=onesf_s[:, :]),
              r=["c:onesf"], w=["c:onesr"])
        tr.op("DVE", lambda e: e.tensor_copy(out=onesb_s[:, :], in_=onesf_s[:, :]),
              r=["c:onesf"], w=["c:onesb"])
        tr.dma("ACTQ", "d_x1init", lambda e: e.dma_start(out=x1h, in_=xTf),
               w=[f"x1h.{g}" for g in range(NG)])
        tr.dma("POOLQ", "d_wv16", lambda e: e.dma_start(out=wv16, in_=wvT), w=["wv16"])
        tr.dma("POOLQ", "d_wk16", lambda e: e.dma_start(out=wk16, in_=wkT), w=["wk16"])
        tr.dma("POOLQ", "d_wq16", lambda e: e.dma_start(out=wq16, in_=wqT), w=["wq16"])

        with ExitStack() as attn:
            _a = attn.enter_context
            ct_s = [_a(nc.sbuf_tensor(f"cts{i}", [128, 512], BF16)) for i in range(2)]
            st_s = [_a(nc.sbuf_tensor(f"sts{i}", [128, 512], BF16)) for i in range(2)]
            rotT_s = _a(nc.sbuf_tensor("rotTs", [128, 128], BF16))
            mask8_s = _a(nc.sbuf_tensor("mask8s", [128, 1024], BF16))
            hmask_s = _a(nc.sbuf_tensor("hmasks", [8, 1024], BF16))
            sel_s = _a(nc.sbuf_tensor("sel16s", [128, 16], F32))
            dg8_s = _a(nc.sbuf_tensor("dg8s", [8, 32], F32))
            z8_s = _a(nc.sbuf_tensor("z8s", [8, 128], F32))
            xsl3 = [_a(nc.sbuf_tensor(f"xsl3{i}", [128, DC, 256], F32R)) for i in range(2)]
            xsq = _a(nc.sbuf_tensor("xsq", [128, DC, 256], BF16))
            xhat = [_a(nc.sbuf_tensor(f"xhat{i}", [128, DC, 512], BF16)) for i in range(2)]
            rrep_s = _a(nc.sbuf_tensor("rrep", [128, 512], F32))
            t1_s = _a(nc.sbuf_tensor("t1s", [128, 256], F32))
            mrow_s = _a(nc.sbuf_tensor("mrow", [1, 1536], F32))
            mrr_s = _a(nc.sbuf_tensor("mrr", [1, 1024], F32R))
            wg_s = _a(nc.sbuf_tensor("wg", [128, DC, 128], F32R))
            wsg_s = _a(nc.sbuf_tensor("wsg", [1, 128], F32R))
            colt_s = _a(nc.sbuf_tensor("colt", [128, 1], F32))
            zt_s = _a(nc.sbuf_tensor("zt", [128, 256], F32))
            sg1_s = _a(nc.sbuf_tensor("sg1", [128, 256], F32))
            ls_s = _a(nc.sbuf_tensor("ls", [128, 512], F32))
            lhat_s = _a(nc.sbuf_tensor("lhat", [8, 512], F32))
            lgml_s = _a(nc.sbuf_tensor("lgml", [8, 512], F32))
            lbuf_s = _a(nc.sbuf_tensor("lbuf", [8, 512], F32))
            ehat_s = [_a(nc.sbuf_tensor(f"ehat{i}", [8, 512], F32)) for i in range(2)]
            em8_s = _a(nc.sbuf_tensor("em8", [8, 1024], BF16))
            tmg_s = [_a(nc.sbuf_tensor(f"tmg{i}", [128, 64], F32)) for i in range(2)]
            egml_s = _a(nc.sbuf_tensor("egml", [128, 8], F32))
            ue8_s = _a(nc.sbuf_tensor("ue8", [128, 8], F32))
            ac8_s = _a(nc.sbuf_tensor("ac8", [128, 8], F32))
            wpb = [_a(nc.sbuf_tensor(f"wpb{i}", [128, 4, 1024], BF16)) for i in range(3)]
            qb = _a(nc.sbuf_tensor("qb", [128, NH, 1024], BF16))
            kb = _a(nc.sbuf_tensor("kb", [128, NH, 1024], BF16))
            vb = _a(nc.sbuf_tensor("vb", [128, 8, D], BF16))
            ropeA = [_a(nc.sbuf_tensor(f"ropeA{i}", [128, 512], BF16)) for i in range(2)]
            tmp1 = [_a(nc.sbuf_tensor(f"tmp1{i}", [128, 512], BF16)) for i in range(2)]
            tmp2 = [_a(nc.sbuf_tensor(f"tmp2{i}", [128, 512], BF16)) for i in range(2)]
            qtl = [_a(nc.sbuf_tensor(f"qtl{i}", [128, NH, 128], BF16)) for i in range(2)]
            pbt_s = _a(nc.sbuf_tensor("pbt", [128, NH, 128], BF16))
            vtl_s = _a(nc.sbuf_tensor("vtl", [128, NH, 128], BF16))
            ktm_s = _a(nc.sbuf_tensor("ktm", [128, NH, 128], BF16))
            msb_s = _a(nc.sbuf_tensor("msb", [128, NH, DH], F32))
            mbb_s = _a(nc.sbuf_tensor("mbb", [128, NH, DH], BF16))
            yring = _a(nc.sbuf_tensor("yring", [128, NH, 512], BF16))
            aosl = _a(nc.sbuf_tensor("aosl", [128, DC, 256], F32))
            woutb = _a(nc.sbuf_tensor("woutb", [128, NH, D], BF16))
            psP0 = _a(nc.psum_tensor("psP0", [128, 512], F32))
            psP1 = _a(nc.psum_tensor("psP1", [128, 512], F32))
            psStat = _a(nc.psum_tensor("psStat", [128, 512], F32))
            psE = _a(nc.psum_tensor("psE", [128, NH * 128], F32))
            psA = _a(nc.psum_tensor("psA", [128, NH * 128], F32))
            psXK = _a(nc.psum_tensor("psXK", [128, NH * 128], BF16))
            if upto < 1:
                raise _Done()

            ldc(rotT_s[:, :], rotT_c, "c:rot"); ldc(mask8_s[:, :], mask8_c, "c:mask8")
            ldc(hmask_s[:, :], hmask_c, "c:hmask"); ldc(sel_s[:, :], sel_c, "c:sel")
            ldc(z8_s[:, :], z8_c, "c:z8")

            tr.dma("POOLQ", "d_wg",
                   lambda e: e.dma_start(out=wg_s[:, :, :],
                                         in_=wgP.rearrange("(c p) e -> p c e", p=128)),
                   w=["wg"])

            def _fold(e):
                for dc in range(DC):
                    i = e.mul(out=wg_s[:, dc, :], in_=wg_s[:, dc, :],
                              mul=nw_s[:, dc:dc + 1])
                return i
            tr.op("ACT", _fold, r=["wg", "c:nw"], w=["wg"])

            def _wgcs(e):
                for dc in range(DC):
                    i = e.matmul(psP0[:, 0:2], lhsT=wg_s[:, dc, :], rhs=onesr_s[:, 0:2],
                                 start=(dc == 0), stop=(dc == DC - 1))
                return i
            tr.op("PE", _wgcs, r=["wg", "c:onesr", "psP0"], w=["psP0"])
            tr.op("ACT", lambda e: e.activation(out=colt_s[:, :], in_=psP0[:, 0:1],
                                                func=AF.Copy),
                  r=["psP0"], w=["colt"])
            tr.op("PE", lambda e: e.transpose(psP0[0:1, 0:128], colt_s[:, :], eyef_s[:, :]),
                  r=["colt", "c:eyef", "psP0"], w=["psP0"])
            pe_settle(psP0[32:34, 200:202], ["psP0"])
            tr.op("ACT", lambda e: e.activation(out=wsg_s[0:1, :], in_=psP0[0:1, 0:128],
                                                func=AF.Copy, scale=-1.0),
                  r=["psP0"], w=["wsg"])
            tr.dma("POOLQ", "d_wout",
                   lambda e: e.dma_start(out=woutb[:, :, :],
                                         in_=woutT.rearrange("(c p) e -> p c e", p=128)),
                   w=["woutb"])

            MR = {"mrow": 0, "msq": 512, "vrow": 1024}
            RR = {"mrowR": 0, "rrow": 512}

            def mrs(k):
                return mrow_s[0:1, MR[k]:MR[k] + 512]

            def rrs(k):
                return mrr_s[0:1, RR[k]:RR[k] + 512]

            def front(g):
                gp = g % 2
                gsl = slice(g * 512, (g + 1) * 512)
                tr.dma("SYNCQ", f"d_ct{gp}",
                       lambda e, _gp=gp, _gsl=gsl: e.dma_start(
                           out=ct_s[_gp][:, :], in_=ctab_c[:, _gsl]),
                       w=[f"ct{gp}"])
                tr.dma("SYNCQ", f"d_st{gp}",
                       lambda e, _gp=gp, _gsl=gsl: e.dma_start(
                           out=st_s[_gp][:, :], in_=stab_c[:, _gsl]),
                       w=[f"st{gp}"])
                for hb in range(2):
                    xg256 = slice(g * 512 + hb * 256, g * 512 + (hb + 1) * 256)
                    tr.dma("SYNCQ", f"d_xsl3{hb}",
                           lambda e, _hb=hb, _s=xg256: e.dma_start(
                               out=xsl3[_hb][:, :, :],
                               in_=xT.rearrange("(c p) t -> p c t", p=128)[:, :, _s]),
                           w=[f"xsl3{hb}"])
                    tr.op("ACT", lambda e, _hb=hb: e.activation(
                        out=xsq[:, :, :].rearrange("p c t -> p (c t)"),
                        in_=xsl3[_hb][:, :, :].rearrange("p c t -> p (c t)"),
                        func=AF.Square),
                        r=[f"xsl3{hb}"], w=["xsq"])

                    def _sums(e, _hb=hb):
                        for dc in range(DC):
                            e.matmul(psStat[0:1, _hb * 256:(_hb + 1) * 256],
                                     lhsT=onesr_s[:, 0:1], rhs=xsl3[_hb][:, dc, :],
                                     start=(dc == 0), stop=(dc == DC - 1))
                        for dc in range(DC):
                            i = e.matmul(psP1[0:1, _hb * 256:(_hb + 1) * 256],
                                         lhsT=onesb_s[:, 0:1], rhs=xsq[:, dc, :],
                                         start=(dc == 0), stop=(dc == DC - 1))
                        return i
                    tr.op("PE", _sums,
                          r=[f"xsl3{hb}", "xsq", "c:onesr", "c:onesb", "psStat", "psP1"],
                          w=["psStat", "st:rows", "psP1"])
                tr.op("ACT", lambda e: e.activation(out=mrs("mrow"), in_=psStat[0:1, :],
                                                    func=AF.Copy, scale=1.0 / D),
                      r=["st:rows"], w=["r:mrow"])
                tr.op("ACT", lambda e: e.activation(out=rrs("mrowR"), in_=psStat[0:1, :],
                                                    func=AF.Copy, scale=1.0 / D),
                      r=["st:rows"], w=["r:mrowR"])
                tr.op("ACT", lambda e: e.activation(out=mrs("msq"), in_=mrs("mrow"),
                                                    func=AF.Square),
                      r=["r:mrow"], w=["r:msq"])
                tr.op("DVE", lambda e: e.tensor_scalar(out=mrs("vrow"), in0=psP1[0:1, :],
                                                       scalar1=1.0 / D, scalar2=None,
                                                       op0=ALU.mult),
                      r=["st:rows", "psP1"], w=["r:vrow"])
                tr.op("DVE", lambda e: e.tensor_sub(mrs("vrow"), mrs("vrow"), mrs("msq")),
                      r=["r:vrow", "r:msq"], w=["r:vrow"])
                tr.op("ACT", lambda e: e.activation(out=mrs("vrow"), in_=mrs("vrow"),
                                                    func=AF.Ln, scale=1.000001,
                                                    bias=cb_s[0:1, 0:1]),
                      r=["r:vrow", "c:cb"], w=["r:vrow"])
                tr.op("ACT", lambda e: e.activation(out=rrs("rrow"), in_=mrs("vrow"),
                                                    func=AF.Exp, scale=-0.5),
                      r=["r:vrow"], w=["r:rrow"])
                tr.op("PE", lambda e: e.matmul(psP0[:, :], lhsT=onesr_s[0:1, :],
                                               rhs=rrs("rrow"), start=True, stop=True),
                      r=["r:rrow", "c:onesr", "psP0"], w=["psP0"])
                tr.op("ACT", lambda e: e.activation(out=rrep_s[:, :], in_=psP0[:, :],
                                                    func=AF.Copy),
                      r=["psP0"], w=["rrep"])
                tr.op("PE", lambda e: e.matmul(psStat[:, :], lhsT=onesr_s[0:1, :],
                                               rhs=rrs("mrowR"), start=True, stop=True),
                      r=["r:mrowR", "c:onesr", "psStat", "st:rows"], w=["psStat", "st:mrep"])
                for hb in range(2):
                    hsl = slice(hb * 256, (hb + 1) * 256)
                    for dc in range(DC):
                        tr.op("DVE", lambda e, _hb=hb, _dc=dc, _hsl=hsl: e.tensor_sub(
                            t1_s[:, :], xsl3[_hb][:, _dc, :], psStat[:, _hsl]),
                            r=[f"xsl3{hb}", "st:mrep"], w=["t1"])
                        tr.op("DVE", lambda e, _hb=hb, _dc=dc, _hsl=hsl, _gp=gp:
                              e.scalar_tensor_tensor(
                                  out=xhat[_gp][:, _dc, _hsl], in0=t1_s[:, :],
                                  scalar=nw_s[:, _dc:_dc + 1], in1=rrep_s[:, _hsl],
                                  op0=ALU.mult, op1=ALU.mult),
                              r=["t1", "c:nw", "rrep"], w=[f"xhat{gp}.{dc}"])
                for hb in range(2):
                    hsl = slice(hb * 256, (hb + 1) * 256)

                    def _zg(e, _hb=hb):
                        for dc in range(DC):
                            e.matmul(psP0[:, 0:256], lhsT=wg_s[:, dc, :],
                                     rhs=xsl3[_hb][:, dc, :], start=(dc == 0), stop=False)
                        return e.matmul(psP0[:, 0:256], lhsT=wsg_s[0:1, :],
                                        rhs=mrr_s[0:1, _hb * 256:(_hb + 1) * 256],
                                        start=False, stop=True)
                    tr.op("PE", _zg, r=["wg", f"xsl3{hb}", "wsg", "r:mrowR", "psP0"],
                          w=["psP0"])
                    tr.op("DVE", lambda e, _hsl=hsl: e.tensor_mul(zt_s[:, :], psP0[:, 0:256],
                                                                  rrep_s[:, _hsl]),
                          r=["psP0", "rrep"], w=["zt"])
                    tr.op("DVE", lambda e: e.tensor_scalar(out=zt_s[:, :], in0=zt_s[:, :],
                                                           scalar1=b128_s[:, :], scalar2=None,
                                                           op0=ALU.add),
                          r=["zt", "c:b128"], w=["zt"])
                    tr.op("ACT", lambda e: e.activation(out=sg1_s[:, :], in_=zt_s[:, :],
                                                        func=AF.Abs),
                          r=["zt"], w=["sg1"])
                    tr.op("ACT", lambda e: e.activation(out=sg1_s[:, :], in_=sg1_s[:, :],
                                                        func=AF.Exp, scale=-1.0),
                          r=["sg1"], w=["sg1"])
                    tr.op("ACT", lambda e: e.activation(out=sg1_s[:, :], in_=sg1_s[:, :],
                                                        func=AF.Ln, bias=cb_s[:, 1:2]),
                          r=["sg1", "c:cb"], w=["sg1"])
                    tr.op("DVE", lambda e: e.tensor_scalar_min(zt_s[:, :], zt_s[:, :], 0.0),
                          r=["zt"], w=["zt"])
                    tr.op("DVE", lambda e, _hsl=hsl: e.tensor_sub(ls_s[:, _hsl], zt_s[:, :],
                                                                  sg1_s[:, :]),
                          r=["zt", "sg1"], w=[f"ls.{hb}"])
                    tr.op("PE", lambda e, _hsl=hsl: e.matmul(
                        psP1[0:8, 0:256], lhsT=sel_s[:, 0:8], rhs=ls_s[:, _hsl],
                        start=True, stop=True),
                        r=["c:sel", f"ls.{hb}", "psP1"], w=["psP1"])
                    tr.op("ACT", lambda e, _hsl=hsl: e.activation(out=lhat_s[:, _hsl],
                                                                  in_=psP1[0:8, 0:256],
                                                                  func=AF.Copy),
                          r=["psP1"], w=["lhat"])
                    tr.op("PE", lambda e, _hsl=hsl: e.matmul(
                        psP1[0:8, 0:256], lhsT=sel_s[:, 8:16], rhs=ls_s[:, _hsl],
                        start=True, stop=True),
                        r=["c:sel", f"ls.{hb}", "psP1"], w=["psP1"])
                    tr.op("ACT", lambda e, _hsl=hsl: e.activation(out=lgml_s[:, _hsl],
                                                                  in_=psP1[0:8, 0:256],
                                                                  func=AF.Copy),
                          r=["psP1"], w=["lgml"])
                for lc in range(4):
                    csl = slice(lc * 128, (lc + 1) * 128)
                    hb = lc // 2
                    tr.op("DVE", lambda e, _sl=csl: e.tensor_tensor_scan(
                        out=lbuf_s[:, _sl], data0=ls_s[0:8, _sl], data1=z8_s[:, :],
                        initial=0.0, op0=ALU.add, op1=ALU.add),
                        r=[f"ls.{hb}", "c:z8"], w=["lbuf"])
                    tr.op("DVE", lambda e, _sl=csl: e.tensor_add(
                        lhat_s[:, _sl], lhat_s[:, _sl], lbuf_s[:, _sl]),
                        r=["lbuf", "lhat"], w=["lhat"])
                    tr.op("DVE", lambda e, _sl=csl: e.tensor_sub(
                        lgml_s[:, _sl], lgml_s[:, _sl], lbuf_s[:, _sl]),
                        r=["lbuf", "lgml"], w=["lgml"])
                tr.op("ACT", lambda e, _gp=gp: e.activation(
                    out=ehat_s[_gp][:, :], in_=lhat_s[:, :], func=AF.Exp,
                    bias=cb_s[0:8, 3:4]),
                    r=["lhat", "c:cb"], w=[f"ehat{gp}"])
                for lc in range(4):
                    tr.op("DVE", lambda e, _lc=lc: e.tensor_scalar(
                        out=dg8_s[:, _lc * 8:(_lc + 1) * 8], in0=eyef_s[0:8, 0:8],
                        scalar1=lbuf_s[:, _lc * 128 + 127:_lc * 128 + 128], scalar2=None,
                        op0=ALU.mult),
                        r=["lbuf", "c:eyef"], w=["dg8"])

                def _hoist(e):
                    for lc in range(4):
                        e.transpose(psStat[:, lc * 16:lc * 16 + 8],
                                    lgml_s[:, lc * 128:(lc + 1) * 128], eyef_s[0:8, 0:8])
                    for lc in range(4):
                        i = e.matmul(psStat[:, lc * 16 + 8:lc * 16 + 16],
                                     lhsT=onesf_s[0:8, :],
                                     rhs=dg8_s[:, lc * 8:(lc + 1) * 8],
                                     start=True, stop=True)
                    return i
                tr.op("PE", _hoist, r=["lgml", "dg8", "c:eyef", "c:onesf"],
                      w=["psStat", "st:hoist", "st:mrep"])
                tr.op("ACT", lambda e, _gp=gp: e.activation(
                    out=tmg_s[_gp][:, :], in_=psStat[:, 0:64], func=AF.Copy),
                    r=["st:hoist"], w=[f"tmg{gp}"])

            wp_ctr = [0]

            def proj(g):
                gp = g % 2
                gsl = slice(g * 512, (g + 1) * 512)
                for pj, wsrc in (("v", wv16), ("k", wk16), ("q", wq16)):
                    sl0 = wp_ctr[0] % 3
                    sl1 = (wp_ctr[0] + 1) % 3
                    wp_ctr[0] += 2
                    for hf, wsl in ((0, sl0), (1, sl1)):
                        tr.dma("SYNCQ", f"d_wp{wsl}",
                               lambda e, _w=wsrc, _hf=hf, _wsl=wsl: e.dma_start(
                                   out=wpb[_wsl][:, :, :],
                                   in_=_w.rearrange("(c p) e -> p c e", p=128)
                                   [:, _hf * 4:(_hf + 1) * 4, :]),
                               r=[f"w{pj}16"], w=[f"wpb{wsl}"])
                    if pj == "v":
                        for tc4 in range(4):
                            cg = g * 4 + tc4
                            slot = cg % 8
                            for eh in range(2):
                                pp, pk = (psP0, "psP0") if eh == 0 else (psP1, "psP1")

                                def _vg(e, _tc=tc4, _eh=eh, _pp=pp, _gp=gp,
                                        _s0=sl0, _s1=sl1):
                                    for dc in range(DC):
                                        i = e.matmul(
                                            _pp[:, :],
                                            lhsT=xhat[_gp][:, dc, _tc * 128:(_tc + 1) * 128],
                                            rhs=wpb[_s0 if dc < 4 else _s1][:, dc % 4,
                                                             _eh * 512:(_eh + 1) * 512],
                                            start=(dc == 0), stop=(dc == DC - 1))
                                    return i
                                tr.op("PE", _vg,
                                      r=[f"xhat{gp}.{dc}" for dc in range(DC)]
                                      + [f"wpb{sl0}", f"wpb{sl1}", pk], w=[pk])
                                tr.op("ACT", lambda e, _s=slot, _eh=eh, _pp=pp: e.activation(
                                    out=vb[:, _s, _eh * 512:(_eh + 1) * 512], in_=_pp[:, :],
                                    func=AF.Copy),
                                    r=[pk], w=[f"vb.{slot}"])
                    else:
                        dst = kb if pj == "k" else qb
                        dkey = pj + "b"
                        for h in range(NH):
                            hp = h % 2
                            pp, pk = (psP0, "psP0") if hp == 0 else (psP1, "psP1")

                            def _qg(e, _h=h, _pp=pp, _gp=gp, _s0=sl0, _s1=sl1):
                                for dc in range(DC):
                                    i = e.matmul(
                                        _pp[:, :],
                                        lhsT=wpb[_s0 if dc < 4 else _s1][:, dc % 4,
                                                                         _h * 128:(_h + 1) * 128],
                                        rhs=xhat[_gp][:, dc, :],
                                        start=(dc == 0), stop=(dc == DC - 1))
                                return i
                            tr.op("PE", _qg,
                                  r=[f"xhat{gp}.{dc}" for dc in range(DC)]
                                  + [f"wpb{sl0}", f"wpb{sl1}", pk], w=[pk])
                            tr.op("ACT", lambda e, _pp=pp, _hp=hp: e.activation(
                                out=ropeA[_hp][:, :], in_=_pp[:, :], func=AF.Copy),
                                r=[pk], w=[f"ropeA{hp}"])
                            tr.op("PE", lambda e, _hp=hp: e.matmul(
                                psStat[:, :], lhsT=rotT_s[:, :], rhs=ropeA[_hp][:, :],
                                start=True, stop=True),
                                r=["c:rot", f"ropeA{hp}"],
                                w=["psStat", "st:mrep", "st:hoist"])
                            tr.op("DVE", lambda e, _hp=hp, _gp=gp: e.tensor_mul(
                                tmp1[_hp][:, :], ropeA[_hp][:, :], ct_s[_gp][:, :]),
                                r=[f"ropeA{hp}", f"ct{gp}"], w=[f"tmp1{hp}"])
                            tr.op("DVE", lambda e, _hp=hp, _gp=gp: e.tensor_mul(
                                tmp2[_hp][:, :], psStat[:, :], st_s[_gp][:, :]),
                                r=["psStat", f"st{gp}"], w=[f"tmp2{hp}"])
                            tr.op("DVE", lambda e, _d=dst, _h=h, _gp=gp, _hp=hp: e.tensor_add(
                                _d[:, _h, _gp * 512:(_gp + 1) * 512],
                                tmp1[_hp][:, :], tmp2[_hp][:, :]),
                                r=[f"tmp1{hp}", f"tmp2{hp}"], w=[f"{dkey}.{h}.g{g}"])

            def scanwout(g):
                gp = g % 2
                for lc in range(4):
                    cg = g * 4 + lc
                    cp = cg % 2
                    slot = cg % 8
                    csl = slice(lc * 128, (lc + 1) * 128)
                    rsl = slice(gp * 512 + lc * 128, gp * 512 + (lc + 1) * 128)
                    tr.op("ACT", lambda e, _gp=gp, _lc=lc: e.activation(
                        out=egml_s[:, :], in_=tmg_s[_gp][:, _lc * 16:_lc * 16 + 8],
                        func=AF.Exp),
                        r=[f"tmg{gp}", "uegate"], w=["egml"])
                    tr.op("DVE", lambda e, _gp=gp, _lc=lc: e.tensor_add(
                        ue8_s[:, :], tmg_s[_gp][:, _lc * 16:_lc * 16 + 8],
                        tmg_s[_gp][:, _lc * 16 + 8:_lc * 16 + 16]),
                        r=[f"tmg{gp}", "uegate"], w=["ue8"])
                    tr.op("ACT", lambda e: e.activation(out=ue8_s[:, :], in_=ue8_s[:, :],
                                                        func=AF.Exp),
                          r=["ue8"], w=["ue8"])
                    tr.op("ACT", lambda e, _gp=gp, _lc=lc: e.activation(
                        out=ac8_s[:, :], in_=tmg_s[_gp][:, _lc * 16 + 8:_lc * 16 + 16],
                        func=AF.Exp),
                        r=[f"tmg{gp}"], w=["ac8"])
                    tr.op("DVE", lambda e, _gp=gp, _csl=csl: e.tensor_mul(
                        em8_s[:, :].rearrange("p (h t) -> p h t", h=NH),
                        ehat_s[_gp][:, _csl].rearrange("p (o t) -> p o t", o=1)
                        .broadcast_to([8, NH, 128]),
                        hmask_s[:, :].rearrange("p (h t) -> p h t", h=NH)),
                        r=[f"ehat{gp}", "c:hmask"], w=["em8"])

                    def _bc(e):
                        for bk in range(2):
                            i = e.matmul(psE[:, bk * 512:(bk + 1) * 512],
                                         lhsT=onesb_s[0:8, :],
                                         rhs=em8_s[:, bk * 512:(bk + 1) * 512],
                                         start=True, stop=True)
                        return i
                    tr.op("PE", _bc, r=["em8", "c:onesb", "psE"], w=["psE"])
                    tr.op("DVE", lambda e, _cp=cp, _rsl=rsl: e.tensor_mul(
                        qtl[_cp][:, :, :],
                        qb[:, :, _rsl],
                        psE[:, :].rearrange("p (h t) -> p h t", h=NH)),
                        r=[f"qb.{h}.g{g}" for h in range(NH)] + ["psE"],
                        w=[f"qtl{cp}"])

                    def _ska(e, _cp=cp, _rsl=rsl):
                        for h in range(NH):
                            e.transpose(psXK[:, h * 128:(h + 1) * 128],
                                        kb[:, h, _rsl], eyeb_s[:, :])
                            i = e.matmul(psA[:, h * 128:(h + 1) * 128],
                                         lhsT=kb[:, h, _rsl], rhs=qtl[_cp][:, h, :],
                                         start=True, stop=True)
                        return i
                    tr.op("PE", _ska,
                          r=[f"kb.{h}.g{g}" for h in range(NH)]
                          + [f"qtl{cp}", "c:eyeb", "psXK", "psA"],
                          w=["psXK", "psA"])
                    tr.op("DVE", lambda e: e.tensor_mul(
                        pbt_s[:, :, :],
                        psA[:, :].rearrange("p (h t) -> p h t", h=NH),
                        egml_s[:, :].rearrange("p (h o) -> p h o", o=1)
                        .broadcast_to([128, NH, 128])),
                        r=["psA", "egml"], w=["pbt"])
                    tr.op("DVE", lambda e: e.tensor_mul(
                        pbt_s[:, :, :].rearrange("p h t -> p (h t)"),
                        pbt_s[:, :, :].rearrange("p h t -> p (h t)"),
                        mask8_s[:, :]),
                        r=["pbt", "c:mask8"], w=["pbt"])
                    tr.op("ACT", lambda e: e.activation(
                        out=ktm_s[:, :, :].rearrange("p h t -> p (h t)"), in_=psXK[:, :],
                        func=AF.Copy),
                        r=["psXK"], w=["ktm"])
                    if cg < NCH - 1:
                        def _vtg(e, _s=slot):
                            for h in range(NH):
                                i = e.tensor_scalar(
                                    out=vtl_s[:, h, :],
                                    in0=vb[:, _s, h * 128:(h + 1) * 128],
                                    scalar1=ue8_s[:, h:h + 1], scalar2=None, op0=ALU.mult)
                            return i
                        tr.op("DVE", _vtg, r=[f"vb.{slot}", "ue8"], w=["vtl", "uegate"])

                        def _kvg(e):
                            for h in range(NH):
                                i = e.matmul(psA[:, h * 128:(h + 1) * 128],
                                             lhsT=ktm_s[:, h, :], rhs=vtl_s[:, h, :],
                                             start=True, stop=True)
                            return i
                        tr.op("PE", _kvg, r=["ktm", "vtl", "pbt", "psA"],
                              w=["psA"])
                    if cg > 0:
                        def _yinter(e, _cp=cp):
                            for h in range(NH):
                                i = e.matmul(psE[:, h * 128:(h + 1) * 128],
                                             lhsT=mbb_s[:, h, :], rhs=qtl[_cp][:, h, :],
                                             start=True, stop=True)
                            return i
                        tr.op("PE", _yinter, r=["mbb", f"qtl{cp}", "psE"], w=["psE"])
                    if cg < NCH - 1:
                        if cg == 0:
                            for bk in range(2):
                                tr.op("DVE", lambda e, _b=bk: e.tensor_copy(
                                    out=msb_s[:, 4 * _b:4 * (_b + 1), :],
                                    in_=psA[:, _b * 512:(_b + 1) * 512]
                                    .rearrange("p (h t) -> p h t", h=4)),
                                    r=["psA"], w=["msb"])
                        else:
                            def _mdec(e):
                                for h in range(NH):
                                    i = e.tensor_scalar(
                                        out=msb_s[:, h, :], in0=msb_s[:, h, :],
                                        scalar1=ac8_s[:, h:h + 1], scalar2=None,
                                        op0=ALU.mult)
                                return i
                            tr.op("DVE", _mdec, r=["msb", "ac8"], w=["msb"])
                            for bk in range(2):
                                tr.op("DVE", lambda e, _b=bk: e.tensor_add(
                                    msb_s[:, 4 * _b:4 * (_b + 1), :],
                                    msb_s[:, 4 * _b:4 * (_b + 1), :],
                                    psA[:, _b * 512:(_b + 1) * 512]
                                    .rearrange("p (h t) -> p h t", h=4)),
                                    r=["msb", "psA"], w=["msb"])
                        tr.op("ACT", lambda e: e.activation(
                            out=mbb_s[:, :, :].rearrange("p h t -> p (h t)"),
                            in_=msb_s[:, :, :].rearrange("p h t -> p (h t)"), func=AF.Copy),
                            r=["msb"], w=["mbb"])
                    def _yintra(e, _s=slot):
                        for h in range(NH):
                            i = e.matmul(psA[:, h * 128:(h + 1) * 128],
                                         lhsT=vb[:, _s, h * 128:(h + 1) * 128],
                                         rhs=pbt_s[:, h, :],
                                         start=True, stop=True)
                        return i
                    tr.op("PE", _yintra, r=[f"vb.{slot}", "pbt", "psA"], w=["psA"])
                    if cg > 0:
                        for bk in range(2):
                            tr.op("DVE", lambda e, _lc=lc, _b=bk: e.tensor_copy(
                                out=yring[:, 4 * _b:4 * (_b + 1), _lc * 128:(_lc + 1) * 128],
                                in_=psE[:, _b * 512:(_b + 1) * 512]
                                .rearrange("p (h t) -> p h t", h=4)),
                                r=["psE"], w=[f"yring.{lc}"])
                        for bk in range(2):
                            tr.op("DVE", lambda e, _lc=lc, _b=bk: e.tensor_add(
                                yring[:, 4 * _b:4 * (_b + 1), _lc * 128:(_lc + 1) * 128],
                                yring[:, 4 * _b:4 * (_b + 1), _lc * 128:(_lc + 1) * 128],
                                psA[:, _b * 512:(_b + 1) * 512]
                                .rearrange("p (h t) -> p h t", h=4)),
                                r=["psA", f"yring.{lc}"], w=[f"yring.{lc}"])
                    else:
                        for bk in range(2):
                            tr.op("DVE", lambda e, _lc=lc, _b=bk: e.tensor_copy(
                                out=yring[:, 4 * _b:4 * (_b + 1), _lc * 128:(_lc + 1) * 128],
                                in_=psA[:, _b * 512:(_b + 1) * 512]
                                .rearrange("p (h t) -> p h t", h=4)),
                                r=["psA"], w=[f"yring.{lc}"])
                for wh in range(2):
                    for dcp in range(DC):
                        pp, pk = (psP0, "psP0") if dcp % 2 == 0 else (psP1, "psP1")

                        def _wog(e, _dcp=dcp, _pp=pp, _wh=wh):
                            for h in range(NH):
                                i = e.matmul(
                                    _pp[:, 0:256],
                                    lhsT=woutb[:, h, _dcp * 128:(_dcp + 1) * 128],
                                    rhs=yring[:, h, _wh * 256:(_wh + 1) * 256],
                                    start=(h == 0), stop=(h == NH - 1))
                            return i
                        tr.op("PE", _wog,
                              r=["woutb"] + [f"yring.{i}" for i in range(4)] + [pk], w=[pk])
                        tr.op("DVE", lambda e, _dcp=dcp, _pp=pp: e.tensor_scalar(
                            out=aosl[:, _dcp, :], in0=_pp[:, 0:256],
                            scalar1=bout_s[:, _dcp:_dcp + 1], scalar2=None, op0=ALU.add),
                            r=[pk, "c:bout"], w=["aosl"])
                    tr.dma("POOLQ", "d_ao",
                           lambda e, _g=g, _wh=wh: e.dma_start(
                               out=x1h.rearrange("(c p) t -> p c t", p=128)
                               [:, :, _g * 512 + _wh * 256:_g * 512 + (_wh + 1) * 256],
                               in_=aosl[:, :, :], accum_op=ALU.add),
                           r=["aosl", f"x1h.{g}"], w=[f"x1h.{g}"])

            front(0)
            if upto < 2:
                raise _Done()
            for g in range(NG):
                if g > 0:
                    front(g)
                proj(g)
                if upto < 3:
                    raise _Done()
                if g > 0:
                    scanwout(g - 1)
                if g == 2:
                    tr.dma("POOLQ", "d_wi16", lambda e: e.dma_start(out=wi16, in_=wiT),
                           w=["wi16"])
            scanwout(NG - 1)
            tr.dma("POOLQ", "d_wo16", lambda e: e.dma_start(out=wo16, in_=woT),
                   w=["wo16"])

        tr.barrier()
        with ExitStack() as ph6:
            _e6 = ph6.enter_context
            wib = _e6(nc.sbuf_tensor("wibuf", [128, DC, FF], BF16))
            wob = _e6(nc.sbuf_tensor("wobuf", [128, FC, D], BF16))
            bcat = _e6(nc.sbuf_tensor("bcat", [32, FF], BF16))
            gab = _e6(nc.sbuf_tensor("gab", [128, DC, 8], BF16))
            aab = _e6(nc.sbuf_tensor("aab", [128, DC, 32], BF16))
            x1sl = _e6(nc.sbuf_tensor("x1sl", [128, DC, 512], BF16))
            fsc = _e6(nc.sbuf_tensor("fsc", [128, 512], BF16))
            xnb = [_e6(nc.sbuf_tensor(f"xnb{i}", [128, DC, 512], BF16)) for i in range(2)]
            inter = _e6(nc.sbuf_tensor("inter", [128, FC, 512], BF16))
            x2sl = [_e6(nc.sbuf_tensor("x2sl0", [128, 512], F32))] * 2
            sqscr = _e6(nc.sbuf_tensor("sqscr", [128, 512], BF16))
            m2bc = _e6(nc.sbuf_tensor("m2bc", [128, 512], F32))
            v2bc = _e6(nc.sbuf_tensor("v2bc", [128, 512], F32))
            rp8 = _e6(nc.sbuf_tensor("rp8", [8, 512], F32))
            pp32 = _e6(nc.sbuf_tensor("pp32", [32, 512], BF16))
            tm8 = _e6(nc.sbuf_tensor("tm8", [128, 32], F32))
            mx8 = _e6(nc.sbuf_tensor("mx8", [128, 8], F32))
            ix8 = _e6(nc.sbuf_tensor("ix8", [128, 8], U32))
            ixf = _e6(nc.sbuf_tensor("ixf", [128, 4], F32))
            m8t = _e6(nc.sbuf_tensor("m8t", [128, 32], F32))
            m8f = _e6(nc.sbuf_tensor("m8f", [8, 512], BF16))
            psS1 = _e6(nc.psum_tensor("psS1", [128, 512], F32))
            psS2 = _e6(nc.psum_tensor("psS2", [128, 512], F32))
            psR8 = _e6(nc.psum_tensor("psR8", [128, 512], F32))
            psP32 = _e6(nc.psum_tensor("psP32", [32, 512], F32))
            psF0 = _e6(nc.psum_tensor("psF0", [128, 512], F32))
            psF1 = _e6(nc.psum_tensor("psF1", [128, 512], F32))
            psO0 = _e6(nc.psum_tensor("psO0", [128, 512], F32))
            psO1 = _e6(nc.psum_tensor("psO1", [128, 512], F32))
            if upto < 5:
                raise _Done()
            tr.dma("POOLQ", "d_ga",
                   lambda e: e.dma_start(out=gab[:, :, :],
                                         in_=gw8T.rearrange("(c p) e -> p c e", p=128)),
                   w=["gab"])
            tr.dma("POOLQ", "d_aa",
                   lambda e: e.dma_start(out=aab[:, :, :],
                                         in_=acatT.rearrange("(c p) e -> p c e", p=128)),
                   w=["aab"])
            tr.dma("POOLQ", "d_bcat", lambda e: e.dma_start(out=bcat[:, :], in_=bcatT),
                   w=["bcat"])

            def load_ffn_weights():
                wiq = ("SYNCQ", "POOLQ", "SYNCQ", "POOLQ")
                woq = ("ACTQ", "SYNCQ", "ACTQ", "POOLQ")
                for i in range(4):
                    tr.dma(wiq[i], f"d_wi{i}",
                           lambda e, _i=i: e.dma_start(
                               out=wib[:, 2 * _i:2 * _i + 2, :],
                               in_=wi16.rearrange("(c p) f -> p c f", p=128)[:, 2 * _i:2 * _i + 2, :]),
                           r=["wi16"], w=[f"wib.{i}"])
                    tr.dma(woq[i], f"d_wo{i}",
                           lambda e, _i=i: e.dma_start(
                               out=wob[:, 8 * _i:8 * _i + 8, :],
                               in_=wo16.rearrange("(c p) f -> p c f", p=128)[:, 8 * _i:8 * _i + 8, :]),
                           r=["wo16"], w=[f"wob.{i}"])

            def frontA(g):
                bb = g % 2
                tr.dma("POOLQ", "d_x1sl",
                       lambda e, _g=g: e.dma_start(
                           out=x1sl[:, :, :],
                           in_=x1h.rearrange("(c p) t -> p c t", p=128)
                           [:, :, _g * 512:(_g + 1) * 512]),
                       r=[f"x1h.{g}"], w=["x1sl"])
                tr.dma("SYNCQ", f"d_oinit{g}",
                       lambda e, _g=g: e.dma_start(
                           out=outT[:, _g * 512:(_g + 1) * 512],
                           in_=x1h[:, _g * 512:(_g + 1) * 512]),
                       r=[f"x1h.{g}"], w=[f"outT.{g}"])

                def _s1g(e):
                    for dc in range(DC):
                        i = e.matmul(psS1[:, :], lhsT=onesb_s[:, :], rhs=x1sl[:, dc, :],
                                     start=(dc == 0), stop=(dc == DC - 1))
                    return i
                tr.op("PE", _s1g, r=["c:onesb", "x1sl", "psS1"], w=["psS1"])
                for dc in range(DC):
                    tr.op("ACT", lambda e, _dc=dc: e.activation(out=sqscr[:, :],
                                                                in_=x1sl[:, _dc, :],
                                                                func=AF.Square),
                          r=["x1sl"], w=["sqscr"])
                    tr.op("PE", lambda e, _dc=dc: e.matmul(
                        psS2[:, :], lhsT=onesb_s[:, :], rhs=sqscr[:, :],
                        start=(_dc == 0), stop=(_dc == DC - 1)),
                        r=["c:onesb", "sqscr", "psS2"], w=["psS2"])
                tr.op("ACT", lambda e: e.activation(out=m2bc[:, :], in_=psS1[:, :],
                                                    func=AF.Copy, scale=1.0 / D),
                      r=["psS1"], w=["m2bc"])
                tr.op("ACT", lambda e: e.activation(out=v2bc[:, :], in_=psS2[:, :],
                                                    func=AF.Copy, scale=1.0 / D),
                      r=["psS2"], w=["v2bc"])
                tr.op("DVE", lambda e: e.tensor_mul(fsc[:, :], m2bc[:, :], m2bc[:, :]),
                      r=["m2bc"], w=["fsc"])
                tr.op("DVE", lambda e: e.tensor_sub(v2bc[:, :], v2bc[:, :], fsc[:, :]),
                      r=["v2bc", "fsc"], w=["v2bc"])
                tr.op("ACT", lambda e: e.activation(out=v2bc[:, :], in_=v2bc[:, :],
                                                    func=AF.Ln, bias=cb_s[:, 2:3]),
                      r=["v2bc", "c:cb"], w=["v2bc"])
                tr.op("ACT", lambda e: e.activation(out=v2bc[:, :], in_=v2bc[:, :],
                                                    func=AF.Exp, scale=-0.5),
                      r=["v2bc"], w=["v2bc"])
                for dc in range(DC):
                    tr.op("DVE", lambda e, _dc=dc: e.tensor_sub(fsc[:, :], x1sl[:, _dc, :],
                                                                m2bc[:, :]),
                          r=["x1sl", "m2bc"], w=["fsc"])
                    tr.op("DVE", lambda e, _dc=dc, _bb=bb: e.tensor_mul(
                        xnb[_bb][:, _dc, :], fsc[:, :], v2bc[:, :]),
                        r=["fsc", "v2bc"], w=[f"xnb{bb}"])

            def frontB(g):
                bb = g % 2

                def _rg(e, _bb=bb):
                    for dc in range(DC):
                        e.matmul(psR8[0:8, :], lhsT=gab[:, dc, :], rhs=xnb[_bb][:, dc, :],
                                 start=(dc == 0), stop=(dc == DC - 1))
                        i = e.matmul(psP32[:, :], lhsT=aab[:, dc, :], rhs=xnb[_bb][:, dc, :],
                                     start=(dc == 0), stop=(dc == DC - 1))
                    return i
                tr.op("PE", _rg, r=["gab", "aab", f"xnb{bb}", "psR8", "psP32"],
                      w=["psR8", "psP32"])
                tr.op("DVE", lambda e: e.tensor_scalar(out=rp8[:, :], in0=psR8[0:8, :],
                                                       scalar1=g8_s[:, :], scalar2=None,
                                                       op0=ALU.add),
                      r=["psR8", "c:g8"], w=["rp8"])
                tr.op("ACT", lambda e: e.activation(out=pp32[:, :], in_=psP32[:, :],
                                                    func=AF.Copy),
                      r=["psP32"], w=["pp32"])
                for tc4 in range(4):
                    tr.op("PE", lambda e, _tc=tc4: e.transpose(
                        psS1[:, _tc * 8:(_tc + 1) * 8], rp8[:, _tc * 128:(_tc + 1) * 128],
                        eyef_s[0:8, 0:8]),
                        r=["rp8", "c:eyef", "psS1"], w=["psS1"])
                pe_settle(psS1[32:34, 40:42], ["psS1", "rp8"])
                tr.op("ACT", lambda e: e.activation(
                    out=tm8[:, :], in_=psS1[:, 0:32], func=AF.Copy),
                    r=["psS1"], w=["tm8"])
                for tc4 in range(4):
                    tr.op("DVE", lambda e, _tc=tc4: e.max(
                        out=mx8[:, :], in_=tm8[:, _tc * 8:(_tc + 1) * 8]),
                        r=["tm8"], w=["mx8"])
                    tr.op("DVE", lambda e, _tc=tc4: e.max_index(
                        out=ix8[:, :], in_max=mx8[:, :],
                        in_values=tm8[:, _tc * 8:(_tc + 1) * 8]),
                        r=["mx8", "tm8"], w=["ix8"])
                    tr.op("DVE", lambda e, _tc=tc4: e.tensor_copy(
                        out=ixf[:, _tc:_tc + 1], in_=ix8[:, 0:1]),
                        r=["ix8"], w=["ixf"])
                    tr.op("DVE", lambda e, _tc=tc4: e.tensor_scalar(
                        out=m8t[:, _tc * 8:(_tc + 1) * 8], in0=iota8_s[:, :],
                        scalar1=ixf[:, _tc:_tc + 1], scalar2=None, op0=ALU.is_equal),
                        r=["c:iota", "ixf"], w=["m8t"])
                for tc4 in range(4):
                    tr.op("PE", lambda e, _tc=tc4: e.transpose(
                        psR8[0:8, _tc * 128:(_tc + 1) * 128], m8t[:, _tc * 8:(_tc + 1) * 8],
                        eyef_s[:, :]),
                        r=["m8t", "c:eyef", "psR8"], w=["psR8"])
                pe_settle(psR8[32:34, 40:42], ["psR8", "m8t"])
                tr.op("ACT", lambda e: e.activation(
                    out=m8f[:, :], in_=psR8[0:8, :], func=AF.Copy),
                    r=["psR8"], w=["m8f"])
                tr.op("PE", lambda e: e.matmul(psP32[:, :], lhsT=expand_s[:, :], rhs=m8f[:, :],
                                               start=True, stop=True),
                      r=["c:exp", "m8f", "psP32"], w=["psP32"])
                tr.op("DVE", lambda e: e.tensor_mul(pp32[:, :], pp32[:, :], psP32[:, :]),
                      r=["pp32", "psP32"], w=["pp32"])

            def body_wi(g):
                bb = g % 2
                for fc in range(FC):
                    pp, pk = (psF0, "psF0") if fc % 2 == 0 else (psF1, "psF1")

                    def _wig(e, _fc=fc, _pp=pp, _bb=bb):
                        for dc in range(DC):
                            e.matmul(_pp[:, :],
                                     lhsT=wib[:, dc, _fc * 128:(_fc + 1) * 128],
                                     rhs=xnb[_bb][:, dc, :], start=(dc == 0), stop=False)
                        return e.matmul(_pp[:, :],
                                        lhsT=bcat[:, _fc * 128:(_fc + 1) * 128],
                                        rhs=pp32[:, :], start=False, stop=True)
                    tr.op("PE", _wig,
                          r=[f"wib.{i}" for i in range(4)] + [f"xnb{bb}", "bcat", "pp32", pk],
                          w=[pk])
                    tr.op("ACT", lambda e, _fc=fc, _pp=pp: e.activation(
                        out=inter[:, _fc, :], in_=_pp[:, :], func=AF.Relu,
                        bias=wib_s[:, _fc:_fc + 1]),
                        r=[pk, "c:wib"], w=[f"inter.{fc}"])

            def body_wo(g):
                for dcp in range(DC):
                    pp, pk = (psO0, "psO0") if dcp % 2 == 0 else (psO1, "psO1")

                    def _wog2(e, _dcp=dcp, _pp=pp):
                        for fc in range(FC):
                            i = e.matmul(_pp[:, :],
                                         lhsT=wob[:, fc, _dcp * 128:(_dcp + 1) * 128],
                                         rhs=inter[:, fc, :], start=(fc == 0),
                                         stop=(fc == FC - 1))
                        return i
                    tr.op("PE", _wog2,
                          r=[f"wob.{i}" for i in range(4)]
                          + [f"inter.{fc}" for fc in range(FC)] + [pk],
                          w=[pk])
                    tr.op("DVE", lambda e, _dcp=dcp, _pp=pp: e.tensor_scalar(
                        out=x2sl[0][:, :], in0=_pp[:, :], scalar1=wob_s[:, _dcp:_dcp + 1],
                        scalar2=None, op0=ALU.add),
                        r=[pk, "c:wob", "x2sl0"], w=["x2sl0"])
                    tr.dma("POOLQ", "d_out0",
                           lambda e, _g=g, _dcp=dcp: e.dma_start(
                               out=outT[_dcp * 128:(_dcp + 1) * 128,
                                        _g * 512:(_g + 1) * 512],
                               in_=x2sl[0][:, :], accum_op=ALU.add),
                           r=["x2sl0", f"outT.{g}"], w=[f"outT.{g}"])

            frontA(0)
            frontB(0)
            load_ffn_weights()
            for g in range(NG):
                body_wi(g)
                if g + 1 < NG:
                    frontA(g + 1)
                    frontB(g + 1)
                body_wo(g)
        raise _Done()
    except _Done:
        with ExitStack() as c2:
            tr.final_wait("POOLQ", "d_out0")
            tr.build_blocks(nc, c2)
    return nc


_NC_CACHE = {}


def _get_nc():
    if "nc" not in _NC_CACHE:
        _NC_CACHE["nc"] = build()
    return _NC_CACHE["nc"]


def kernel(**inputs):
    x = np.asarray(inputs["L0_attn_x"])
    nc = _get_nc()
    shared = _shared_arrays(inputs)
    in_maps = []
    for c in range(8):
        m = dict(shared)
        xt = np.ascontiguousarray(x[c].T)
        m["xT"] = xt
        m["xTf"] = xt
        in_maps.append(m)
    res = run_bass_kernel_spmd(nc, in_maps, core_ids=list(range(8)))
    return np.stack([np.asarray(r["outT"]).T for r in res.results], axis=0)


def _shared_arrays(inputs):
    ii = {k: np.asarray(v) for k, v in inputs.items()}
    wgP = np.zeros((D, 128), np.float32)
    wgP[:, 0:8] = ii["L0_attn_Wbeta"].T
    wgP[:, 32:40] = ii["L0_attn_Wig"].T
    wgP[:, 64:72] = ii["L0_attn_Wog"].T
    b128 = np.zeros((128, 1), np.float32)
    b128[0:8, 0] = ii["L0_attn_bbeta"]
    b128[32:40, 0] = ii["L0_attn_big"]
    b128[64:72, 0] = ii["L0_attn_bog"]
    return {
        "wqT": np.ascontiguousarray(ii["L0_attn_Wq"].T),
        "wkT": np.ascontiguousarray(ii["L0_attn_Wk"].T),
        "wvT": np.ascontiguousarray(ii["L0_attn_Wv"].T),
        "woutT": np.ascontiguousarray(ii["L0_attn_Wout"].T),
        "wgP": wgP, "b128": b128,
        "nw8": np.ascontiguousarray(ii["L0_attn_norm_w"].reshape(DC, 128).T),
        "gw8T": np.ascontiguousarray(ii["L0_ffn_gate_W"].T),
        "acatT": np.ascontiguousarray(
            np.transpose(ii["L0_ffn_lora_A"], (2, 0, 1)).reshape(D, 32)),
        "g8": np.ascontiguousarray(ii["L0_ffn_gate_b"].reshape(8, 1)),
        "bcatT": np.ascontiguousarray(
            np.transpose(ii["L0_ffn_lora_B"], (0, 2, 1)).reshape(32, FF)),
        "wiT": np.ascontiguousarray(ii["L0_ffn_wi_W"].T),
        "woT": np.ascontiguousarray(ii["L0_ffn_wo_W"].T),
        "wib32": np.ascontiguousarray(ii["L0_ffn_wi_b"].reshape(FC, 128).T),
        "wob8": np.ascontiguousarray(ii["L0_ffn_wo_b"].reshape(DC, 128).T),
        "bout8": np.ascontiguousarray(ii["L0_attn_bout"].reshape(DC, 128).T),
    }


# version 12
# speedup vs baseline: 1.1408x; 1.0119x over previous
import numpy as np
from contextlib import ExitStack

import concourse.bass as bass
import concourse.mybir as mybir
from concourse.bass_utils import run_bass_kernel_spmd

F32 = mybir.dt.float32
F32R = mybir.dt.float32r
BF16 = mybir.dt.bfloat16
U32 = mybir.dt.uint32
AF = mybir.ActivationFunctionType
ALU = mybir.AluOpType
AX = mybir.AxisListType

T, D, NH, DH, FF = 2048, 1024, 8, 128, 4096
NCH, NG, DC, FC = 16, 4, 8, 32
SCALE = 1.0 / float(np.sqrt(DH))

EMAP = {"PE": "tensor", "DVE": "vector", "ACT": "scalar", "POOL": "gpsimd",
        "SYNCQ": "sync", "ACTQ": "scalar", "POOLQ": "gpsimd"}
ENGS = ("PE", "DVE", "ACT", "POOL")


class TR:

    def __init__(self):
        self.cnt = {}
        self.lastw = {}
        self.readers = {}
        self.waited = {}
        self.floor = {}
        self.ops = {k: [] for k in ("tensor", "vector", "scalar", "gpsimd", "sync")}
        self.sem_names = set(ENGS)

    def barrier(self):
        self.floor = dict(self.cnt)

    def _wl(self, stream, r, w):
        need = dict(self.floor)
        for k in r:
            lw = self.lastw.get(k)
            if lw:
                need[lw[0]] = max(need.get(lw[0], 0), lw[1])
        for k in w:
            lw = self.lastw.get(k)
            if lw:
                need[lw[0]] = max(need.get(lw[0], 0), lw[1])
            for s2, v2 in self.readers.get(k, {}).items():
                need[s2] = max(need.get(s2, 0), v2)
        out = []
        wt = self.waited.setdefault(stream, {})
        for s2 in sorted(need):
            if wt.get(s2, 0) < need[s2]:
                wt[s2] = need[s2]
                out.append((s2, need[s2]))
        return out

    def _commit(self, sem, val, r, w):
        for k in w:
            self.lastw[k] = (sem, val)
            self.readers[k] = {}
        for k in r:
            self.readers.setdefault(k, {})[sem] = val

    def op(self, eng, fn, r=(), w=()):
        wl = self._wl(eng, r, w)
        self.cnt[eng] = self.cnt.get(eng, 0) + 1
        val = self.cnt[eng]

        def emit(engobj, sems, _wl=wl, _fn=fn, _eng=eng):
            for s2, v2 in _wl:
                engobj.wait_ge(sems[s2], v2)
            _fn(engobj).then_inc(sems[_eng], 1)
        self.ops[EMAP[eng]].append(emit)
        self._commit(eng, val, r, w)

    def dma(self, q, dsem, fn, r=(), w=()):
        self.sem_names.add(dsem)
        wl = self._wl(q, r, w)
        self.cnt[dsem] = self.cnt.get(dsem, 0) + 16
        val = self.cnt[dsem]

        def emit(engobj, sems, _wl=wl, _fn=fn, _ds=dsem):
            for s2, v2 in _wl:
                engobj.wait_ge(sems[s2], v2)
            _fn(engobj).then_inc(sems[_ds], 16)
        self.ops[EMAP[q]].append(emit)
        self._commit(dsem, val, r, w)

    def final_wait(self, stream, sem):
        v = self.cnt.get(sem, 0)
        self.ops[EMAP[stream]].append(
            lambda e, s, _sm=sem, _v=v: e.wait_ge(s[_sm], _v) if _v else None)

    def build_blocks(self, nc, ctx):
        sems = {n: ctx.enter_context(nc.semaphore(f"s_{n}"))
                for n in sorted(self.sem_names)}
        with nc.Block() as block:
            @block.sync
            def _(eng):
                for emit in self.ops["sync"]:
                    emit(eng, sems)

            @block.tensor
            def _(eng):
                for emit in self.ops["tensor"]:
                    emit(eng, sems)

            @block.vector
            def _(eng):
                for emit in self.ops["vector"]:
                    emit(eng, sems)

            @block.scalar
            def _(eng):
                for emit in self.ops["scalar"]:
                    emit(eng, sems)

            @block.gpsimd
            def _(eng):
                for emit in self.ops["gpsimd"]:
                    emit(eng, sems)


def build(upto=9):
    import ml_dtypes
    nc = bass.Bass("TRN2", target_bir_lowering=False, debug=False)

    def din(name, shape, dt=F32):
        return nc.dram_tensor(name, shape, dt, kind="ExternalInput").ap()
    xT = din("xT", [D, T], F32R)
    xTf = din("xTf", [D, T])
    wqT = din("wqT", [D, D]); wkT = din("wkT", [D, D]); wvT = din("wvT", [D, D])
    woutT = din("woutT", [D, D])
    wgP = din("wgP", [D, 128])
    b128 = din("b128", [128, 1])
    nw8 = din("nw8", [128, DC])
    gw8T = din("gw8T", [D, 8])
    acatT = din("acatT", [D, 32])
    g8 = din("g8", [8, 1])
    bcatT = din("bcatT", [32, FF])
    wiT = din("wiT", [D, FF])
    woT = din("woT", [FF, D])
    wib32 = din("wib32", [128, FC])
    wob8 = din("wob8", [128, DC])
    bout8 = din("bout8", [128, DC])
    outT = nc.dram_tensor("outT", [D, T], F32, kind="ExternalOutput").ap()
    x1h = nc.dram_tensor("x1h", [D, T], F32, kind="Internal").ap()
    wi16 = nc.dram_tensor("wi16", [D, FF], BF16, kind="Internal").ap()
    wo16 = nc.dram_tensor("wo16", [FF, D], BF16, kind="Internal").ap()
    wv16 = nc.dram_tensor("wv16", [D, D], BF16, kind="Internal").ap()
    wk16 = nc.dram_tensor("wk16", [D, D], BF16, kind="Internal").ap()
    wq16 = nc.dram_tensor("wq16", [D, D], BF16, kind="Internal").ap()

    eye_f = nc.inline_tensor(np.eye(128, dtype=np.float32), "eye_f").ap()
    eye_b = nc.inline_tensor(np.eye(128, dtype=ml_dtypes.bfloat16), "eye_b").ap()
    ones_f = nc.inline_tensor(np.ones((128, 128), np.float32), "ones_f").ap()
    m01 = (np.arange(128)[:, None] <= np.arange(128)[None, :]).astype(np.float64)
    mask8_np = np.tile(m01, (1, 8)).astype(ml_dtypes.bfloat16)
    mask8_c = nc.inline_tensor(mask8_np, "mask8").ap()
    hm_np = np.zeros((8, 1024), np.float64)
    for h in range(8):
        hm_np[h, h * 128:(h + 1) * 128] = 1.0
    hmask_c = nc.inline_tensor(hm_np.astype(ml_dtypes.bfloat16), "hmask8").ap()
    iota8_c = nc.inline_tensor(
        np.broadcast_to(np.arange(8, dtype=np.float32), (128, 8)).copy(), "iota8").ap()
    exp_np = np.zeros((8, 32), ml_dtypes.bfloat16)
    for e in range(8):
        exp_np[e, e * 4:(e + 1) * 4] = 1.0
    expand_c = nc.inline_tensor(exp_np, "expand832").ap()
    rot = np.zeros((128, 128), np.float64)
    rot[np.arange(64), 64 + np.arange(64)] = -1.0
    rot[64 + np.arange(64), np.arange(64)] = 1.0
    rotT_c = nc.inline_tensor(rot.T.astype(ml_dtypes.bfloat16), "rotT").ap()
    sel = np.zeros((128, 16), np.float32)
    sel[64 + np.arange(8), np.arange(8)] = 1.0
    sel[32 + np.arange(8), 8 + np.arange(8)] = 1.0
    sel_c = nc.inline_tensor(sel, "sel16").ap()
    half = DH // 2
    inv_freq = 1.0 / (10000.0 ** (np.arange(half, dtype=np.float64) / half))
    frq = np.arange(T, dtype=np.float64)[None, :] * inv_freq[:, None]
    cs2 = np.concatenate([np.cos(frq), np.cos(frq)], axis=0).astype(ml_dtypes.bfloat16)
    sn2 = np.concatenate([np.sin(frq), np.sin(frq)], axis=0).astype(ml_dtypes.bfloat16)
    ctab_c = nc.inline_tensor(cs2, "ctab").ap()
    stab_c = nc.inline_tensor(sn2, "stab").ap()
    z8_c = nc.inline_tensor(np.zeros((8, 128), np.float32), "z8").ap()

    tr = TR()

    class _Done(Exception):
        pass

    def ldc(dst, src, key):
        tr.dma("SYNCQ", "dc_" + key.replace(":", ""),
               lambda e, _d=dst, _s=src: e.dma_start(out=_d, in_=_s), w=[key])

    def pe_settle(ps_ap, keys):
        tr.op("PE", lambda e, _p=ps_ap: e.matmul(
            _p, lhsT=onesf_s[0:1, 0:2], rhs=onesf_s[0:1, 0:2],
            start=True, stop=True), r=["c:onesf"], w=list(keys))

    ctx = ExitStack()
    try:
      with ctx:
        SB = lambda n, s, d=F32: ctx.enter_context(nc.sbuf_tensor(n, s, d))
        eyef_s = SB("eyef", [128, 128])
        eyeb_s = SB("eyeb", [128, 128], BF16)
        onesf_s = SB("onesf", [128, 128]); onesr_s = SB("onesr", [128, 128], F32R)
        onesb_s = SB("onesb", [128, 128], BF16)
        iota8_s = SB("iota8s", [128, 8])
        expand_s = SB("expand", [8, 32], BF16)
        nw_s = SB("nw", [128, DC]); b128_s = SB("b128s", [128, 1])
        g8_s = SB("g8s", [8, 1])
        wib_s = SB("wibc", [128, FC]); wob_s = SB("wobc", [128, DC]); bout_s = SB("boutc", [128, DC])
        cb_s = SB("cbias", [128, 4])
        ldc(eyef_s[:, :], eye_f, "c:eyef")
        ldc(eyeb_s[:, :], eye_b, "c:eyeb")
        ldc(onesf_s[:, :], ones_f, "c:onesf")
        ldc(iota8_s[:, :], iota8_c, "c:iota"); ldc(expand_s[:, :], expand_c, "c:exp")
        ldc(nw_s[:, :], nw8, "c:nw"); ldc(b128_s[:, :], b128, "c:b128")
        ldc(g8_s[:, :], g8, "c:g8"); ldc(wib_s[:, :], wib32, "c:wib")
        ldc(wob_s[:, :], wob8, "c:wob"); ldc(bout_s[:, :], bout8, "c:bout")
        for j, v in ((0, 1e-11), (1, 1.0), (2, 1e-5),
                     (3, float(np.log(SCALE)))):
            tr.op("DVE", lambda e, _j=j, _v=v: e.memset(cb_s[:, _j:_j + 1], _v), w=["c:cb"])
        tr.op("DVE", lambda e: e.tensor_copy(out=onesr_s[:, :], in_=onesf_s[:, :]),
              r=["c:onesf"], w=["c:onesr"])
        tr.op("DVE", lambda e: e.tensor_copy(out=onesb_s[:, :], in_=onesf_s[:, :]),
              r=["c:onesf"], w=["c:onesb"])
        tr.dma("ACTQ", "d_x1init", lambda e: e.dma_start(out=x1h, in_=xTf),
               w=[f"x1h.{g}" for g in range(NG)])

        with ExitStack() as attn:
            _a = attn.enter_context
            ct_s = [_a(nc.sbuf_tensor(f"cts{i}", [128, 512], BF16)) for i in range(2)]
            st_s = [_a(nc.sbuf_tensor(f"sts{i}", [128, 512], BF16)) for i in range(2)]
            rotT_s = _a(nc.sbuf_tensor("rotTs", [128, 128], BF16))
            mask8_s = _a(nc.sbuf_tensor("mask8s", [128, 1024], BF16))
            hmask_s = _a(nc.sbuf_tensor("hmasks", [8, 1024], BF16))
            sel_s = _a(nc.sbuf_tensor("sel16s", [128, 16], F32))
            dg8_s = _a(nc.sbuf_tensor("dg8s", [8, 32], F32))
            z8_s = _a(nc.sbuf_tensor("z8s", [8, 128], F32))
            xsl3 = [_a(nc.sbuf_tensor(f"xsl3{i}", [128, DC, 256], F32R)) for i in range(2)]
            xsq = _a(nc.sbuf_tensor("xsq", [128, DC, 256], BF16))
            xhat = [_a(nc.sbuf_tensor(f"xhat{i}", [128, DC, 512], BF16)) for i in range(2)]
            rrep_s = _a(nc.sbuf_tensor("rrep", [128, 512], F32))
            t1_s = _a(nc.sbuf_tensor("t1s", [128, 256], F32))
            mrow_s = _a(nc.sbuf_tensor("mrow", [1, 1536], F32))
            mrr_s = _a(nc.sbuf_tensor("mrr", [1, 1024], F32R))
            wg_s = _a(nc.sbuf_tensor("wg", [128, DC, 128], F32R))
            wsg_s = _a(nc.sbuf_tensor("wsg", [1, 128], F32R))
            colt_s = _a(nc.sbuf_tensor("colt", [128, 1], F32))
            zt_s = _a(nc.sbuf_tensor("zt", [128, 256], F32))
            sg1_s = _a(nc.sbuf_tensor("sg1", [128, 256], F32))
            ls_s = _a(nc.sbuf_tensor("ls", [128, 512], F32))
            lhat_s = _a(nc.sbuf_tensor("lhat", [8, 512], F32))
            lgml_s = _a(nc.sbuf_tensor("lgml", [8, 512], F32))
            lbuf_s = _a(nc.sbuf_tensor("lbuf", [8, 512], F32))
            ehat_s = [_a(nc.sbuf_tensor(f"ehat{i}", [8, 512], F32)) for i in range(3)]
            em8_s = _a(nc.sbuf_tensor("em8", [8, 1024], BF16))
            tmg_s = [_a(nc.sbuf_tensor(f"tmg{i}", [128, 64], F32)) for i in range(3)]
            egml_s = _a(nc.sbuf_tensor("egml", [128, 8], F32))
            ue8_s = _a(nc.sbuf_tensor("ue8", [128, 8], F32))
            ac8_s = _a(nc.sbuf_tensor("ac8", [128, 8], F32))
            wpb = [_a(nc.sbuf_tensor(f"wpb{i}", [128, 4, 1024], BF16)) for i in range(3)]
            qb = _a(nc.sbuf_tensor("qb", [128, NH, 1024], BF16))
            kb = _a(nc.sbuf_tensor("kb", [128, NH, 1024], BF16))
            vb = _a(nc.sbuf_tensor("vb", [128, 8, D], BF16))
            ropeA = [_a(nc.sbuf_tensor(f"ropeA{i}", [128, 512], BF16)) for i in range(2)]
            tmp2 = [_a(nc.sbuf_tensor(f"tmp2{i}", [128, 512], BF16)) for i in range(2)]
            qtl = [_a(nc.sbuf_tensor(f"qtl{i}", [128, NH, 128], BF16)) for i in range(2)]
            pbt_s = _a(nc.sbuf_tensor("pbt", [128, NH, 128], BF16))
            vtl_s = _a(nc.sbuf_tensor("vtl", [128, NH, 128], BF16))
            ktm_s = _a(nc.sbuf_tensor("ktm", [128, NH, 128], BF16))
            msb_s = _a(nc.sbuf_tensor("msb", [128, NH, DH], F32))
            mbb_s = _a(nc.sbuf_tensor("mbb", [128, NH, DH], BF16))
            yring = _a(nc.sbuf_tensor("yring", [128, NH, 512], BF16))
            aosl = _a(nc.sbuf_tensor("aosl", [128, DC, 256], F32))
            woutb = _a(nc.sbuf_tensor("woutb", [128, NH, D], BF16))
            psP0 = _a(nc.psum_tensor("psP0", [128, 512], F32))
            psP1 = _a(nc.psum_tensor("psP1", [128, 512], F32))
            psStat = _a(nc.psum_tensor("psStat", [128, 512], F32))
            psE = _a(nc.psum_tensor("psE", [128, NH * 128], F32))
            psA = _a(nc.psum_tensor("psA", [128, NH * 128], F32))
            psXK = _a(nc.psum_tensor("psXK", [128, NH * 128], BF16))
            if upto < 1:
                raise _Done()

            ldc(rotT_s[:, :], rotT_c, "c:rot"); ldc(mask8_s[:, :], mask8_c, "c:mask8")
            ldc(hmask_s[:, :], hmask_c, "c:hmask"); ldc(sel_s[:, :], sel_c, "c:sel")
            ldc(z8_s[:, :], z8_c, "c:z8")

            tr.dma("POOLQ", "d_wg",
                   lambda e: e.dma_start(out=wg_s[:, :, :],
                                         in_=wgP.rearrange("(c p) e -> p c e", p=128)),
                   w=["wg"])
            tr.dma("POOLQ", "d_wv16", lambda e: e.dma_start(out=wv16, in_=wvT), w=["wv16"])
            tr.dma("POOLQ", "d_wk16", lambda e: e.dma_start(out=wk16, in_=wkT), w=["wk16"])
            tr.dma("POOLQ", "d_wq16", lambda e: e.dma_start(out=wq16, in_=wqT), w=["wq16"])

            def _fold(e):
                for dc in range(DC):
                    i = e.mul(out=wg_s[:, dc, :], in_=wg_s[:, dc, :],
                              mul=nw_s[:, dc:dc + 1])
                return i
            tr.op("ACT", _fold, r=["wg", "c:nw"], w=["wg"])

            def _wgcs(e):
                for dc in range(DC):
                    i = e.matmul(psP0[:, 0:2], lhsT=wg_s[:, dc, :], rhs=onesr_s[:, 0:2],
                                 start=(dc == 0), stop=(dc == DC - 1))
                return i
            tr.op("PE", _wgcs, r=["wg", "c:onesr", "psP0"], w=["psP0"])
            tr.op("ACT", lambda e: e.activation(out=colt_s[:, :], in_=psP0[:, 0:1],
                                                func=AF.Copy),
                  r=["psP0"], w=["colt"])
            tr.op("PE", lambda e: e.transpose(psP0[0:1, 0:128], colt_s[:, :], eyef_s[:, :]),
                  r=["colt", "c:eyef", "psP0"], w=["psP0"])
            pe_settle(psP0[32:34, 200:202], ["psP0"])
            tr.op("ACT", lambda e: e.activation(out=wsg_s[0:1, :], in_=psP0[0:1, 0:128],
                                                func=AF.Copy, scale=-1.0),
                  r=["psP0"], w=["wsg"])
            tr.dma("POOLQ", "d_wout",
                   lambda e: e.dma_start(out=woutb[:, :, :],
                                         in_=woutT.rearrange("(c p) e -> p c e", p=128)),
                   w=["woutb"])

            MR = {"mrow": 0, "msq": 512, "vrow": 1024}
            RR = {"mrowR": 0, "rrow": 512}

            def mrs(k):
                return mrow_s[0:1, MR[k]:MR[k] + 512]

            def rrs(k):
                return mrr_s[0:1, RR[k]:RR[k] + 512]

            def front(g):
                gp = g % 2
                gsl = slice(g * 512, (g + 1) * 512)
                tr.dma("SYNCQ", f"d_ct{gp}",
                       lambda e, _gp=gp, _gsl=gsl: e.dma_start(
                           out=ct_s[_gp][:, :], in_=ctab_c[:, _gsl]),
                       w=[f"ct{gp}"])
                tr.dma("SYNCQ", f"d_st{gp}",
                       lambda e, _gp=gp, _gsl=gsl: e.dma_start(
                           out=st_s[_gp][:, :], in_=stab_c[:, _gsl]),
                       w=[f"st{gp}"])
                for hb in range(2):
                    xg256 = slice(g * 512 + hb * 256, g * 512 + (hb + 1) * 256)
                    tr.dma("SYNCQ", f"d_xsl3{hb}",
                           lambda e, _hb=hb, _s=xg256: e.dma_start(
                               out=xsl3[_hb][:, :, :],
                               in_=xT.rearrange("(c p) t -> p c t", p=128)[:, :, _s]),
                           w=[f"xsl3{hb}"])
                    tr.op("ACT", lambda e, _hb=hb: e.activation(
                        out=xsq[:, :, :].rearrange("p c t -> p (c t)"),
                        in_=xsl3[_hb][:, :, :].rearrange("p c t -> p (c t)"),
                        func=AF.Square),
                        r=[f"xsl3{hb}"], w=["xsq"])

                    def _sums(e, _hb=hb):
                        for dc in range(DC):
                            e.matmul(psStat[0:1, _hb * 256:(_hb + 1) * 256],
                                     lhsT=onesr_s[:, 0:1], rhs=xsl3[_hb][:, dc, :],
                                     start=(dc == 0), stop=(dc == DC - 1))
                        for dc in range(DC):
                            i = e.matmul(psP1[0:1, _hb * 256:(_hb + 1) * 256],
                                         lhsT=onesb_s[:, 0:1], rhs=xsq[:, dc, :],
                                         start=(dc == 0), stop=(dc == DC - 1))
                        return i
                    tr.op("PE", _sums,
                          r=[f"xsl3{hb}", "xsq", "c:onesr", "c:onesb", "psStat", "psP1"],
                          w=["psStat", "st:rows", "psP1"])
                tr.op("ACT", lambda e: e.activation(out=mrs("mrow"), in_=psStat[0:1, :],
                                                    func=AF.Copy, scale=1.0 / D),
                      r=["st:rows"], w=["r:mrow"])
                tr.op("ACT", lambda e: e.activation(out=rrs("mrowR"), in_=psStat[0:1, :],
                                                    func=AF.Copy, scale=1.0 / D),
                      r=["st:rows"], w=["r:mrowR"])
                tr.op("ACT", lambda e: e.activation(out=mrs("msq"), in_=mrs("mrow"),
                                                    func=AF.Square),
                      r=["r:mrow"], w=["r:msq"])
                tr.op("DVE", lambda e: e.tensor_scalar(out=mrs("vrow"), in0=psP1[0:1, :],
                                                       scalar1=1.0 / D, scalar2=None,
                                                       op0=ALU.mult),
                      r=["st:rows", "psP1"], w=["r:vrow"])
                tr.op("DVE", lambda e: e.tensor_sub(mrs("vrow"), mrs("vrow"), mrs("msq")),
                      r=["r:vrow", "r:msq"], w=["r:vrow"])
                tr.op("ACT", lambda e: e.activation(out=mrs("vrow"), in_=mrs("vrow"),
                                                    func=AF.Ln, scale=1.000001,
                                                    bias=cb_s[0:1, 0:1]),
                      r=["r:vrow", "c:cb"], w=["r:vrow"])
                tr.op("ACT", lambda e: e.activation(out=rrs("rrow"), in_=mrs("vrow"),
                                                    func=AF.Exp, scale=-0.5),
                      r=["r:vrow"], w=["r:rrow"])
                tr.op("PE", lambda e: e.matmul(psP0[:, :], lhsT=onesr_s[0:1, :],
                                               rhs=rrs("rrow"), start=True, stop=True),
                      r=["r:rrow", "c:onesr", "psP0"], w=["psP0"])
                tr.op("ACT", lambda e: e.activation(out=rrep_s[:, :], in_=psP0[:, :],
                                                    func=AF.Copy),
                      r=["psP0"], w=["rrep"])
                tr.op("PE", lambda e: e.matmul(psStat[:, :], lhsT=onesr_s[0:1, :],
                                               rhs=rrs("mrowR"), start=True, stop=True),
                      r=["r:mrowR", "c:onesr", "psStat", "st:rows"], w=["psStat", "st:mrep"])
                for hb in range(2):
                    hsl = slice(hb * 256, (hb + 1) * 256)
                    for dc in range(DC):
                        tr.op("DVE", lambda e, _hb=hb, _dc=dc, _hsl=hsl: e.tensor_sub(
                            t1_s[:, :], xsl3[_hb][:, _dc, :], psStat[:, _hsl]),
                            r=[f"xsl3{hb}", "st:mrep"], w=["t1"])
                        tr.op("DVE", lambda e, _hb=hb, _dc=dc, _hsl=hsl, _gp=gp:
                              e.scalar_tensor_tensor(
                                  out=xhat[_gp][:, _dc, _hsl], in0=t1_s[:, :],
                                  scalar=nw_s[:, _dc:_dc + 1], in1=rrep_s[:, _hsl],
                                  op0=ALU.mult, op1=ALU.mult),
                              r=["t1", "c:nw", "rrep"], w=[f"xhat{gp}.{dc}"])
                for hb in range(2):
                    hsl = slice(hb * 256, (hb + 1) * 256)

                    def _zg(e, _hb=hb):
                        for dc in range(DC):
                            e.matmul(psP0[:, 0:256], lhsT=wg_s[:, dc, :],
                                     rhs=xsl3[_hb][:, dc, :], start=(dc == 0), stop=False)
                        return e.matmul(psP0[:, 0:256], lhsT=wsg_s[0:1, :],
                                        rhs=mrr_s[0:1, _hb * 256:(_hb + 1) * 256],
                                        start=False, stop=True)
                    tr.op("PE", _zg, r=["wg", f"xsl3{hb}", "wsg", "r:mrowR", "psP0"],
                          w=["psP0"])
                    tr.op("DVE", lambda e, _hsl=hsl: e.tensor_mul(zt_s[:, :], psP0[:, 0:256],
                                                                  rrep_s[:, _hsl]),
                          r=["psP0", "rrep"], w=["zt"])
                    tr.op("DVE", lambda e: e.tensor_scalar(out=zt_s[:, :], in0=zt_s[:, :],
                                                           scalar1=b128_s[:, :], scalar2=None,
                                                           op0=ALU.add),
                          r=["zt", "c:b128"], w=["zt"])
                    tr.op("ACT", lambda e: e.activation(out=sg1_s[:, :], in_=zt_s[:, :],
                                                        func=AF.Abs),
                          r=["zt"], w=["sg1"])
                    tr.op("ACT", lambda e: e.activation(out=sg1_s[:, :], in_=sg1_s[:, :],
                                                        func=AF.Exp, scale=-1.0),
                          r=["sg1"], w=["sg1"])
                    tr.op("ACT", lambda e: e.activation(out=sg1_s[:, :], in_=sg1_s[:, :],
                                                        func=AF.Ln, bias=cb_s[:, 1:2]),
                          r=["sg1", "c:cb"], w=["sg1"])
                    tr.op("DVE", lambda e: e.tensor_scalar_min(zt_s[:, :], zt_s[:, :], 0.0),
                          r=["zt"], w=["zt"])
                    tr.op("DVE", lambda e, _hsl=hsl: e.tensor_sub(ls_s[:, _hsl], zt_s[:, :],
                                                                  sg1_s[:, :]),
                          r=["zt", "sg1"], w=[f"ls.{hb}"])
                    tr.op("PE", lambda e, _hsl=hsl: e.matmul(
                        psP1[0:8, 0:256], lhsT=sel_s[:, 0:8], rhs=ls_s[:, _hsl],
                        start=True, stop=True),
                        r=["c:sel", f"ls.{hb}", "psP1"], w=["psP1"])
                    tr.op("ACT", lambda e, _hsl=hsl: e.activation(out=lhat_s[:, _hsl],
                                                                  in_=psP1[0:8, 0:256],
                                                                  func=AF.Copy),
                          r=["psP1"], w=["lhat"])
                    tr.op("PE", lambda e, _hsl=hsl: e.matmul(
                        psP1[0:8, 0:256], lhsT=sel_s[:, 8:16], rhs=ls_s[:, _hsl],
                        start=True, stop=True),
                        r=["c:sel", f"ls.{hb}", "psP1"], w=["psP1"])
                    tr.op("ACT", lambda e, _hsl=hsl: e.activation(out=lgml_s[:, _hsl],
                                                                  in_=psP1[0:8, 0:256],
                                                                  func=AF.Copy),
                          r=["psP1"], w=["lgml"])
                for lc in range(4):
                    csl = slice(lc * 128, (lc + 1) * 128)
                    hb = lc // 2
                    tr.op("DVE", lambda e, _sl=csl: e.tensor_tensor_scan(
                        out=lbuf_s[:, _sl], data0=ls_s[0:8, _sl], data1=z8_s[:, :],
                        initial=0.0, op0=ALU.add, op1=ALU.add),
                        r=[f"ls.{hb}", "c:z8"], w=["lbuf"])
                    tr.op("DVE", lambda e, _sl=csl: e.tensor_add(
                        lhat_s[:, _sl], lhat_s[:, _sl], lbuf_s[:, _sl]),
                        r=["lbuf", "lhat"], w=["lhat"])
                    tr.op("DVE", lambda e, _sl=csl: e.tensor_sub(
                        lgml_s[:, _sl], lgml_s[:, _sl], lbuf_s[:, _sl]),
                        r=["lbuf", "lgml"], w=["lgml"])
                g3 = g % 3
                tr.op("ACT", lambda e, _g3=g3: e.activation(
                    out=ehat_s[_g3][:, :], in_=lhat_s[:, :], func=AF.Exp,
                    bias=cb_s[0:8, 3:4]),
                    r=["lhat", "c:cb"], w=[f"ehat{g3}"])
                for lc in range(4):
                    tr.op("DVE", lambda e, _lc=lc: e.tensor_scalar(
                        out=dg8_s[:, _lc * 8:(_lc + 1) * 8], in0=eyef_s[0:8, 0:8],
                        scalar1=lbuf_s[:, _lc * 128 + 127:_lc * 128 + 128], scalar2=None,
                        op0=ALU.mult),
                        r=["lbuf", "c:eyef"], w=["dg8"])

                def _hoist(e):
                    for lc in range(4):
                        e.transpose(psStat[:, lc * 16:lc * 16 + 8],
                                    lgml_s[:, lc * 128:(lc + 1) * 128], eyef_s[0:8, 0:8])
                    for lc in range(4):
                        i = e.matmul(psStat[:, lc * 16 + 8:lc * 16 + 16],
                                     lhsT=onesf_s[0:8, :],
                                     rhs=dg8_s[:, lc * 8:(lc + 1) * 8],
                                     start=True, stop=True)
                    return i
                tr.op("PE", _hoist, r=["lgml", "dg8", "c:eyef", "c:onesf"],
                      w=["psStat", "st:hoist", "st:mrep"])
                tr.op("ACT", lambda e, _g3=g3: e.activation(
                    out=tmg_s[_g3][:, :], in_=psStat[:, 0:64], func=AF.Copy),
                    r=["st:hoist"], w=[f"tmg{g3}"])

            wp_ctr = [0]

            def proj(g):
                gp = g % 2
                gsl = slice(g * 512, (g + 1) * 512)
                for pj, wsrc in (("v", wv16), ("k", wk16), ("q", wq16)):
                    sl0 = wp_ctr[0] % 3
                    sl1 = (wp_ctr[0] + 1) % 3
                    wp_ctr[0] += 2
                    for hf, wsl in ((0, sl0), (1, sl1)):
                        tr.dma("SYNCQ", f"d_wp{wsl}",
                               lambda e, _w=wsrc, _hf=hf, _wsl=wsl: e.dma_start(
                                   out=wpb[_wsl][:, :, :],
                                   in_=_w.rearrange("(c p) e -> p c e", p=128)
                                   [:, _hf * 4:(_hf + 1) * 4, :]),
                               r=[f"w{pj}16"], w=[f"wpb{wsl}"])
                    if pj == "v":
                        for tc4 in range(4):
                            cg = g * 4 + tc4
                            slot = cg % 8
                            for eh in range(2):
                                pp, pk = (psP0, "psP0") if eh == 0 else (psP1, "psP1")

                                def _vg(e, _tc=tc4, _eh=eh, _pp=pp, _gp=gp,
                                        _s0=sl0, _s1=sl1):
                                    for dc in range(DC):
                                        i = e.matmul(
                                            _pp[:, :],
                                            lhsT=xhat[_gp][:, dc, _tc * 128:(_tc + 1) * 128],
                                            rhs=wpb[_s0 if dc < 4 else _s1][:, dc % 4,
                                                             _eh * 512:(_eh + 1) * 512],
                                            start=(dc == 0), stop=(dc == DC - 1))
                                    return i
                                tr.op("PE", _vg,
                                      r=[f"xhat{gp}.{dc}" for dc in range(DC)]
                                      + [f"wpb{sl0}", f"wpb{sl1}", pk], w=[pk])
                                tr.op("ACT", lambda e, _s=slot, _eh=eh, _pp=pp: e.activation(
                                    out=vb[:, _s, _eh * 512:(_eh + 1) * 512], in_=_pp[:, :],
                                    func=AF.Copy),
                                    r=[pk], w=[f"vb.{slot}"])
                    else:
                        dst = kb if pj == "k" else qb
                        dkey = pj + "b"
                        for h in range(NH):
                            hp = h % 2
                            pp, pk = (psP0, "psP0") if hp == 0 else (psP1, "psP1")

                            def _qg(e, _h=h, _pp=pp, _gp=gp, _s0=sl0, _s1=sl1):
                                for dc in range(DC):
                                    i = e.matmul(
                                        _pp[:, :],
                                        lhsT=wpb[_s0 if dc < 4 else _s1][:, dc % 4,
                                                                         _h * 128:(_h + 1) * 128],
                                        rhs=xhat[_gp][:, dc, :],
                                        start=(dc == 0), stop=(dc == DC - 1))
                                return i
                            tr.op("PE", _qg,
                                  r=[f"xhat{gp}.{dc}" for dc in range(DC)]
                                  + [f"wpb{sl0}", f"wpb{sl1}", pk], w=[pk])
                            tr.op("ACT", lambda e, _pp=pp, _hp=hp: e.activation(
                                out=ropeA[_hp][:, :], in_=_pp[:, :], func=AF.Copy),
                                r=[pk], w=[f"ropeA{hp}"])
                            tr.op("PE", lambda e, _hp=hp, _pp=pp: e.matmul(
                                _pp[:, :], lhsT=rotT_s[:, :], rhs=ropeA[_hp][:, :],
                                start=True, stop=True),
                                r=["c:rot", f"ropeA{hp}"],
                                w=[pk])
                            tr.op("DVE", lambda e, _hp=hp, _gp=gp, _pp=pp: e.tensor_mul(
                                tmp2[_hp][:, :], _pp[:, :], st_s[_gp][:, :]),
                                r=[pk, f"st{gp}"], w=[f"tmp2{hp}"])
                            tr.op("DVE", lambda e, _hp=hp, _gp=gp: e.tensor_mul(
                                ropeA[_hp][:, :], ropeA[_hp][:, :], ct_s[_gp][:, :]),
                                r=[f"ropeA{hp}", f"ct{gp}"], w=[f"ropeA{hp}"])
                            tr.op("DVE", lambda e, _d=dst, _h=h, _gp=gp, _hp=hp: e.tensor_add(
                                _d[:, _h, _gp * 512:(_gp + 1) * 512],
                                ropeA[_hp][:, :], tmp2[_hp][:, :]),
                                r=[f"ropeA{hp}", f"tmp2{hp}"], w=[f"{dkey}.{h}.g{g}"])

            def scanwout(g):
                gp = g % 2
                g3 = g % 3
                for lc in range(4):
                    cg = g * 4 + lc
                    cp = cg % 2
                    slot = cg % 8
                    csl = slice(lc * 128, (lc + 1) * 128)
                    rsl = slice(gp * 512 + lc * 128, gp * 512 + (lc + 1) * 128)
                    tr.op("ACT", lambda e, _g3=g3, _lc=lc: e.activation(
                        out=egml_s[:, :], in_=tmg_s[_g3][:, _lc * 16:_lc * 16 + 8],
                        func=AF.Exp),
                        r=[f"tmg{g3}", "uegate"], w=["egml"])
                    tr.op("DVE", lambda e, _g3=g3, _lc=lc: e.tensor_add(
                        ue8_s[:, :], tmg_s[_g3][:, _lc * 16:_lc * 16 + 8],
                        tmg_s[_g3][:, _lc * 16 + 8:_lc * 16 + 16]),
                        r=[f"tmg{g3}", "uegate"], w=["ue8"])
                    tr.op("ACT", lambda e: e.activation(out=ue8_s[:, :], in_=ue8_s[:, :],
                                                        func=AF.Exp),
                          r=["ue8"], w=["ue8"])
                    tr.op("ACT", lambda e, _g3=g3, _lc=lc: e.activation(
                        out=ac8_s[:, :], in_=tmg_s[_g3][:, _lc * 16 + 8:_lc * 16 + 16],
                        func=AF.Exp),
                        r=[f"tmg{g3}"], w=["ac8"])
                    tr.op("DVE", lambda e, _g3=g3, _csl=csl: e.tensor_mul(
                        em8_s[:, :].rearrange("p (h t) -> p h t", h=NH),
                        ehat_s[_g3][:, _csl].rearrange("p (o t) -> p o t", o=1)
                        .broadcast_to([8, NH, 128]),
                        hmask_s[:, :].rearrange("p (h t) -> p h t", h=NH)),
                        r=[f"ehat{g3}", "c:hmask"], w=["em8"])

                    def _bc(e):
                        for bk in range(2):
                            i = e.matmul(psE[:, bk * 512:(bk + 1) * 512],
                                         lhsT=onesb_s[0:8, :],
                                         rhs=em8_s[:, bk * 512:(bk + 1) * 512],
                                         start=True, stop=True)
                        return i
                    tr.op("PE", _bc, r=["em8", "c:onesb", "psE"], w=["psE"])
                    tr.op("DVE", lambda e, _cp=cp, _rsl=rsl: e.tensor_mul(
                        qtl[_cp][:, :, :],
                        qb[:, :, _rsl],
                        psE[:, :].rearrange("p (h t) -> p h t", h=NH)),
                        r=[f"qb.{h}.g{g}" for h in range(NH)] + ["psE"],
                        w=[f"qtl{cp}"])

                    def _ska(e, _cp=cp, _rsl=rsl):
                        for h in range(NH):
                            e.transpose(psXK[:, h * 128:(h + 1) * 128],
                                        kb[:, h, _rsl], eyeb_s[:, :])
                            i = e.matmul(psA[:, h * 128:(h + 1) * 128],
                                         lhsT=kb[:, h, _rsl], rhs=qtl[_cp][:, h, :],
                                         start=True, stop=True)
                        return i
                    tr.op("PE", _ska,
                          r=[f"kb.{h}.g{g}" for h in range(NH)]
                          + [f"qtl{cp}", "c:eyeb", "psXK", "psA"],
                          w=["psXK", "psA"])
                    tr.op("DVE", lambda e: e.tensor_mul(
                        pbt_s[:, :, :],
                        psA[:, :].rearrange("p (h t) -> p h t", h=NH),
                        egml_s[:, :].rearrange("p (h o) -> p h o", o=1)
                        .broadcast_to([128, NH, 128])),
                        r=["psA", "egml"], w=["pbt"])
                    tr.op("DVE", lambda e: e.tensor_mul(
                        pbt_s[:, :, :].rearrange("p h t -> p (h t)"),
                        pbt_s[:, :, :].rearrange("p h t -> p (h t)"),
                        mask8_s[:, :]),
                        r=["pbt", "c:mask8"], w=["pbt"])
                    tr.op("ACT", lambda e: e.activation(
                        out=ktm_s[:, :, :].rearrange("p h t -> p (h t)"), in_=psXK[:, :],
                        func=AF.Copy),
                        r=["psXK"], w=["ktm"])
                    if cg < NCH - 1:
                        def _vtg(e, _s=slot):
                            for h in range(NH):
                                i = e.tensor_scalar(
                                    out=vtl_s[:, h, :],
                                    in0=vb[:, _s, h * 128:(h + 1) * 128],
                                    scalar1=ue8_s[:, h:h + 1], scalar2=None, op0=ALU.mult)
                            return i
                        tr.op("DVE", _vtg, r=[f"vb.{slot}", "ue8"], w=["vtl", "uegate"])

                        def _kvg(e):
                            for h in range(NH):
                                i = e.matmul(psA[:, h * 128:(h + 1) * 128],
                                             lhsT=ktm_s[:, h, :], rhs=vtl_s[:, h, :],
                                             start=True, stop=True)
                            return i
                        tr.op("PE", _kvg, r=["ktm", "vtl", "pbt", "psA"],
                              w=["psA"])
                    if cg > 0:
                        def _yinter(e, _cp=cp):
                            for h in range(NH):
                                i = e.matmul(psE[:, h * 128:(h + 1) * 128],
                                             lhsT=mbb_s[:, h, :], rhs=qtl[_cp][:, h, :],
                                             start=True, stop=True)
                            return i
                        tr.op("PE", _yinter, r=["mbb", f"qtl{cp}", "psE"], w=["psE"])
                    if cg < NCH - 1:
                        if cg == 0:
                            for bk in range(2):
                                tr.op("DVE", lambda e, _b=bk: e.tensor_copy(
                                    out=msb_s[:, 4 * _b:4 * (_b + 1), :],
                                    in_=psA[:, _b * 512:(_b + 1) * 512]
                                    .rearrange("p (h t) -> p h t", h=4)),
                                    r=["psA"], w=["msb"])
                        else:
                            def _mdec(e):
                                for h in range(NH):
                                    i = e.tensor_scalar(
                                        out=msb_s[:, h, :], in0=msb_s[:, h, :],
                                        scalar1=ac8_s[:, h:h + 1], scalar2=None,
                                        op0=ALU.mult)
                                return i
                            tr.op("DVE", _mdec, r=["msb", "ac8"], w=["msb"])
                            for bk in range(2):
                                tr.op("DVE", lambda e, _b=bk: e.tensor_add(
                                    msb_s[:, 4 * _b:4 * (_b + 1), :],
                                    msb_s[:, 4 * _b:4 * (_b + 1), :],
                                    psA[:, _b * 512:(_b + 1) * 512]
                                    .rearrange("p (h t) -> p h t", h=4)),
                                    r=["msb", "psA"], w=["msb"])
                        tr.op("ACT", lambda e: e.activation(
                            out=mbb_s[:, :, :].rearrange("p h t -> p (h t)"),
                            in_=msb_s[:, :, :].rearrange("p h t -> p (h t)"), func=AF.Copy),
                            r=["msb"], w=["mbb"])
                    def _yintra(e, _s=slot):
                        for h in range(NH):
                            i = e.matmul(psA[:, h * 128:(h + 1) * 128],
                                         lhsT=vb[:, _s, h * 128:(h + 1) * 128],
                                         rhs=pbt_s[:, h, :],
                                         start=True, stop=True)
                        return i
                    tr.op("PE", _yintra, r=[f"vb.{slot}", "pbt", "psA"], w=["psA"])
                    if cg > 0:
                        for bk in range(2):
                            tr.op("DVE", lambda e, _lc=lc, _b=bk: e.tensor_copy(
                                out=yring[:, 4 * _b:4 * (_b + 1), _lc * 128:(_lc + 1) * 128],
                                in_=psE[:, _b * 512:(_b + 1) * 512]
                                .rearrange("p (h t) -> p h t", h=4)),
                                r=["psE"], w=[f"yring.{lc}"])
                        for bk in range(2):
                            tr.op("DVE", lambda e, _lc=lc, _b=bk: e.tensor_add(
                                yring[:, 4 * _b:4 * (_b + 1), _lc * 128:(_lc + 1) * 128],
                                yring[:, 4 * _b:4 * (_b + 1), _lc * 128:(_lc + 1) * 128],
                                psA[:, _b * 512:(_b + 1) * 512]
                                .rearrange("p (h t) -> p h t", h=4)),
                                r=["psA", f"yring.{lc}"], w=[f"yring.{lc}"])
                    else:
                        for bk in range(2):
                            tr.op("DVE", lambda e, _lc=lc, _b=bk: e.tensor_copy(
                                out=yring[:, 4 * _b:4 * (_b + 1), _lc * 128:(_lc + 1) * 128],
                                in_=psA[:, _b * 512:(_b + 1) * 512]
                                .rearrange("p (h t) -> p h t", h=4)),
                                r=["psA"], w=[f"yring.{lc}"])
                for wh in range(2):
                    for dcp in range(DC):
                        pp, pk = (psP0, "psP0") if dcp % 2 == 0 else (psP1, "psP1")

                        def _wog(e, _dcp=dcp, _pp=pp, _wh=wh):
                            for h in range(NH):
                                i = e.matmul(
                                    _pp[:, 0:256],
                                    lhsT=woutb[:, h, _dcp * 128:(_dcp + 1) * 128],
                                    rhs=yring[:, h, _wh * 256:(_wh + 1) * 256],
                                    start=(h == 0), stop=(h == NH - 1))
                            return i
                        tr.op("PE", _wog,
                              r=["woutb"] + [f"yring.{i}" for i in range(4)] + [pk], w=[pk])
                        tr.op("DVE", lambda e, _dcp=dcp, _pp=pp: e.tensor_scalar(
                            out=aosl[:, _dcp, :], in0=_pp[:, 0:256],
                            scalar1=bout_s[:, _dcp:_dcp + 1], scalar2=None, op0=ALU.add),
                            r=[pk, "c:bout"], w=["aosl"])
                    tr.dma("POOLQ", "d_ao",
                           lambda e, _g=g, _wh=wh: e.dma_start(
                               out=x1h.rearrange("(c p) t -> p c t", p=128)
                               [:, :, _g * 512 + _wh * 256:_g * 512 + (_wh + 1) * 256],
                               in_=aosl[:, :, :], accum_op=ALU.add),
                           r=["aosl", f"x1h.{g}"], w=[f"x1h.{g}"])

            front(0)
            if upto < 2:
                raise _Done()
            for g in range(NG):
                proj(g)
                if upto < 3:
                    raise _Done()
                if g + 1 < NG:
                    front(g + 1)
                if g > 0:
                    scanwout(g - 1)
                if g == 1:
                    tr.dma("POOLQ", "d_wi16", lambda e: e.dma_start(out=wi16, in_=wiT),
                           w=["wi16"])
                if g == 2:
                    tr.dma("POOLQ", "d_wo16", lambda e: e.dma_start(out=wo16, in_=woT),
                           w=["wo16"])
            scanwout(NG - 1)

        tr.barrier()
        with ExitStack() as ph6:
            _e6 = ph6.enter_context
            wib = _e6(nc.sbuf_tensor("wibuf", [128, DC, FF], BF16))
            wob = _e6(nc.sbuf_tensor("wobuf", [128, FC, D], BF16))
            bcat = _e6(nc.sbuf_tensor("bcat", [32, FF], BF16))
            gab = _e6(nc.sbuf_tensor("gab", [128, DC, 8], BF16))
            aab = _e6(nc.sbuf_tensor("aab", [128, DC, 32], BF16))
            x1sl = _e6(nc.sbuf_tensor("x1sl", [128, DC, 512], BF16))
            fsc = _e6(nc.sbuf_tensor("fsc", [128, 512], BF16))
            xnb = [_e6(nc.sbuf_tensor(f"xnb{i}", [128, DC, 512], BF16)) for i in range(2)]
            inter = _e6(nc.sbuf_tensor("inter", [128, FC, 512], BF16))
            x2sl = [_e6(nc.sbuf_tensor("x2sl0", [128, 512], F32))] * 2
            sqscr = _e6(nc.sbuf_tensor("sqscr", [128, 512], BF16))
            m2bc = _e6(nc.sbuf_tensor("m2bc", [128, 512], F32))
            v2bc = _e6(nc.sbuf_tensor("v2bc", [128, 512], F32))
            rp8 = _e6(nc.sbuf_tensor("rp8", [8, 512], F32))
            pp32 = _e6(nc.sbuf_tensor("pp32", [32, 512], BF16))
            tm8 = _e6(nc.sbuf_tensor("tm8", [128, 32], F32))
            mx8 = _e6(nc.sbuf_tensor("mx8", [128, 8], F32))
            ix8 = _e6(nc.sbuf_tensor("ix8", [128, 8], U32))
            ixf = _e6(nc.sbuf_tensor("ixf", [128, 4], F32))
            m8t = _e6(nc.sbuf_tensor("m8t", [128, 32], F32))
            m8f = _e6(nc.sbuf_tensor("m8f", [8, 512], BF16))
            psS1 = _e6(nc.psum_tensor("psS1", [128, 512], F32))
            psS2 = _e6(nc.psum_tensor("psS2", [128, 512], F32))
            psR8 = _e6(nc.psum_tensor("psR8", [128, 512], F32))
            psP32 = _e6(nc.psum_tensor("psP32", [32, 512], F32))
            psF0 = _e6(nc.psum_tensor("psF0", [128, 512], F32))
            psF1 = _e6(nc.psum_tensor("psF1", [128, 512], F32))
            psO0 = _e6(nc.psum_tensor("psO0", [128, 512], F32))
            psO1 = _e6(nc.psum_tensor("psO1", [128, 512], F32))
            if upto < 5:
                raise _Done()
            tr.dma("POOLQ", "d_ga",
                   lambda e: e.dma_start(out=gab[:, :, :],
                                         in_=gw8T.rearrange("(c p) e -> p c e", p=128)),
                   w=["gab"])
            tr.dma("POOLQ", "d_aa",
                   lambda e: e.dma_start(out=aab[:, :, :],
                                         in_=acatT.rearrange("(c p) e -> p c e", p=128)),
                   w=["aab"])
            tr.dma("POOLQ", "d_bcat", lambda e: e.dma_start(out=bcat[:, :], in_=bcatT),
                   w=["bcat"])

            def load_ffn_weights():
                wiq = ("SYNCQ", "POOLQ", "SYNCQ", "POOLQ")
                woq = ("ACTQ", "SYNCQ", "ACTQ", "POOLQ")
                for i in range(4):
                    tr.dma(wiq[i], f"d_wi{i}",
                           lambda e, _i=i: e.dma_start(
                               out=wib[:, 2 * _i:2 * _i + 2, :],
                               in_=wi16.rearrange("(c p) f -> p c f", p=128)[:, 2 * _i:2 * _i + 2, :]),
                           r=["wi16"], w=[f"wib.{i}"])
                    tr.dma(woq[i], f"d_wo{i}",
                           lambda e, _i=i: e.dma_start(
                               out=wob[:, 8 * _i:8 * _i + 8, :],
                               in_=wo16.rearrange("(c p) f -> p c f", p=128)[:, 8 * _i:8 * _i + 8, :]),
                           r=["wo16"], w=[f"wob.{i}"])

            def frontA(g):
                bb = g % 2
                tr.dma("POOLQ", "d_x1sl",
                       lambda e, _g=g: e.dma_start(
                           out=x1sl[:, :, :],
                           in_=x1h.rearrange("(c p) t -> p c t", p=128)
                           [:, :, _g * 512:(_g + 1) * 512]),
                       r=[f"x1h.{g}"], w=["x1sl"])
                tr.dma("SYNCQ", f"d_oinit{g}",
                       lambda e, _g=g: e.dma_start(
                           out=outT[:, _g * 512:(_g + 1) * 512],
                           in_=x1h[:, _g * 512:(_g + 1) * 512]),
                       r=[f"x1h.{g}"], w=[f"outT.{g}"])

                def _s1g(e):
                    for dc in range(DC):
                        i = e.matmul(psS1[:, :], lhsT=onesb_s[:, :], rhs=x1sl[:, dc, :],
                                     start=(dc == 0), stop=(dc == DC - 1))
                    return i
                tr.op("PE", _s1g, r=["c:onesb", "x1sl", "psS1"], w=["psS1"])
                for dc in range(DC):
                    tr.op("ACT", lambda e, _dc=dc: e.activation(out=sqscr[:, :],
                                                                in_=x1sl[:, _dc, :],
                                                                func=AF.Square),
                          r=["x1sl"], w=["sqscr"])
                    tr.op("PE", lambda e, _dc=dc: e.matmul(
                        psS2[:, :], lhsT=onesb_s[:, :], rhs=sqscr[:, :],
                        start=(_dc == 0), stop=(_dc == DC - 1)),
                        r=["c:onesb", "sqscr", "psS2"], w=["psS2"])
                tr.op("ACT", lambda e: e.activation(out=m2bc[:, :], in_=psS1[:, :],
                                                    func=AF.Copy, scale=1.0 / D),
                      r=["psS1"], w=["m2bc"])
                tr.op("ACT", lambda e: e.activation(out=v2bc[:, :], in_=psS2[:, :],
                                                    func=AF.Copy, scale=1.0 / D),
                      r=["psS2"], w=["v2bc"])
                tr.op("DVE", lambda e: e.tensor_mul(fsc[:, :], m2bc[:, :], m2bc[:, :]),
                      r=["m2bc"], w=["fsc"])
                tr.op("DVE", lambda e: e.tensor_sub(v2bc[:, :], v2bc[:, :], fsc[:, :]),
                      r=["v2bc", "fsc"], w=["v2bc"])
                tr.op("ACT", lambda e: e.activation(out=v2bc[:, :], in_=v2bc[:, :],
                                                    func=AF.Ln, bias=cb_s[:, 2:3]),
                      r=["v2bc", "c:cb"], w=["v2bc"])
                tr.op("ACT", lambda e: e.activation(out=v2bc[:, :], in_=v2bc[:, :],
                                                    func=AF.Exp, scale=-0.5),
                      r=["v2bc"], w=["v2bc"])
                for dc in range(DC):
                    tr.op("DVE", lambda e, _dc=dc: e.tensor_sub(fsc[:, :], x1sl[:, _dc, :],
                                                                m2bc[:, :]),
                          r=["x1sl", "m2bc"], w=["fsc"])
                    tr.op("DVE", lambda e, _dc=dc, _bb=bb: e.tensor_mul(
                        xnb[_bb][:, _dc, :], fsc[:, :], v2bc[:, :]),
                        r=["fsc", "v2bc"], w=[f"xnb{bb}"])

            def frontB(g):
                bb = g % 2

                def _rg(e, _bb=bb):
                    for dc in range(DC):
                        e.matmul(psR8[0:8, :], lhsT=gab[:, dc, :], rhs=xnb[_bb][:, dc, :],
                                 start=(dc == 0), stop=(dc == DC - 1))
                        i = e.matmul(psP32[:, :], lhsT=aab[:, dc, :], rhs=xnb[_bb][:, dc, :],
                                     start=(dc == 0), stop=(dc == DC - 1))
                    return i
                tr.op("PE", _rg, r=["gab", "aab", f"xnb{bb}", "psR8", "psP32"],
                      w=["psR8", "psP32"])
                tr.op("DVE", lambda e: e.tensor_scalar(out=rp8[:, :], in0=psR8[0:8, :],
                                                       scalar1=g8_s[:, :], scalar2=None,
                                                       op0=ALU.add),
                      r=["psR8", "c:g8"], w=["rp8"])
                tr.op("ACT", lambda e: e.activation(out=pp32[:, :], in_=psP32[:, :],
                                                    func=AF.Copy),
                      r=["psP32"], w=["pp32"])
                for tc4 in range(4):
                    tr.op("PE", lambda e, _tc=tc4: e.transpose(
                        psS1[:, _tc * 8:(_tc + 1) * 8], rp8[:, _tc * 128:(_tc + 1) * 128],
                        eyef_s[0:8, 0:8]),
                        r=["rp8", "c:eyef", "psS1"], w=["psS1"])
                pe_settle(psS1[32:34, 40:42], ["psS1", "rp8"])
                tr.op("ACT", lambda e: e.activation(
                    out=tm8[:, :], in_=psS1[:, 0:32], func=AF.Copy),
                    r=["psS1"], w=["tm8"])
                for tc4 in range(4):
                    tr.op("DVE", lambda e, _tc=tc4: e.max(
                        out=mx8[:, :], in_=tm8[:, _tc * 8:(_tc + 1) * 8]),
                        r=["tm8"], w=["mx8"])
                    tr.op("DVE", lambda e, _tc=tc4: e.max_index(
                        out=ix8[:, :], in_max=mx8[:, :],
                        in_values=tm8[:, _tc * 8:(_tc + 1) * 8]),
                        r=["mx8", "tm8"], w=["ix8"])
                    tr.op("DVE", lambda e, _tc=tc4: e.tensor_copy(
                        out=ixf[:, _tc:_tc + 1], in_=ix8[:, 0:1]),
                        r=["ix8"], w=["ixf"])
                    tr.op("DVE", lambda e, _tc=tc4: e.tensor_scalar(
                        out=m8t[:, _tc * 8:(_tc + 1) * 8], in0=iota8_s[:, :],
                        scalar1=ixf[:, _tc:_tc + 1], scalar2=None, op0=ALU.is_equal),
                        r=["c:iota", "ixf"], w=["m8t"])
                for tc4 in range(4):
                    tr.op("PE", lambda e, _tc=tc4: e.transpose(
                        psR8[0:8, _tc * 128:(_tc + 1) * 128], m8t[:, _tc * 8:(_tc + 1) * 8],
                        eyef_s[:, :]),
                        r=["m8t", "c:eyef", "psR8"], w=["psR8"])
                pe_settle(psR8[32:34, 40:42], ["psR8", "m8t"])
                tr.op("ACT", lambda e: e.activation(
                    out=m8f[:, :], in_=psR8[0:8, :], func=AF.Copy),
                    r=["psR8"], w=["m8f"])
                tr.op("PE", lambda e: e.matmul(psP32[:, :], lhsT=expand_s[:, :], rhs=m8f[:, :],
                                               start=True, stop=True),
                      r=["c:exp", "m8f", "psP32"], w=["psP32"])
                tr.op("DVE", lambda e: e.tensor_mul(pp32[:, :], pp32[:, :], psP32[:, :]),
                      r=["pp32", "psP32"], w=["pp32"])

            def body_wi(g):
                bb = g % 2
                for fc in range(FC):
                    pp, pk = (psF0, "psF0") if fc % 2 == 0 else (psF1, "psF1")

                    def _wig(e, _fc=fc, _pp=pp, _bb=bb):
                        for dc in range(DC):
                            e.matmul(_pp[:, :],
                                     lhsT=wib[:, dc, _fc * 128:(_fc + 1) * 128],
                                     rhs=xnb[_bb][:, dc, :], start=(dc == 0), stop=False)
                        return e.matmul(_pp[:, :],
                                        lhsT=bcat[:, _fc * 128:(_fc + 1) * 128],
                                        rhs=pp32[:, :], start=False, stop=True)
                    tr.op("PE", _wig,
                          r=[f"wib.{i}" for i in range(4)] + [f"xnb{bb}", "bcat", "pp32", pk],
                          w=[pk])
                    tr.op("ACT", lambda e, _fc=fc, _pp=pp: e.activation(
                        out=inter[:, _fc, :], in_=_pp[:, :], func=AF.Relu,
                        bias=wib_s[:, _fc:_fc + 1]),
                        r=[pk, "c:wib"], w=[f"inter.{fc}"])

            def body_wo(g):
                for dcp in range(DC):
                    pp, pk = (psO0, "psO0") if dcp % 2 == 0 else (psO1, "psO1")

                    def _wog2(e, _dcp=dcp, _pp=pp):
                        for fc in range(FC):
                            i = e.matmul(_pp[:, :],
                                         lhsT=wob[:, fc, _dcp * 128:(_dcp + 1) * 128],
                                         rhs=inter[:, fc, :], start=(fc == 0),
                                         stop=(fc == FC - 1))
                        return i
                    tr.op("PE", _wog2,
                          r=[f"wob.{i}" for i in range(4)]
                          + [f"inter.{fc}" for fc in range(FC)] + [pk],
                          w=[pk])
                    tr.op("DVE", lambda e, _dcp=dcp, _pp=pp: e.tensor_scalar(
                        out=x2sl[0][:, :], in0=_pp[:, :], scalar1=wob_s[:, _dcp:_dcp + 1],
                        scalar2=None, op0=ALU.add),
                        r=[pk, "c:wob", "x2sl0"], w=["x2sl0"])
                    tr.dma("POOLQ", "d_out0",
                           lambda e, _g=g, _dcp=dcp: e.dma_start(
                               out=outT[_dcp * 128:(_dcp + 1) * 128,
                                        _g * 512:(_g + 1) * 512],
                               in_=x2sl[0][:, :], accum_op=ALU.add),
                           r=["x2sl0", f"outT.{g}"], w=[f"outT.{g}"])

            frontA(0)
            frontB(0)
            load_ffn_weights()
            for g in range(NG):
                body_wi(g)
                if g + 1 < NG:
                    frontA(g + 1)
                    frontB(g + 1)
                body_wo(g)
        raise _Done()
    except _Done:
        with ExitStack() as c2:
            tr.final_wait("POOLQ", "d_out0")
            tr.build_blocks(nc, c2)
    return nc


_NC_CACHE = {}


def _get_nc():
    if "nc" not in _NC_CACHE:
        _NC_CACHE["nc"] = build()
    return _NC_CACHE["nc"]


def kernel(**inputs):
    x = np.asarray(inputs["L0_attn_x"])
    nc = _get_nc()
    shared = _shared_arrays(inputs)
    in_maps = []
    for c in range(8):
        m = dict(shared)
        xt = np.ascontiguousarray(x[c].T)
        m["xT"] = xt
        m["xTf"] = xt
        in_maps.append(m)
    res = run_bass_kernel_spmd(nc, in_maps, core_ids=list(range(8)))
    return np.stack([np.asarray(r["outT"]).T for r in res.results], axis=0)


def _shared_arrays(inputs):
    ii = {k: np.asarray(v) for k, v in inputs.items()}
    wgP = np.zeros((D, 128), np.float32)
    wgP[:, 0:8] = ii["L0_attn_Wbeta"].T
    wgP[:, 32:40] = ii["L0_attn_Wig"].T
    wgP[:, 64:72] = ii["L0_attn_Wog"].T
    b128 = np.zeros((128, 1), np.float32)
    b128[0:8, 0] = ii["L0_attn_bbeta"]
    b128[32:40, 0] = ii["L0_attn_big"]
    b128[64:72, 0] = ii["L0_attn_bog"]
    return {
        "wqT": np.ascontiguousarray(ii["L0_attn_Wq"].T),
        "wkT": np.ascontiguousarray(ii["L0_attn_Wk"].T),
        "wvT": np.ascontiguousarray(ii["L0_attn_Wv"].T),
        "woutT": np.ascontiguousarray(ii["L0_attn_Wout"].T),
        "wgP": wgP, "b128": b128,
        "nw8": np.ascontiguousarray(ii["L0_attn_norm_w"].reshape(DC, 128).T),
        "gw8T": np.ascontiguousarray(ii["L0_ffn_gate_W"].T),
        "acatT": np.ascontiguousarray(
            np.transpose(ii["L0_ffn_lora_A"], (2, 0, 1)).reshape(D, 32)),
        "g8": np.ascontiguousarray(ii["L0_ffn_gate_b"].reshape(8, 1)),
        "bcatT": np.ascontiguousarray(
            np.transpose(ii["L0_ffn_lora_B"], (0, 2, 1)).reshape(32, FF)),
        "wiT": np.ascontiguousarray(ii["L0_ffn_wi_W"].T),
        "woT": np.ascontiguousarray(ii["L0_ffn_wo_W"].T),
        "wib32": np.ascontiguousarray(ii["L0_ffn_wi_b"].reshape(FC, 128).T),
        "wob8": np.ascontiguousarray(ii["L0_ffn_wo_b"].reshape(DC, 128).T),
        "bout8": np.ascontiguousarray(ii["L0_attn_bout"].reshape(DC, 128).T),
    }
